# version 02
# baseline (speedup 1.0000x reference)
import contextlib
import numpy as np
import ml_dtypes
import concourse.bass as bass
import concourse.mybir as mybir
from concourse.bass_utils import run_bass_kernel_spmd

F32 = mybir.dt.float32
BF16 = mybir.dt.bfloat16
AF = mybir.ActivationFunctionType
ALU = mybir.AluOpType
AX = mybir.AxisListType
BF16NP = ml_dtypes.bfloat16

NCORES = 8
B = 4
L = 4096
DM = 1024
DI = 2048
NST = 16
DTR = 64
NH = 16
NKV = 4
HD = 64
HF = 32
NEXP = 8
DFF = 4096
NBLK = L // 128
QOWN = L // 2
EPS_LN = 1e-5

ENGINES = ("te", "dv", "sc", "gp", "sy")


def own_blocks(parity):
    out = []
    for k in range(NBLK // 4):
        out += ([4 * k, 4 * k + 3] if parity == 0 else [4 * k + 1, 4 * k + 2])
    return sorted(out)


class Tok:
    __slots__ = ("eng", "cnt")

    def __init__(self, eng, cnt):
        self.eng = eng
        self.cnt = cnt


class Buf:
    __slots__ = ("ap_", "writer", "readers", "name", "cm")

    def __init__(self, ap, name="", cm=None):
        self.ap_ = ap
        self.writer = None
        self.readers = []
        self.name = name
        self.cm = cm

    def ap(self):
        return self.ap_

    def __getitem__(self, idx):
        return self.ap_[idx]


class Sched:
    def __init__(self, nc, stack):
        self.nc = nc
        self.recs = {e: [] for e in ENGINES}
        self.cnt = {e: 0 for e in ENGINES}
        self.waited = {c: {p: 0 for p in ENGINES} for c in ENGINES}
        self.sems = {e: stack.enter_context(nc.semaphore("sem_" + e))
                     for e in ENGINES}

    def _wait(self, eng, tok):
        if tok is None or tok.cnt == 0:
            return
        if self.waited[eng][tok.eng] >= tok.cnt:
            return
        self.waited[eng][tok.eng] = tok.cnt
        sem, cnt = self.sems[tok.eng], tok.cnt

        def emit(h):
            h.wait_ge(sem, cnt)
        self.recs[eng].append(emit)

    def op(self, eng, emit_fn, reads=(), writes=(), dma=False):
        for b in reads:
            if isinstance(b, Buf):
                self._wait(eng, b.writer)
        for b in writes:
            if isinstance(b, Buf):
                self._wait(eng, b.writer)
                for r in b.readers:
                    self._wait(eng, r)
        inc = 16 if dma else 1
        self.cnt[eng] += inc
        tok = Tok(eng, self.cnt[eng])
        sem = self.sems[eng]

        def emit(h):
            emit_fn(h).then_inc(sem, inc)
        self.recs[eng].append(emit)
        for b in reads:
            if isinstance(b, Buf):
                b.readers.append(tok)
                if len(b.readers) > 8:
                    b.readers = b.readers[-8:]
        for b in writes:
            if isinstance(b, Buf):
                b.writer = tok
                b.readers = []
        return tok

    def fence(self, eng):
        for e in ENGINES:
            self._wait(eng, Tok(e, self.cnt[e]))

    def fence_all(self):
        for e in ENGINES:
            self.fence(e)

    def flush(self, block):
        for ename, section in (("te", block.tensor), ("dv", block.vector),
                               ("sc", block.scalar), ("gp", block.gpsimd),
                               ("sy", block.sync)):
            recs = self.recs[ename]

            def make(rr):
                def _(h):
                    for r in rr:
                        r(h)
                return _
            section(make(recs))


class Prog:
    def __init__(self, name="p"):
        self.nc = bass.Bass("TRN2", target_bir_lowering=False, debug=False)
        self.stack = contextlib.ExitStack()
        self.stack.__enter__()
        self.s = Sched(self.nc, self.stack)
        self.ext_in = {}
        self.in_shapes = {}
        self.sb_stack = []
        self.ps_stack = []

    def inp(self, name, shape, dt=F32):
        b = Buf(self.nc.dram_tensor(name, shape, dt, kind="ExternalInput").ap(), name)
        self.ext_in[name] = b
        self.in_shapes[name] = (tuple(shape), dt)
        return b

    def out(self, name, shape, dt=F32):
        return Buf(self.nc.dram_tensor(name, shape, dt, kind="ExternalOutput").ap(), name)

    def scratch(self, name, shape, dt):
        return Buf(self.nc.dram_tensor(name, shape, dt).ap(), name)

    def sbuf(self, name, shape, dt):
        cm = self.nc.sbuf_tensor(name, shape, dt)
        h = cm.__enter__()
        b = Buf(h.ap(), name, cm)
        self.sb_stack.append(b)
        return b

    def psum(self, name, shape, dt=F32):
        cm = self.nc.psum_tensor(name, shape, dt)
        h = cm.__enter__()
        b = Buf(h.ap(), name, cm)
        self.ps_stack.append(b)
        return b

    def mark(self):
        return (len(self.sb_stack), len(self.ps_stack))

    def free_to(self, mark):
        ns, np_ = mark
        while len(self.sb_stack) > ns:
            b = self.sb_stack.pop()
            if b.cm is not None:
                b.cm.__exit__(None, None, None)
                b.cm = None
        while len(self.ps_stack) > np_:
            b = self.ps_stack.pop()
            if b.cm is not None:
                b.cm.__exit__(None, None, None)
                b.cm = None

    def finish(self):
        self.s.fence("sy")
        with self.nc.Block() as block:
            self.s.flush(block)
        self.free_to((0, 0))
        self.stack.__exit__(None, None, None)
        return self.nc



def ln0_rows(p, src_buf, src_ap, stat_buf, c0, scrap, F=DM):
    s = p.s
    mu = stat_buf.ap()[:, c0:c0 + 1]
    rs = stat_buf.ap()[:, c0 + 1:c0 + 2]
    s.op("dv", lambda h: h.reduce_sum(mu, src_ap, axis=AX.X),
         reads=[src_buf], writes=[stat_buf])
    s.op("sc", lambda h: h.activation(out=scrap.ap()[:, 0:F], in_=src_ap,
                                      func=AF.Square, accum_out=rs),
         reads=[src_buf], writes=[stat_buf, scrap])
    s.op("dv", lambda h: h.tensor_scalar_mul(out=mu, in0=mu, scalar1=1.0 / F),
         reads=[stat_buf], writes=[stat_buf])
    s.op("dv", lambda h: h.tensor_scalar_mul(out=rs, in0=rs, scalar1=1.0 / F),
         reads=[stat_buf], writes=[stat_buf])
    s.op("dv", lambda h: h.scalar_tensor_tensor(out=rs, in0=mu, scalar=mu, in1=rs,
                                                op0=ALU.mult, op1=ALU.subtract),
         reads=[stat_buf], writes=[stat_buf])
    s.op("sc", lambda h: h.activation(out=rs, in_=rs, func=AF.Sqrt, scale=-1.0,
                                      bias=EPS_LN), reads=[stat_buf], writes=[stat_buf])
    s.op("dv", lambda h: h.reciprocal(out=rs, in_=rs), reads=[stat_buf], writes=[stat_buf])


def cast_via_hbm(p, src_dram_ap, dst_dram_buf, nrows, ncols, stg, tag):
    s = p.s
    SW = stg.ap().shape[-1]
    rows_per = (128 * SW) // ncols // 128 * 128
    assert rows_per > 0
    r = 0
    while r < nrows:
        rr = min(rows_per, nrows - r)
        a = rr // 128
        s.op("gp", lambda h, r=r, rr=rr, a=a: h.dma_start(
            out=stg.ap()[:, 0:a * ncols].rearrange("p (a d) -> p a d", a=a),
            in_=src_dram_ap[r:r + rr, :].rearrange("(a p) d -> p a d", p=128)),
            writes=[stg], dma=True)
        s.op("sy", lambda h, r=r, rr=rr, a=a: h.dma_start(
            out=dst_dram_buf.ap()[r:r + rr, :].rearrange("(a p) d -> p a d", p=128),
            in_=stg.ap()[:, 0:a * ncols].rearrange("p (a d) -> p a d", a=a)),
            reads=[stg], writes=[dst_dram_buf], dma=True)
        r += rr



def emit_mamba(p, x, x1_hbm, upto=99.0):
    nc, s = p.nc, p.s
    mark_phase = p.mark()
    NT = L // 128
    NC = DI // 128
    NQ = 4
    QL = L // NQ

    ident, onesb = p.ident, p.onesb

    wb = p.scratch("m_wb", [2 * DI, DM], BF16)
    wdtp = p.scratch("m_wdtp", [DI, 128], BF16)
    wxp = p.scratch("m_wxp", [128, DI], BF16)
    wop = p.scratch("m_wop", [DM, DI], BF16)
    t2h = p.scratch("m_t2h", [L, DM], BF16)
    xch = p.scratch("m_xch", [NC, 128, L], BF16)
    zsh = p.scratch("m_zsh", [NC, 128, L], BF16)
    ygh = p.scratch("m_ygh", [NC, 128, L], BF16)

    mark_stg = p.mark()
    stg = p.sbuf("m_stg", [128, 8192], BF16)
    cast_via_hbm(p, p.ext_in["L0_attn_in_proj_w"].ap(), wb, 2 * DI, DM, stg, "wb")
    cast_via_hbm(p, p.ext_in["xpw_pad"].ap(), wxp, 128, DI, stg, "wxp")
    cast_via_hbm(p, p.ext_in["dtp_pad"].ap(), wdtp, DI, 128, stg, "wdtp")
    cast_via_hbm(p, p.ext_in["L0_attn_out_proj_w"].ap(), wop, DM, DI, stg, "wop")
    s.fence_all()
    p.free_to(mark_stg)
    if upto < 1:
        return

    gcol = p.sbuf("m_gcol", [128, 8], F32)
    bcol = p.sbuf("m_bcol", [128, 8], BF16)
    cvec = p.sbuf("m_cvec", [1, 4160], BF16)
    convw = p.sbuf("m_convw", [128, NC * 4], F32)
    convb = p.sbuf("m_convb", [128, NC], F32)
    dtb = p.sbuf("m_dtb", [128, NC], F32)
    dpar = p.sbuf("m_dpar", [128, NC], F32)
    asc = p.sbuf("m_asc", [128, NC * NST], F32)
    dtpT = p.sbuf("m_dtpT", [128, DI], BF16)
    xpwT = p.sbuf("m_xpwT", [128, NC * 128], BF16)
    musig = p.sbuf("m_musig", [128, 2 * NT], F32)
    st2 = p.sbuf("m_st2", [128, 2], F32)
    xdbl = p.sbuf("m_xdbl", [128, 4096], F32)
    mark_w = p.mark()
    pc1 = p.psum("m_pc1", [1, 512])
    inWT = p.sbuf("m_inWT", [128, 8 * 4096], BF16)
    t2T = p.sbuf("m_t2T", [128, 8 * 4096], BF16)

    for c in range(8):
        s.op("sy", lambda h, c=c: h.dma_start(
            out=gcol.ap()[:, c:c + 1],
            in_=p.ext_in["L0_attn_ln_gamma"].ap()[c * 128:(c + 1) * 128].unsqueeze(1)),
            writes=[gcol], dma=True)
        s.op("gp", lambda h, c=c: h.dma_start(
            out=bcol.ap()[:, c:c + 1],
            in_=p.ext_in["L0_attn_ln_beta"].ap()[c * 128:(c + 1) * 128].unsqueeze(1)),
            writes=[bcol], dma=True)
    for c in range(8):
        s.op("sc", lambda h, c=c: h.dma_start_transpose(
            inWT.ap()[:, c * 4096:(c + 1) * 4096], wb.ap()[:, c * 128:(c + 1) * 128]),
            reads=[wb], writes=[inWT], dma=True)
    for cc in range(8):
        for d in range(8):
            s.op("te", lambda h, cc=cc, d=d: h.matmul(
                pc1.ap(), bcol.ap()[:, d:d + 1],
                inWT.ap()[:, d * 4096 + cc * 512:d * 4096 + (cc + 1) * 512],
                start=(d == 0), stop=(d == 7)), reads=[bcol, inWT], writes=[pc1])
        s.op("dv", lambda h, cc=cc: h.tensor_copy(
            out=cvec.ap()[0:1, cc * 512:(cc + 1) * 512], in_=pc1.ap()),
            reads=[pc1], writes=[cvec])
    for c in range(8):
        s.op("dv", lambda h, c=c: h.tensor_scalar_mul(
            out=inWT.ap()[:, c * 4096:(c + 1) * 4096],
            in0=inWT.ap()[:, c * 4096:(c + 1) * 4096],
            scalar1=gcol.ap()[:, c:c + 1]), reads=[gcol, inWT], writes=[inWT])
    for c in range(NC):
        s.op("sy", lambda h, c=c: h.dma_start(
            out=convw.ap()[:, c * 4:(c + 1) * 4],
            in_=p.ext_in["L0_attn_conv_w"].ap()[c * 128:(c + 1) * 128, :]),
            writes=[convw], dma=True)
        s.op("sy", lambda h, c=c: h.dma_start(
            out=convb.ap()[:, c:c + 1],
            in_=p.ext_in["L0_attn_conv_b"].ap()[c * 128:(c + 1) * 128].unsqueeze(1)),
            writes=[convb], dma=True)
        s.op("sy", lambda h, c=c: h.dma_start(
            out=dtb.ap()[:, c:c + 1],
            in_=p.ext_in["L0_attn_dt_proj_b"].ap()[c * 128:(c + 1) * 128].unsqueeze(1)),
            writes=[dtb], dma=True)
        s.op("sy", lambda h, c=c: h.dma_start(
            out=dpar.ap()[:, c:c + 1],
            in_=p.ext_in["L0_attn_D_param"].ap()[c * 128:(c + 1) * 128].unsqueeze(1)),
            writes=[dpar], dma=True)
        s.op("sy", lambda h, c=c: h.dma_start(
            out=asc.ap()[:, c * NST:(c + 1) * NST],
            in_=p.ext_in["L0_attn_A_log"].ap()[c * 128:(c + 1) * 128, :]),
            writes=[asc], dma=True)
    s.op("sc", lambda h: h.activation(out=asc.ap(), in_=asc.ap(), func=AF.Exp),
         reads=[asc], writes=[asc])
    s.op("dv", lambda h: h.tensor_scalar_mul(out=asc.ap(), in0=asc.ap(), scalar1=-1.0),
         reads=[asc], writes=[asc])
    for c in range(NC):
        s.op("sc", lambda h, c=c: h.dma_start_transpose(
            xpwT.ap()[:, c * 128:(c + 1) * 128], wxp.ap()[:, c * 128:(c + 1) * 128]),
            reads=[wxp], writes=[xpwT], dma=True)
    s.op("sc", lambda h: h.dma_start_transpose(dtpT.ap(), wdtp.ap()),
         reads=[wdtp], writes=[dtpT], dma=True)
    if upto < 2:
        return

    mark_1a = p.mark()
    scrap = p.sbuf("m_scrap", [128, 1024], F32)
    xt = [p.sbuf(f"m_xt{i}", [128, 1024], F32) for i in range(2)]
    tt = [p.sbuf(f"m_tt{i}", [128, 1024], F32) for i in range(2)]
    t2t = [p.sbuf(f"m_t2t{i}", [128, 1024], BF16) for i in range(2)]

    for i in range(NT):
        xb, tb, t2b = xt[i % 2], tt[i % 2], t2t[i % 2]
        s.op("sy", lambda h, i=i, xb=xb: h.dma_start(
            out=xb.ap(), in_=x.ap()[i * 128:(i + 1) * 128, :]), writes=[xb], dma=True)
        ln0_rows(p, xb, xb.ap(), musig, 2 * i, scrap)
        mu = musig.ap()[:, 2 * i:2 * i + 1]
        rs = musig.ap()[:, 2 * i + 1:2 * i + 2]
        s.op("dv", lambda h, xb=xb, tb=tb, mu=mu, rs=rs: h.tensor_scalar(
            out=tb.ap(), in0=xb.ap(), scalar1=mu, scalar2=rs,
            op0=ALU.subtract, op1=ALU.mult), reads=[xb, musig], writes=[tb])
        ln0_rows(p, tb, tb.ap(), st2, 0, scrap)
        s.op("dv", lambda h, tb=tb, t2b=t2b: h.tensor_scalar(
            out=t2b.ap(), in0=tb.ap(), scalar1=st2.ap()[:, 0:1],
            scalar2=st2.ap()[:, 1:2], op0=ALU.subtract, op1=ALU.mult),
            reads=[tb, st2], writes=[t2b])
        s.op("sy", lambda h, i=i, t2b=t2b: h.dma_start(
            out=t2h.ap()[i * 128:(i + 1) * 128, :], in_=t2b.ap()),
            reads=[t2b], writes=[t2h], dma=True)
    for c in range(8):
        s.op("sc", lambda h, c=c: h.dma_start_transpose(
            t2T.ap()[:, c * 4096:(c + 1) * 4096], t2h.ap()[:, c * 128:(c + 1) * 128]),
            reads=[t2h], writes=[t2T], dma=True)
    s.fence_all()
    p.free_to(mark_1a)
    if upto < 3:
        return

    xs_sb = p.sbuf("m_xs", [128, 4100], F32)
    cacc = p.sbuf("m_cacc", [128, 4096], F32)
    xc_sb = p.sbuf("m_xcsb", [128, 4096], BF16)
    pxz = [p.psum(f"m_pxz{i}", [128, 512]) for i in range(2)]
    pxd = p.psum("m_pxd", [128, 512])

    def inproj_chunk(row0, evac):
        for tb in range(8):
            ps = pxz[tb % 2]

            def grp(h, row0=row0, tb=tb, ps=ps):
                for d in range(8):
                    h.matmul(
                        ps.ap(), inWT.ap()[:, d * 4096 + row0:d * 4096 + row0 + 128],
                        t2T.ap()[:, d * 4096 + tb * 512:d * 4096 + (tb + 1) * 512],
                        start=(d == 0), stop=False)
                return h.matmul(
                    ps.ap(), cvec.ap()[0:1, row0:row0 + 128], onesb.ap()[0:1, 0:512],
                    start=False, stop=True)
            s.op("te", grp, reads=[inWT, t2T, cvec, onesb], writes=[ps])
            evac(tb, ps)

    s.op("dv", lambda h: h.memset(xs_sb.ap()[:, 0:4], 0.0), writes=[xs_sb])
    for cc in range(NC):
        def evac_xs(tb, ps):
            s.op("sc", lambda h, tb=tb, ps=ps: h.copy(
                out=xs_sb.ap()[:, 4 + tb * 512:4 + (tb + 1) * 512], in_=ps.ap()),
                reads=[ps], writes=[xs_sb])
        inproj_chunk(cc * 128, evac_xs)
        s.op("dv", lambda h, cc=cc: h.tensor_scalar_mul(
            out=cacc.ap(), in0=xs_sb.ap()[:, 1:4097],
            scalar1=convw.ap()[:, cc * 4:cc * 4 + 1]),
            reads=[xs_sb, convw], writes=[cacc])
        for j in (1, 2, 3):
            s.op("dv", lambda h, cc=cc, j=j: h.scalar_tensor_tensor(
                out=cacc.ap(), in0=xs_sb.ap()[:, 1 + j:4097 + j],
                scalar=convw.ap()[:, cc * 4 + j:cc * 4 + j + 1],
                in1=cacc.ap(), op0=ALU.mult, op1=ALU.add),
                reads=[xs_sb, convw, cacc], writes=[cacc])
        s.op("sc", lambda h, cc=cc: h.activation(
            out=xc_sb.ap(), in_=cacc.ap(), func=AF.Silu,
            bias=convb.ap()[:, cc:cc + 1]), reads=[cacc, convb], writes=[xc_sb])
        s.op("sy", lambda h, cc=cc: h.dma_start(
            out=xch.ap()[cc], in_=xc_sb.ap()), reads=[xc_sb], writes=[xch], dma=True)
        for tb in range(8):
            s.op("te", lambda h, cc=cc, tb=tb: h.matmul(
                pxd.ap(), xpwT.ap()[:, cc * 128:(cc + 1) * 128],
                xc_sb.ap()[:, tb * 512:(tb + 1) * 512], start=True, stop=True),
                reads=[xpwT, xc_sb], writes=[pxd])
            if cc == 0:
                s.op("dv", lambda h, tb=tb: h.tensor_copy(
                    out=xdbl.ap()[:, tb * 512:(tb + 1) * 512], in_=pxd.ap()),
                    reads=[pxd], writes=[xdbl])
            else:
                s.op("dv", lambda h, tb=tb: h.tensor_tensor(
                    out=xdbl.ap()[:, tb * 512:(tb + 1) * 512],
                    in0=xdbl.ap()[:, tb * 512:(tb + 1) * 512],
                    in1=pxd.ap(), op=ALU.add), reads=[pxd, xdbl], writes=[xdbl])
    for zc in range(NC):
        def evac_z(tb, ps):
            s.op("sc", lambda h, tb=tb, ps=ps: h.activation(
                out=xc_sb.ap()[:, tb * 512:(tb + 1) * 512], in_=ps.ap(),
                func=AF.Silu), reads=[ps], writes=[xc_sb])
        inproj_chunk(DI + zc * 128, evac_z)
        s.op("sy", lambda h, zc=zc: h.dma_start(
            out=zsh.ap()[zc], in_=xc_sb.ap()), reads=[xc_sb], writes=[zsh], dma=True)
    s.fence_all()
    p.free_to(mark_w)
    if upto < 4:
        return

    mark_2 = p.mark()
    dtlT = p.sbuf("m_dtlT", [64, 4096], BF16)
    s.op("dv", lambda h: h.tensor_copy(out=dtlT.ap(), in_=xdbl.ap()[0:64, :]),
         reads=[xdbl], writes=[dtlT])
    Bb = p.sbuf("m_Bb", [128, NST * QL], BF16)
    Cb = p.sbuf("m_Cb", [128, NST * QL], BF16)
    rowb = [p.sbuf(f"m_rowb{i}", [1, QL], BF16) for i in range(2)]
    dt_sb = p.sbuf("m_dtsb", [128, QL], F32)
    rbuf = p.sbuf("m_rbuf", [128, QL], F32)
    abuf = p.sbuf("m_abuf", [128, QL], F32)
    lbuf = p.sbuf("m_lbuf", [128, QL], F32)
    ubuf = p.sbuf("m_ubuf", [128, QL], BF16)
    xcq = p.sbuf("m_xcq", [128, QL], BF16)
    zsq = p.sbuf("m_zsq", [128, QL], BF16)
    dab = [p.sbuf(f"m_dab{i}", [128, QL], BF16) for i in range(2)]
    dxb = [p.sbuf(f"m_dxb{i}", [128, QL], BF16) for i in range(2)]
    hbuf = [p.sbuf(f"m_hbuf{i}", [128, QL], BF16) for i in range(2)]
    zbuf = [p.sbuf(f"m_zbuf{i}", [128, QL], BF16) for i in range(2)]
    hmid = p.sbuf("m_hmid", [128, NC * NST], BF16)
    gb = p.sbuf("m_gb", [128, QL], BF16)
    pdt = p.psum("m_pdt", [128, QL])
    pyy = p.psum("m_pyy", [128, QL])
    pbc = p.psum("m_pbc", [128, QL])

    for q in range(NQ if upto >= 4.9 else 1):
        q0 = q * QL
        for n in range((2 * NST) if upto >= 4.1 else 0):
            rb = rowb[n % 2]
            dst = Bb if n < NST else Cb
            nn = n % NST
            s.op("gp", lambda h, n=n, q0=q0, rb=rb: h.dma_start(
                out=rb.ap(), in_=xdbl.ap()[64 + n:65 + n, q0:q0 + QL]),
                reads=[xdbl], writes=[rb], dma=True)
            def bcgrp(h, rb=rb):
                for sub in range(QL // 512):
                    ins = h.matmul(
                        pbc.ap()[:, sub * 512:(sub + 1) * 512], onesb.ap()[0:1, 0:128],
                        rb.ap()[0:1, sub * 512:(sub + 1) * 512], start=True, stop=True)
                return ins
            s.op("te", bcgrp, reads=[onesb, rb], writes=[pbc])
            s.op("dv", lambda h, nn=nn, dst=dst: h.tensor_copy(
                out=dst.ap()[:, nn * QL:(nn + 1) * QL], in_=pbc.ap()),
                reads=[pbc], writes=[dst])
        for cc in range(NC if upto >= 4.2 else 0):
            def dtgrp(h, cc=cc, q0=q0):
                for sub in range(QL // 512):
                    ins = h.matmul(
                        pdt.ap()[:, sub * 512:(sub + 1) * 512],
                        dtpT.ap()[0:64, cc * 128:(cc + 1) * 128],
                        dtlT.ap()[:, q0 + sub * 512:q0 + (sub + 1) * 512],
                        start=True, stop=True)
                return ins
            s.op("te", dtgrp, reads=[dtpT, dtlT], writes=[pdt])
            if upto < 4.21:
                continue
            s.op("dv", lambda h, cc=cc: h.tensor_scalar(
                out=rbuf.ap(), in0=pdt.ap(), scalar1=dtb.ap()[:, cc:cc + 1],
                scalar2=0.0, op0=ALU.add, op1=ALU.max),
                reads=[pdt, dtb], writes=[rbuf])
            if upto < 4.212:
                continue
            s.op("dv", lambda h: h.scalar_tensor_tensor(
                out=abuf.ap(), in0=rbuf.ap(), scalar=-2.0, in1=pdt.ap(),
                op0=ALU.mult, op1=ALU.add), reads=[rbuf, pdt], writes=[abuf])
            if upto < 4.213:
                continue
            s.op("sc", lambda h, cc=cc: h.activation(out=abuf.ap(), in_=abuf.ap(),
                                              func=AF.Exp,
                                              bias=dtb.ap()[:, cc:cc + 1]),
                 reads=[abuf, dtb], writes=[abuf])
            if upto < 4.214:
                continue
            s.op("sc", lambda h: h.activation(out=lbuf.ap(), in_=abuf.ap(),
                                              func=AF.Ln, bias=1.0),
                 reads=[abuf], writes=[lbuf])
            if upto < 4.215:
                continue
            s.op("dv", lambda h: h.tensor_tensor(out=dt_sb.ap(), in0=rbuf.ap(),
                                                 in1=lbuf.ap(), op=ALU.add),
                 reads=[rbuf, lbuf], writes=[dt_sb])
            if upto < 4.22:
                continue
            s.op("sy", lambda h, cc=cc, q0=q0: h.dma_start(
                out=xcq.ap(), in_=xch.ap()[cc][:, q0:q0 + QL]), writes=[xcq], dma=True)
            s.op("sy", lambda h, cc=cc, q0=q0: h.dma_start(
                out=zsq.ap(), in_=zsh.ap()[cc][:, q0:q0 + QL]), writes=[zsq], dma=True)
            if upto < 4.23:
                continue
            s.op("dv", lambda h: h.tensor_tensor(out=ubuf.ap(), in0=dt_sb.ap(),
                                                 in1=xcq.ap(), op=ALU.mult),
                 reads=[dt_sb, xcq], writes=[ubuf])
            for n in range(NST if upto >= 4.3 else 0):
                da, dx, hb, zb = dab[n % 2], dxb[n % 2], hbuf[n % 2], zbuf[n % 2]
                s.op("sc", lambda h, cc=cc, n=n, da=da: h.activation(
                    out=da.ap(), in_=dt_sb.ap(), func=AF.Exp,
                    scale=asc.ap()[:, cc * NST + n:cc * NST + n + 1]),
                    reads=[dt_sb, asc], writes=[da])
                s.op("gp", lambda h, n=n, dx=dx: h.tensor_tensor(
                    out=dx.ap(), in0=ubuf.ap(), in1=Bb.ap()[:, n * QL:(n + 1) * QL],
                    op=ALU.mult), reads=[ubuf, Bb], writes=[dx])
                init = 0.0 if q == 0 else hmid.ap()[:, cc * NST + n:cc * NST + n + 1]
                s.op("dv", lambda h, da=da, dx=dx, hb=hb, init=init: h.tensor_tensor_scan(
                    out=hb.ap(), data0=da.ap(), data1=dx.ap(), initial=init,
                    op0=ALU.mult, op1=ALU.add),
                    reads=[da, dx] + ([hmid] if q > 0 else []), writes=[hb])
                if q < NQ - 1:
                    s.op("sc", lambda h, cc=cc, n=n, hb=hb: h.copy(
                        out=hmid.ap()[:, cc * NST + n:cc * NST + n + 1],
                        in_=hb.ap()[:, QL - 1:QL]), reads=[hb], writes=[hmid])
                s.op("dv", lambda h, n=n, hb=hb, zb=zb: h.tensor_tensor(
                    out=zb.ap(), in0=hb.ap(), in1=Cb.ap()[:, n * QL:(n + 1) * QL],
                    op=ALU.mult), reads=[hb, Cb], writes=[zb])
                def ygrp(h, n=n, zb=zb):
                    for sub in range(QL // 512):
                        ins = h.matmul(
                            pyy.ap()[:, sub * 512:(sub + 1) * 512], ident.ap(),
                            zb.ap()[:, sub * 512:(sub + 1) * 512],
                            start=(n == 0), stop=(n == NST - 1))
                    return ins
                s.op("te", ygrp, reads=[ident, zb], writes=[pyy])
            if upto < 4.3:
                continue
            s.op("dv", lambda h, cc=cc: h.scalar_tensor_tensor(
                out=gb.ap(), in0=xcq.ap(), scalar=dpar.ap()[:, cc:cc + 1],
                in1=pyy.ap(), op0=ALU.mult, op1=ALU.add),
                reads=[xcq, dpar, pyy], writes=[gb])
            s.op("dv", lambda h: h.tensor_tensor(out=gb.ap(), in0=gb.ap(),
                                                 in1=zsq.ap(), op=ALU.mult),
                 reads=[gb, zsq], writes=[gb])
            s.op("sy", lambda h, cc=cc, q0=q0: h.dma_start(
                out=ygh.ap()[cc][:, q0:q0 + QL], in_=gb.ap()),
                reads=[gb], writes=[ygh], dma=True)
    s.fence_all()
    p.free_to(mark_2)
    if upto < 5:
        return

    mark_3 = p.mark()
    opwT = p.sbuf("m_opwT", [128, NC * 1024], BF16)
    scrap = p.sbuf("m_scrap3", [128, 1024], F32)
    xt = [p.sbuf(f"m_3xt{i}", [128, 1024], F32) for i in range(2)]
    tt = [p.sbuf(f"m_3tt{i}", [128, 1024], F32) for i in range(2)]
    sbf = p.sbuf("m_sbf", [128, 1024], F32)
    x1t = [p.sbuf(f"m_x1t{i}", [128, 1024], BF16) for i in range(2)]
    ygt = [p.sbuf(f"m_ygt{i}", [128, 2048], BF16) for i in range(2)]
    pout = p.psum("m_pout", [128, 1024])

    for c in range(NC):
        s.op("sc", lambda h, c=c: h.dma_start_transpose(
            opwT.ap()[:, c * 1024:(c + 1) * 1024], wop.ap()[:, c * 128:(c + 1) * 128]),
            reads=[wop], writes=[opwT], dma=True)

    for i in range(NT):
        ygs = ygt[i % 2]
        for cc in range(NC):
            s.op("sy", lambda h, i=i, cc=cc, ygs=ygs: h.dma_start(
                out=ygs.ap()[:, cc * 128:(cc + 1) * 128],
                in_=ygh.ap()[cc][:, i * 128:(i + 1) * 128]),
                writes=[ygs], dma=True)

        def opgrp(h, ygs=ygs):
            for cc in range(NC):
                for dh in range(2):
                    ins = h.matmul(
                        pout.ap()[:, dh * 512:(dh + 1) * 512],
                        ygs.ap()[:, cc * 128:(cc + 1) * 128],
                        opwT.ap()[:, cc * 1024 + dh * 512:cc * 1024 + (dh + 1) * 512],
                        start=(cc == 0), stop=(cc == NC - 1))
            return ins
        s.op("te", opgrp, reads=[ygs, opwT], writes=[pout])
        xb, tb = xt[i % 2], tt[i % 2]
        s.op("sy", lambda h, i=i, xb=xb: h.dma_start(
            out=xb.ap(), in_=x.ap()[i * 128:(i + 1) * 128, :]), writes=[xb], dma=True)
        mu = musig.ap()[:, 2 * i:2 * i + 1]
        rs = musig.ap()[:, 2 * i + 1:2 * i + 2]
        s.op("dv", lambda h, xb=xb, tb=tb, mu=mu, rs=rs: h.tensor_scalar(
            out=tb.ap(), in0=xb.ap(), scalar1=mu, scalar2=rs,
            op0=ALU.subtract, op1=ALU.mult), reads=[xb, musig], writes=[tb])
        s.op("dv", lambda h, tb=tb: h.tensor_tensor(
            out=sbf.ap(), in0=pout.ap(), in1=tb.ap(), op=ALU.add),
            reads=[pout, tb], writes=[sbf])
        ln0_rows(p, sbf, sbf.ap(), st2, 0, scrap)
        x1b = x1t[i % 2]
        s.op("dv", lambda h: h.tensor_scalar(
            out=sbf.ap(), in0=sbf.ap(), scalar1=st2.ap()[:, 0:1],
            scalar2=st2.ap()[:, 1:2], op0=ALU.subtract, op1=ALU.mult),
            reads=[sbf, st2], writes=[sbf])
        s.op("dv", lambda h, xb=xb, x1b=x1b: h.tensor_tensor(
            out=x1b.ap(), in0=sbf.ap(), in1=xb.ap(), op=ALU.add),
            reads=[sbf, xb], writes=[x1b])
        s.op("sy", lambda h, i=i, x1b=x1b: h.dma_start(
            out=x1_hbm.ap()[i * 128:(i + 1) * 128, :], in_=x1b.ap()),
            reads=[x1b], writes=[x1_hbm], dma=True)
    s.fence_all()
    p.free_to(mark_phase)



CONST_VALS = [0.0, 1.0, -1.0, 0.5, -0.5, 1e-5, 1e-6, 2.0]


def make_consts(p):
    identf_h = p.inp("c_identf", [128, 128], F32)
    ident_h = p.inp("c_ident", [128, 128], BF16)
    ones_h = p.inp("c_ones", [1, 512], BF16)
    cf_h = p.inp("c_f32", [128, len(CONST_VALS)], F32)
    p.ident = p.sbuf("c_ident_sb", [128, 128], BF16)
    p.identf = p.sbuf("c_identf_sb", [128, 128], F32)
    p.onesb = p.sbuf("c_ones_sb", [1, 512], BF16)
    p.cpool = p.sbuf("c_f32_sb", [128, len(CONST_VALS)], F32)
    p.cpoolb = p.sbuf("c_bf_sb", [128, len(CONST_VALS)], BF16)
    p.s.op("sy", lambda h: h.dma_start(out=p.ident.ap(), in_=ident_h.ap()),
           writes=[p.ident], dma=True)
    p.s.op("sy", lambda h: h.dma_start(out=p.identf.ap(), in_=identf_h.ap()),
           writes=[p.identf], dma=True)
    p.s.op("sy", lambda h: h.dma_start(out=p.onesb.ap(), in_=ones_h.ap()),
           writes=[p.onesb], dma=True)
    p.s.op("sy", lambda h: h.dma_start(out=p.cpool.ap(), in_=cf_h.ap()),
           writes=[p.cpool], dma=True)
    p.s.op("gp", lambda h: h.dma_start(out=p.cpoolb.ap(), in_=cf_h.ap()),
           writes=[p.cpoolb], dma=True)
    for i, v in enumerate(CONST_VALS):
        p.nc.const_aps.register(F32, v, p.cpool.ap()[:, i:i + 1])
        p.nc.const_aps.register(BF16, v, p.cpoolb.ap()[:, i:i + 1])


def const_arrays():
    return {
        "c_ident": np.eye(128, dtype=BF16NP),
        "c_identf": np.eye(128, dtype=np.float32),
        "c_ones": np.ones((1, 512), dtype=BF16NP),
        "c_f32": np.tile(np.array([CONST_VALS], dtype=np.float32), (128, 1)),
    }


M_WEIGHTS = ("L0_attn_ln_gamma", "L0_attn_ln_beta", "L0_attn_in_proj_w",
             "L0_attn_conv_w", "L0_attn_conv_b", "L0_attn_dt_proj_w",
             "L0_attn_dt_proj_b", "L0_attn_A_log", "L0_attn_D_param",
             "L0_attn_out_proj_w")


def build_prog_m(upto=99.0):
    p = Prog("m")
    make_consts(p)
    x = p.inp("x", [L, DM], F32)
    for nm, shp in (("L0_attn_ln_gamma", [DM]), ("L0_attn_ln_beta", [DM]),
                    ("L0_attn_in_proj_w", [2 * DI, DM]),
                    ("L0_attn_conv_w", [DI, 4]), ("L0_attn_conv_b", [DI]),
                    ("xpw_pad", [128, DI]),
                    ("dtp_pad", [DI, 128]), ("L0_attn_dt_proj_b", [DI]),
                    ("L0_attn_A_log", [DI, NST]), ("L0_attn_D_param", [DI]),
                    ("L0_attn_out_proj_w", [DM, DI])):
        p.inp(nm, shp, F32)
    x1 = p.out("x1", [L, DM], BF16)
    emit_mamba(p, x, x1, upto)
    return p





def emit_ffn(p, x1, x2_hbm):
    nc, s = p.nc, p.s
    mark_phase = p.mark()
    NT = L // 128
    ident, onesb = p.ident, p.onesb

    wq1h = p.scratch("f_wq1h", [DFF, DM], BF16)
    wq2h = p.scratch("f_wq2h", [DM, DFF], BF16)
    xn1h = p.scratch("f_xn1h", [L, DM], BF16)
    gscr = p.scratch("f_gscr", [1, 128], F32)

    mark_f = p.mark()
    st2 = p.sbuf("f_st2", [128, 2], F32)
    scrap = p.sbuf("f_scrap", [128, 1024], F32)

    wt = [p.sbuf(f"f_wt{i}", [128, 4096], F32) for i in range(2)]
    asum = p.sbuf("f_asum", [128, 64], F32)
    grow = p.sbuf("f_grow", [1, 130], F32)
    rg = p.sbuf("f_rg", [128, 2], F32)
    wq = p.sbuf("f_wq", [128, 4096], BF16)
    za = p.sbuf("f_za", [128, 4096], F32)
    zb = p.sbuf("f_zb", [128, 4096], F32)

    def quant(widx, src_name, nrows, ncols, dst):
        ntile = nrows // 128
        for t in range(ntile):
            w = wt[t % 2]
            s.op("sy", lambda h, t=t, w=w: h.dma_start(
                out=w.ap()[:, 0:ncols],
                in_=p.ext_in[src_name].ap()[t * 128:(t + 1) * 128, :]),
                writes=[w], dma=True)
            s.op("dv", lambda h, t=t, w=w: h.tensor_reduce(
                out=asum.ap()[:, t:t + 1], in_=w.ap()[:, 0:ncols],
                axis=AX.X, op=ALU.add, apply_absolute_value=True),
                reads=[w], writes=[asum])
        s.op("dv", lambda h, ntile=ntile: h.tensor_reduce(
            out=rg.ap()[:, 0:1], in_=asum.ap()[:, 0:ntile], axis=AX.X, op=ALU.add),
            reads=[asum], writes=[rg])
        s.op("gp", lambda h: h.dma_start(out=grow.ap()[0:1, 0:128],
                                         in_=rg.ap()[:, 0:1]),
             reads=[rg], writes=[grow], dma=True)
        s.op("dv", lambda h: h.tensor_reduce(
            out=grow.ap()[0:1, 128:129], in_=grow.ap()[0:1, 0:128],
            axis=AX.X, op=ALU.add), reads=[grow], writes=[grow])
        s.op("dv", lambda h, n=nrows * ncols: h.tensor_scalar(
            out=grow.ap()[0:1, 128:129], in0=grow.ap()[0:1, 128:129],
            scalar1=1.0 / n, scalar2=EPS_LN, op0=ALU.mult, op1=ALU.add),
            reads=[grow], writes=[grow])
        s.op("dv", lambda h: h.reciprocal(out=grow.ap()[0:1, 129:130],
                                          in_=grow.ap()[0:1, 128:129]),
             reads=[grow], writes=[grow])
        s.op("sy", lambda h: h.dma_start(out=gscr.ap()[0:1, widx:widx + 1],
                                         in_=grow.ap()[0:1, 129:130]),
             reads=[grow], writes=[p.gscr_buf], dma=True)
        s.op("sy", lambda h: h.dma_start(
            out=rg.ap()[:, 1:2],
            in_=gscr.ap()[0:1, widx:widx + 1].partition_broadcast(128)),
            reads=[p.gscr_buf], writes=[rg], dma=True)
        for t in range(ntile):
            w = wt[t % 2]
            s.op("sy", lambda h, t=t, w=w: h.dma_start(
                out=w.ap()[:, 0:ncols],
                in_=p.ext_in[src_name].ap()[t * 128:(t + 1) * 128, :]),
                writes=[w], dma=True)
            s.op("dv", lambda h, w=w: h.tensor_scalar_mul(
                out=w.ap()[:, 0:ncols], in0=w.ap()[:, 0:ncols],
                scalar1=rg.ap()[:, 1:2]), reads=[w, rg], writes=[w])
            s.op("dv", lambda h, w=w: h.tensor_scalar(
                out=za.ap()[:, 0:ncols], in0=w.ap()[:, 0:ncols],
                scalar1=0.5, scalar2=None, op0=ALU.is_ge),
                reads=[w], writes=[za])
            s.op("dv", lambda h, w=w: h.tensor_scalar(
                out=zb.ap()[:, 0:ncols], in0=w.ap()[:, 0:ncols],
                scalar1=-0.5, scalar2=None, op0=ALU.is_le),
                reads=[w], writes=[zb])
            s.op("dv", lambda h: h.tensor_tensor(
                out=wq.ap()[:, 0:ncols], in0=za.ap()[:, 0:ncols],
                in1=zb.ap()[:, 0:ncols], op=ALU.subtract),
                reads=[za, zb], writes=[wq])
            s.op("sy", lambda h, t=t: h.dma_start(
                out=dst.ap()[t * 128:(t + 1) * 128, :], in_=wq.ap()[:, 0:ncols]),
                reads=[wq], writes=[dst], dma=True)

    p.gscr_buf = Buf(gscr.ap(), "gscr")
    quant(0, "L0_ffn_W1", DFF, DM, wq1h)
    quant(1, "L0_ffn_W2", DM, DFF, wq2h)

    x1t = [p.sbuf(f"f_x1t{i}", [128, 1024], BF16) for i in range(2)]
    xnt = [p.sbuf(f"f_xnt{i}", [128, 1024], BF16) for i in range(2)]
    for i in range(NT):
        xb, xn = x1t[i % 2], xnt[i % 2]
        s.op("sy", lambda h, i=i, xb=xb: h.dma_start(
            out=xb.ap(), in_=x1.ap()[i * 128:(i + 1) * 128, :]), writes=[xb], dma=True)
        ln0_rows(p, xb, xb.ap(), st2, 0, scrap)
        s.op("dv", lambda h, xb=xb, xn=xn: h.tensor_scalar(
            out=xn.ap(), in0=xb.ap(), scalar1=st2.ap()[:, 0:1],
            scalar2=st2.ap()[:, 1:2], op0=ALU.subtract, op1=ALU.mult),
            reads=[xb, st2], writes=[xn])
        s.op("sy", lambda h, i=i, xn=xn: h.dma_start(
            out=xn1h.ap()[i * 128:(i + 1) * 128, :], in_=xn.ap()),
            reads=[xn], writes=[xn1h], dma=True)
    s.fence_all()
    p.free_to(mark_f)

    mark_mm = p.mark()
    st2 = p.sbuf("f_st2b", [128, 2], F32)
    scrap = p.sbuf("f_scrapb", [128, 1024], F32)
    w1T = p.sbuf("f_w1T", [128, 8 * 4096], BF16)
    w2T = p.sbuf("f_w2T", [128, 32 * 1024], BF16)
    b1c = p.sbuf("f_b1c", [128, 32], F32)
    b2r = p.sbuf("f_b2r", [1, 1024], BF16)
    xn1g = p.sbuf("f_xn1g", [128, 8 * 512], BF16)
    h_sb = p.sbuf("f_h", [128, 32 * 512], BF16)
    x1s = p.sbuf("f_x1s", [128, 1024], BF16)
    x2s = p.sbuf("f_x2s", [128, 1024], BF16)
    sfb = p.sbuf("f_sfb", [128, 1024], F32)
    ph = [p.psum(f"f_ph{i}", [128, 512]) for i in range(2)]
    py = p.psum("f_py", [128, 1024])

    for c in range(8):
        s.op("sc", lambda h, c=c: h.dma_start_transpose(
            w1T.ap()[:, c * 4096:(c + 1) * 4096], wq1h.ap()[:, c * 128:(c + 1) * 128]),
            reads=[wq1h], writes=[w1T], dma=True)
    for c in range(32):
        s.op("sc", lambda h, c=c: h.dma_start_transpose(
            w2T.ap()[:, c * 1024:(c + 1) * 1024], wq2h.ap()[:, c * 128:(c + 1) * 128]),
            reads=[wq2h], writes=[w2T], dma=True)
    for c in range(32):
        s.op("sy", lambda h, c=c: h.dma_start(
            out=b1c.ap()[:, c:c + 1],
            in_=p.ext_in["L0_ffn_b1"].ap()[c * 128:(c + 1) * 128].unsqueeze(1)),
            writes=[b1c], dma=True)
    s.op("gp", lambda h: h.dma_start(out=b2r.ap(),
                                     in_=p.ext_in["L0_ffn_b2"].ap().unsqueeze(0)),
         writes=[b2r], dma=True)

    for g in range(8):
        for d in range(8):
            s.op("sc", lambda h, g=g, d=d: h.dma_start_transpose(
                xn1g.ap()[:, d * 512:(d + 1) * 512],
                xn1h.ap()[g * 512:(g + 1) * 512, d * 128:(d + 1) * 128]),
                reads=[xn1h], writes=[xn1g], dma=True)
        for fc in range(32):
            pp = ph[fc % 2]

            def f1grp(h, fc=fc, pp=pp):
                for d in range(8):
                    ins = h.matmul(
                        pp.ap(), w1T.ap()[:, d * 4096 + fc * 128:d * 4096 + (fc + 1) * 128],
                        xn1g.ap()[:, d * 512:(d + 1) * 512],
                        start=(d == 0), stop=(d == 7))
                return ins
            s.op("te", f1grp, reads=[w1T, xn1g], writes=[pp])
            s.op("sc", lambda h, fc=fc, pp=pp: h.activation(
                out=h_sb.ap()[:, fc * 512:(fc + 1) * 512], in_=pp.ap(),
                func=AF.Relu, bias=b1c.ap()[:, fc:fc + 1]),
                reads=[pp, b1c], writes=[h_sb])
        for ts in range(4):
            i = g * 4 + ts

            def f2grp(h, ts=ts):
                for fc in range(32):
                    for dh in range(2):
                        ins = h.matmul(
                            py.ap()[:, dh * 512:(dh + 1) * 512],
                            h_sb.ap()[:, fc * 512 + ts * 128:fc * 512 + (ts + 1) * 128],
                            w2T.ap()[:, fc * 1024 + dh * 512:fc * 1024 + (dh + 1) * 512],
                            start=(fc == 0), stop=False)
                for dh in range(2):
                    ins = h.matmul(
                        py.ap()[:, dh * 512:(dh + 1) * 512], onesb.ap()[0:1, 0:128],
                        b2r.ap()[0:1, dh * 512:(dh + 1) * 512],
                        start=False, stop=(dh == 1))
                return ins
            s.op("te", f2grp, reads=[h_sb, w2T, onesb, b2r], writes=[py])
            s.op("sy", lambda h, i=i: h.dma_start(
                out=x1s.ap(), in_=x1.ap()[i * 128:(i + 1) * 128, :]),
                writes=[x1s], dma=True)
            ln0_rows(p, py, py.ap(), st2, 0, scrap)
            s.op("dv", lambda h: h.tensor_scalar(
                out=sfb.ap(), in0=py.ap(), scalar1=st2.ap()[:, 0:1],
                scalar2=st2.ap()[:, 1:2], op0=ALU.subtract, op1=ALU.mult),
                reads=[py, st2], writes=[sfb])
            s.op("dv", lambda h: h.tensor_tensor(
                out=x2s.ap(), in0=sfb.ap(), in1=x1s.ap(), op=ALU.add),
                reads=[sfb, x1s], writes=[x2s])
            s.op("sy", lambda h, i=i: h.dma_start(
                out=x2_hbm.ap()[i * 128:(i + 1) * 128, :], in_=x2s.ap()),
                reads=[x2s], writes=[x2_hbm], dma=True)
    s.fence_all()
    p.free_to(mark_phase)


def build_prog_f():
    p = Prog("f")
    make_consts(p)
    x1 = p.inp("x1", [L, DM], BF16)
    for nm, shp in (("L0_ffn_W1", [DFF, DM]), ("L0_ffn_b1", [DFF]),
                    ("L0_ffn_W2", [DM, DFF]), ("L0_ffn_b2", [DM])):
        p.inp(nm, shp, F32)
    x2 = p.out("x2", [L, DM], BF16)
    emit_ffn(p, x1, x2)
    return p





def blocks_of(parity):
    A, Bl = [], []
    for k in range(NBLK // 4):
        if parity == 0:
            A.append(4 * k)
            Bl.append(4 * k + 3)
        else:
            A.append(4 * k + 1)
            Bl.append(4 * k + 2)
    return A, Bl


def emit_attn(p, x2, x3_hbm):
    nc, s = p.nc, p.s
    mark_phase = p.mark()
    NT = L // 128
    NQT = QOWN // 128
    ident, onesb = p.ident, p.onesb
    NHH = 65

    xn2h = p.scratch("a_xn2h", [L, DM], BF16)
    xo2h = p.scratch("a_xo2h", [QOWN, DM], BF16)
    wqb = p.scratch("a_wqb", [DM, DM], BF16)
    wkvb = p.scratch("a_wkvb", [512, DM], BF16)
    wob = p.scratch("a_wob", [DM, DM], BF16)
    qph = p.scratch("a_qph", [QOWN, DM], BF16)
    kph = p.scratch("a_kph", [L, 256], BF16)
    y1h = p.scratch("a_y1h", [NH * NHH, QOWN], BF16)
    y2h = p.scratch("a_y2h", [NH * NHH, QOWN], BF16)
    ynh = p.scratch("a_ynh", [QOWN, DM], BF16)
    lamh = p.scratch("a_lamh", [1, NH], F32)

    mark_a1 = p.mark()
    stg = p.sbuf("a_stg", [128, 8192], BF16)
    cast_via_hbm(p, p.ext_in["L1_attn_Wq"].ap(), wqb, DM, DM, stg, "wq")
    cast_via_hbm(p, p.ext_in["L1_attn_Wkv"].ap(), wkvb, 512, DM, stg, "wkv")
    cast_via_hbm(p, p.ext_in["L1_attn_Wout"].ap(), wob, DM, DM, stg, "wo")
    s.fence_all()
    p.free_to(mark_a1)

    st2 = p.sbuf("a_st2", [128, 2], F32)
    scrap = p.sbuf("a_scrap", [128, 1024], F32)
    lam_b = p.sbuf("a_lamb", [128, NH], F32)
    rmsw = p.sbuf("a_rmsw", [128, HD], F32)
    flg = p.sbuf("a_flg", [128, 2], F32)

    s.op("sy", lambda h: h.dma_start(
        out=lam_b.ap()[0:1, :], in_=p.ext_in["L1_attn_lambda_init"].ap().unsqueeze(0)),
        writes=[lam_b], dma=True)
    s.op("sc", lambda h: h.activation(out=lam_b.ap()[0:1, :], in_=lam_b.ap()[0:1, :],
                                      func=AF.Sigmoid), reads=[lam_b], writes=[lam_b])
    s.op("sy", lambda h: h.dma_start(out=lamh.ap(), in_=lam_b.ap()[0:1, :]),
         reads=[lam_b], writes=[lamh], dma=True)
    s.op("sy", lambda h: h.dma_start(
        out=lam_b.ap(), in_=lamh.ap()[0:1, :].partition_broadcast(128)),
        reads=[lamh], writes=[lam_b], dma=True)
    s.op("sy", lambda h: h.dma_start(
        out=rmsw.ap(),
        in_=p.ext_in["L1_attn_rms_weight"].ap().unsqueeze(0).partition_broadcast(128)),
        writes=[rmsw], dma=True)
    s.op("sy", lambda h: h.dma_start(out=flg.ap(), in_=p.ext_in["a_flags"].ap()),
         writes=[flg], dma=True)

    mark_xn = p.mark()
    x2t = [p.sbuf(f"a_x2t{i}", [128, 1024], BF16) for i in range(2)]
    xnt = [p.sbuf(f"a_xnt{i}", [128, 1024], BF16) for i in range(2)]
    for i in range(NT):
        xb, xn = x2t[i % 2], xnt[i % 2]
        s.op("sy", lambda h, i=i, xb=xb: h.dma_start(
            out=xb.ap(), in_=x2.ap()[i * 128:(i + 1) * 128, :]), writes=[xb], dma=True)
        ln0_rows(p, xb, xb.ap(), st2, 0, scrap)
        s.op("dv", lambda h, xb=xb, xn=xn: h.tensor_scalar(
            out=xn.ap(), in0=xb.ap(), scalar1=st2.ap()[:, 0:1],
            scalar2=st2.ap()[:, 1:2], op0=ALU.subtract, op1=ALU.mult),
            reads=[xb, st2], writes=[xn])
        s.op("sy", lambda h, i=i, xn=xn: h.dma_start(
            out=xn2h.ap()[i * 128:(i + 1) * 128, :], in_=xn.ap()),
            reads=[xn], writes=[xn2h], dma=True)
    s.fence_all()
    p.free_to(mark_xn)

    A0, B0 = blocks_of(0)
    A1, B1 = blocks_of(1)
    ga = p.sbuf("a_ga", [128, 16 * 1024], BF16)
    gbv = p.sbuf("a_gb", [128, 16 * 1024], BF16)
    for (pats, dst) in (((A0, B0), ga), ((A1, B1), gbv)):
        s1l, s2l = pats
        s.op("sy", lambda h, s1l=s1l, dst=dst: h.dma_start(
            out=dst.ap().rearrange("p (m two d) -> p m two d", two=2, d=1024)[:, :, 0, :],
            in_=xn2h.ap().rearrange("(m g p) d -> p m g d", p=128, g=4)[:, :, s1l[0] % 4, :]),
            reads=[xn2h], writes=[dst], dma=True)
        s.op("sy", lambda h, s2l=s2l, dst=dst: h.dma_start(
            out=dst.ap().rearrange("p (m two d) -> p m two d", two=2, d=1024)[:, :, 1, :],
            in_=xn2h.ap().rearrange("(m g p) d -> p m g d", p=128, g=4)[:, :, s2l[0] % 4, :]),
            reads=[xn2h], writes=[dst], dma=True)
    xo = p.sbuf("a_xo", [128, 16 * 1024], BF16)
    s.op("dv", lambda h: h.tensor_scalar_mul(out=xo.ap(), in0=ga.ap(),
                                             scalar1=flg.ap()[:, 0:1]),
         reads=[ga, flg], writes=[xo])
    s.op("dv", lambda h: h.scalar_tensor_tensor(
        out=xo.ap(), in0=gbv.ap(), scalar=flg.ap()[:, 1:2], in1=xo.ap(),
        op0=ALU.mult, op1=ALU.add), reads=[gbv, flg, xo], writes=[xo])
    for i in range(NQT):
        s.op("sy", lambda h, i=i: h.dma_start(
            out=xo2h.ap()[i * 128:(i + 1) * 128, :],
            in_=xo.ap()[:, i * 1024:(i + 1) * 1024]),
            reads=[xo], writes=[xo2h], dma=True)
    s.fence_all()
    p.free_to(mark_xn)

    va = p.sbuf("a_va", [128, NKV * 32 * NHH], BF16)
    mark_pr = p.mark()
    xn2T = p.sbuf("a_xn2T", [128, 8 * 4096], BF16)
    xooT = p.sbuf("a_xooT", [128, 8 * 2048], BF16)
    wqT = p.sbuf("a_wqT", [128, 8 * 1024], BF16)
    wkvT = p.sbuf("a_wkvT", [128, 8 * 512], BF16)
    qcs = [p.sbuf(f"a_qcs{i}", [128, 1024], BF16) for i in range(2)]
    qsn = [p.sbuf(f"a_qsn{i}", [128, 1024], BF16) for i in range(2)]
    rr = [p.sbuf(f"a_rr{i}", [128, 1024], BF16) for i in range(4)]
    pq = p.psum("a_pq", [128, 1024])
    for c in range(8):
        s.op("sc", lambda h, c=c: h.dma_start_transpose(
            xn2T.ap()[:, c * 4096:(c + 1) * 4096], xn2h.ap()[:, c * 128:(c + 1) * 128]),
            reads=[xn2h], writes=[xn2T], dma=True)
        s.op("sc", lambda h, c=c: h.dma_start_transpose(
            xooT.ap()[:, c * 2048:(c + 1) * 2048], xo2h.ap()[:, c * 128:(c + 1) * 128]),
            reads=[xo2h], writes=[xooT], dma=True)
        s.op("sc", lambda h, c=c: h.dma_start_transpose(
            wqT.ap()[:, c * 1024:(c + 1) * 1024], wqb.ap()[:, c * 128:(c + 1) * 128]),
            reads=[wqb], writes=[wqT], dma=True)
        s.op("sc", lambda h, c=c: h.dma_start_transpose(
            wkvT.ap()[:, c * 512:(c + 1) * 512], wkvb.ap()[:, c * 128:(c + 1) * 128]),
            reads=[wkvb], writes=[wkvT], dma=True)
    for kh in range(NKV):
        for t in range(32):
            s.op("dv", lambda h, kh=kh, t=t: h.memset(
                va.ap()[:, (kh * 32 + t) * NHH + HD:(kh * 32 + t) * NHH + NHH], 1.0),
                writes=[va])

    def rope(src_psum, cosb, sinb, width, outb):
        r0, r1, r2 = rr[0], rr[1], rr[2]
        s.op("dv", lambda h: h.tensor_copy(out=r0.ap()[:, 0:width], in_=src_psum.ap()[:, 0:width]),
             reads=[src_psum], writes=[r0])
        s.op("dv", lambda h: h.tensor_copy(
            out=r1.ap()[:, 0:width].rearrange("p (a two) -> p a two", two=2)[:, :, 0],
            in_=r0.ap()[:, 0:width].rearrange("p (a two) -> p a two", two=2)[:, :, 1]),
            reads=[r0], writes=[r1])
        s.op("dv", lambda h: h.tensor_copy(
            out=r1.ap()[:, 0:width].rearrange("p (a two) -> p a two", two=2)[:, :, 1],
            in_=r0.ap()[:, 0:width].rearrange("p (a two) -> p a two", two=2)[:, :, 0]),
            reads=[r0], writes=[r1])
        s.op("dv", lambda h: h.tensor_tensor(out=r2.ap()[:, 0:width],
                                             in0=r0.ap()[:, 0:width],
                                             in1=cosb.ap()[:, 0:width], op=ALU.mult),
             reads=[r0, cosb], writes=[r2])
        s.op("dv", lambda h: h.tensor_tensor(out=r1.ap()[:, 0:width],
                                             in0=r1.ap()[:, 0:width],
                                             in1=sinb.ap()[:, 0:width], op=ALU.mult),
             reads=[r1, sinb], writes=[r1])
        s.op("dv", lambda h: h.tensor_tensor(out=outb.ap()[:, 0:width],
                                             in0=r2.ap()[:, 0:width],
                                             in1=r1.ap()[:, 0:width], op=ALU.add),
             reads=[r2, r1], writes=[outb])

    for i in range(NQT):
        def qgrp(h, i=i):
            for dh in range(2):
                for d in range(8):
                    ins = h.matmul(
                        pq.ap()[:, dh * 512:(dh + 1) * 512],
                        xooT.ap()[:, d * 2048 + i * 128:d * 2048 + (i + 1) * 128],
                        wqT.ap()[:, d * 1024 + dh * 512:d * 1024 + (dh + 1) * 512],
                        start=(d == 0), stop=(d == 7))
            return ins
        s.op("te", qgrp, reads=[xooT, wqT], writes=[pq])
        cq, sq = qcs[i % 2], qsn[i % 2]
        s.op("sy", lambda h, i=i, cq=cq: h.dma_start(
            out=cq.ap(), in_=p.ext_in["a_qcos"].ap()[i * 128:(i + 1) * 128, :]),
            writes=[cq], dma=True)
        s.op("sy", lambda h, i=i, sq=sq: h.dma_start(
            out=sq.ap(), in_=p.ext_in["a_qsin"].ap()[i * 128:(i + 1) * 128, :]),
            writes=[sq], dma=True)
        rope(pq, cq, sq, 1024, rr[3])
        s.op("sy", lambda h, i=i: h.dma_start(
            out=qph.ap()[i * 128:(i + 1) * 128, :], in_=rr[3].ap()),
            reads=[rr[3]], writes=[qph], dma=True)
    for i in range(NT):
        def kvgrp(h, i=i):
            for d in range(8):
                ins = h.matmul(
                    pq.ap()[:, 0:512],
                    xn2T.ap()[:, d * 4096 + i * 128:d * 4096 + (i + 1) * 128],
                    wkvT.ap()[:, d * 512:(d + 1) * 512],
                    start=(d == 0), stop=(d == 7))
            return ins
        s.op("te", kvgrp, reads=[xn2T, wkvT], writes=[pq])
        cq, sq = qcs[i % 2], qsn[i % 2]
        s.op("sy", lambda h, i=i, cq=cq: h.dma_start(
            out=cq.ap()[:, 0:256], in_=p.ext_in["a_kcos"].ap()[i * 128:(i + 1) * 128, :]),
            writes=[cq], dma=True)
        s.op("sy", lambda h, i=i, sq=sq: h.dma_start(
            out=sq.ap()[:, 0:256], in_=p.ext_in["a_ksin"].ap()[i * 128:(i + 1) * 128, :]),
            writes=[sq], dma=True)
        rope(pq, cq, sq, 256, rr[3])
        s.op("sy", lambda h, i=i: h.dma_start(
            out=kph.ap()[i * 128:(i + 1) * 128, :], in_=rr[3].ap()[:, 0:256]),
            reads=[rr[3]], writes=[kph], dma=True)
        for kh in range(NKV):
            s.op("dv", lambda h, i=i, kh=kh: h.tensor_copy(
                out=va.ap()[:, (kh * 32 + i) * NHH:(kh * 32 + i) * NHH + HD],
                in_=pq.ap()[:, 256 + kh * HD:256 + (kh + 1) * HD]),
                reads=[pq], writes=[va])
    s.fence_all()
    p.free_to(mark_pr)

    mark_sc = p.mark()
    va2 = p.sbuf("a_va2", [128, NKV * 32 * NHH], BF16)
    p.free_to(mark_sc)

    qT = p.sbuf("a_qT", [128, NH * 2048], BF16)
    kT = p.sbuf("a_kT", [128, NKV * 4096], BF16)
    msk = p.sbuf("a_msk", [128, 32 * 512], BF16)
    tmpT = p.sbuf("a_tmpT", [128, 4096], BF16)
    e_sb = p.sbuf("a_esb", [128, 2048], BF16)
    yev = p.sbuf("a_yev", [65, 512], BF16)
    psT = p.psum("a_psT", [128, 2048])
    py1 = p.psum("a_py1", [65, 512])
    py2 = p.psum("a_py2", [65, 512])

    s.op("sy", lambda h: h.dma_start(
        out=msk.ap().rearrange("p (g w q) -> p g w q", g=4, w=8),
        in_=p.ext_in["a_mask"].ap().rearrange("g w p q -> p g w q")),
        writes=[msk], dma=True)

    for c in range(8):
        s.op("sc", lambda h, c=c: h.dma_start_transpose(
            tmpT.ap()[:, 0:2048], qph.ap()[:, c * 128:(c + 1) * 128]),
            reads=[qph], writes=[tmpT], dma=True)
        for j in range(4):
            hh = c * 2 + j // 2
            half = j % 2
            s.op("gp", lambda h, hh=hh, half=half, j=j: h.dma_start(
                out=qT.ap()[half * 64:half * 64 + 32, hh * 2048:(hh + 1) * 2048],
                in_=tmpT.ap()[j * 32:(j + 1) * 32, 0:2048]),
                reads=[tmpT], writes=[qT], dma=True)
    for c in range(2):
        s.op("sc", lambda h, c=c: h.dma_start_transpose(
            tmpT.ap()[:, 0:4096], kph.ap()[:, c * 128:(c + 1) * 128]),
            reads=[kph], writes=[tmpT], dma=True)
        for j in range(4):
            kh = c * 2 + j // 2
            half = j % 2
            s.op("gp", lambda h, c=c, j=j, kh=kh, half=half: h.dma_start(
                out=kT.ap()[half * 64:half * 64 + 32, kh * 4096:(kh + 1) * 4096],
                in_=tmpT.ap()[j * 32:(j + 1) * 32, 0:4096]),
                reads=[tmpT], writes=[kT], dma=True)

    SCL = 1.0 / (HF ** 0.5)
    for kh in range(NKV):
        for hq in range(4):
            hh = kh * 4 + hq
            for gq in range(4):
                E = 8 * gq + 8
                for sm in range(2):
                    base = sm * 64
                    pyy_ = py1 if sm == 0 else py2
                    for tg in range((E + 3) // 4):
                        nt = min(4, E - tg * 4)

                        def sgrp(h, kh=kh, hh=hh, gq=gq, tg=tg, nt=nt, base=base):
                            for tt_ in range(nt):
                                t = tg * 4 + tt_
                                ins = h.matmul(
                                    psT.ap()[:, tt_ * 512:(tt_ + 1) * 512],
                                    kT.ap()[base:base + 32,
                                            kh * 4096 + t * 128:kh * 4096 + (t + 1) * 128],
                                    qT.ap()[base:base + 32,
                                            hh * 2048 + gq * 512:hh * 2048 + (gq + 1) * 512],
                                    start=True, stop=True)
                            return ins
                        s.op("te", sgrp, reads=[kT, qT], writes=[psT])
                        s.op("sc", lambda h, nt=nt: h.activation(
                            out=e_sb.ap()[:, 0:nt * 512], in_=psT.ap()[:, 0:nt * 512],
                            func=AF.Exp, scale=SCL), reads=[psT], writes=[e_sb])
                        for tt_ in range(nt):
                            t = tg * 4 + tt_
                            if t >= E - 8:
                                w = t - (E - 8)
                                s.op("dv", lambda h, gq=gq, w=w, tt_=tt_: h.tensor_tensor(
                                    out=e_sb.ap()[:, tt_ * 512:(tt_ + 1) * 512],
                                    in0=e_sb.ap()[:, tt_ * 512:(tt_ + 1) * 512],
                                    in1=msk.ap()[:, (gq * 8 + w) * 512:(gq * 8 + w + 1) * 512],
                                    op=ALU.mult), reads=[e_sb, msk], writes=[e_sb])
                        def pvgrp(h, kh=kh, tg=tg, nt=nt, pyy_=pyy_, E=E):
                            for tt_ in range(nt):
                                t = tg * 4 + tt_
                                ins = h.matmul(
                                    pyy_.ap(),
                                    va.ap()[:, (kh * 32 + t) * NHH:(kh * 32 + t + 1) * NHH],
                                    e_sb.ap()[:, tt_ * 512:(tt_ + 1) * 512],
                                    start=(t == 0), stop=(t == E - 1))
                            return ins
                        s.op("te", pvgrp, reads=[va, e_sb], writes=[pyy_])
                    s.op("dv", lambda h, pyy_=pyy_: h.tensor_copy(
                        out=yev.ap(), in_=pyy_.ap()), reads=[pyy_], writes=[yev])
                    dsth = y1h if sm == 0 else y2h
                    s.op("sy", lambda h, hh=hh, gq=gq, dsth=dsth: h.dma_start(
                        out=dsth.ap()[hh * NHH:(hh + 1) * NHH, gq * 512:(gq + 1) * 512],
                        in_=yev.ap()), reads=[yev], writes=[dsth], dma=True)
    s.fence_all()
    p.free_to(mark_sc)

    mark_a7 = p.mark()
    woT = p.sbuf("a_woT", [128, 8 * 1024], BF16)
    y1t = p.sbuf("a_y1t", [128, NH * NHH], BF16)
    y2t = p.sbuf("a_y2t", [128, NH * NHH], BF16)
    cbuf = p.sbuf("a_cbuf", [128, 8], F32)
    dbuf = p.sbuf("a_dbuf", [128, HD], F32)
    tbuf = p.sbuf("a_tbuf", [128, HD], F32)
    ynb = p.sbuf("a_ynb", [128, DM], BF16)
    ynT = p.sbuf("a_ynT", [128, 8 * 2048], BF16)
    x2a = p.sbuf("a_x2a", [128, 1024], BF16)
    x2b = p.sbuf("a_x2b", [128, 1024], BF16)
    sfb = p.sbuf("a_sfb", [128, 1024], F32)
    x3t = p.sbuf("a_x3t", [128, 1024], BF16)
    pat = p.psum("a_pat", [128, 1024])
    st2b = p.sbuf("a_st2b", [128, 2], F32)

    for c in range(8):
        s.op("sc", lambda h, c=c: h.dma_start_transpose(
            woT.ap()[:, c * 1024:(c + 1) * 1024], wob.ap()[:, c * 128:(c + 1) * 128]),
            reads=[wob], writes=[woT], dma=True)

    A0l, B0l = blocks_of(0)
    A1l, B1l = blocks_of(1)
    own0 = sorted(A0l + B0l)
    own1 = sorted(A1l + B1l)
    for i in range(NQT):
        s.op("sc", lambda h, i=i: h.dma_start_transpose(
            y1t.ap(), y1h.ap()[:, i * 128:(i + 1) * 128]),
            reads=[y1h], writes=[y1t], dma=True)
        s.op("sc", lambda h, i=i: h.dma_start_transpose(
            y2t.ap(), y2h.ap()[:, i * 128:(i + 1) * 128]),
            reads=[y2h], writes=[y2t], dma=True)
        for hh in range(NH):
            o = hh * NHH
            s.op("dv", lambda h, o=o: h.reciprocal(
                out=cbuf.ap()[:, 0:1], in_=y1t.ap()[:, o + HD:o + HD + 1]),
                reads=[y1t], writes=[cbuf])
            s.op("dv", lambda h, o=o: h.reciprocal(
                out=cbuf.ap()[:, 1:2], in_=y2t.ap()[:, o + HD:o + HD + 1]),
                reads=[y2t], writes=[cbuf])
            s.op("dv", lambda h, hh=hh: h.tensor_tensor(
                out=cbuf.ap()[:, 2:3], in0=cbuf.ap()[:, 1:2],
                in1=lam_b.ap()[:, hh:hh + 1], op=ALU.mult),
                reads=[cbuf, lam_b], writes=[cbuf])
            s.op("dv", lambda h, o=o: h.tensor_scalar_mul(
                out=tbuf.ap(), in0=y2t.ap()[:, o:o + HD],
                scalar1=cbuf.ap()[:, 2:3]), reads=[y2t, cbuf], writes=[tbuf])
            s.op("dv", lambda h, o=o: h.scalar_tensor_tensor(
                out=dbuf.ap(), in0=y1t.ap()[:, o:o + HD],
                scalar=cbuf.ap()[:, 0:1], in1=tbuf.ap(),
                op0=ALU.mult, op1=ALU.subtract),
                reads=[y1t, cbuf, tbuf], writes=[dbuf])
            s.op("sc", lambda h: h.activation(
                out=tbuf.ap(), in_=dbuf.ap(), func=AF.Square,
                accum_out=cbuf.ap()[:, 3:4]), reads=[dbuf], writes=[cbuf, tbuf])
            s.op("sc", lambda h: h.activation(
                out=cbuf.ap()[:, 4:5], in_=cbuf.ap()[:, 3:4], func=AF.Sqrt,
                scale=1.0 / HD, bias=EPS6), reads=[cbuf], writes=[cbuf])
            s.op("dv", lambda h: h.reciprocal(out=cbuf.ap()[:, 5:6],
                                              in_=cbuf.ap()[:, 4:5]),
                 reads=[cbuf], writes=[cbuf])
            s.op("dv", lambda h: h.tensor_scalar_mul(
                out=dbuf.ap(), in0=dbuf.ap(), scalar1=cbuf.ap()[:, 5:6]),
                reads=[dbuf, cbuf], writes=[dbuf])
            s.op("dv", lambda h, hh=hh: h.tensor_tensor(
                out=ynb.ap()[:, hh * HD:(hh + 1) * HD], in0=dbuf.ap(),
                in1=rmsw.ap(), op=ALU.mult), reads=[dbuf, rmsw], writes=[ynb])
        s.op("sy", lambda h, i=i: h.dma_start(
            out=ynh.ap()[i * 128:(i + 1) * 128, :], in_=ynb.ap()),
            reads=[ynb], writes=[ynh], dma=True)
    for c in range(8):
        s.op("sc", lambda h, c=c: h.dma_start_transpose(
            ynT.ap()[:, c * 2048:(c + 1) * 2048], ynh.ap()[:, c * 128:(c + 1) * 128]),
            reads=[ynh], writes=[ynT], dma=True)
    for i in range(NQT):
        def ogrp(h, i=i):
            for dh in range(2):
                for d in range(8):
                    ins = h.matmul(
                        pat.ap()[:, dh * 512:(dh + 1) * 512],
                        ynT.ap()[:, d * 2048 + i * 128:d * 2048 + (i + 1) * 128],
                        woT.ap()[:, d * 1024 + dh * 512:d * 1024 + (dh + 1) * 512],
                        start=(d == 0), stop=(d == 7))
            return ins
        s.op("te", ogrp, reads=[ynT, woT], writes=[pat])
        blkA = own0[i]
        blkB = own1[i]
        s.op("sy", lambda h, blkA=blkA: h.dma_start(
            out=x2a.ap(), in_=x2.ap()[blkA * 128:(blkA + 1) * 128, :]),
            writes=[x2a], dma=True)
        s.op("sy", lambda h, blkB=blkB: h.dma_start(
            out=x2b.ap(), in_=x2.ap()[blkB * 128:(blkB + 1) * 128, :]),
            writes=[x2b], dma=True)
        s.op("dv", lambda h: h.tensor_scalar_mul(
            out=x2a.ap(), in0=x2a.ap(), scalar1=flg.ap()[:, 0:1]),
            reads=[x2a, flg], writes=[x2a])
        s.op("dv", lambda h: h.scalar_tensor_tensor(
            out=x2a.ap(), in0=x2b.ap(), scalar=flg.ap()[:, 1:2], in1=x2a.ap(),
            op0=ALU.mult, op1=ALU.add), reads=[x2b, flg, x2a], writes=[x2a])
        ln0_rows(p, pat, pat.ap(), st2b, 0, scrap)
        s.op("dv", lambda h: h.tensor_scalar(
            out=sfb.ap(), in0=pat.ap(), scalar1=st2b.ap()[:, 0:1],
            scalar2=st2b.ap()[:, 1:2], op0=ALU.subtract, op1=ALU.mult),
            reads=[pat, st2b], writes=[sfb])
        s.op("dv", lambda h: h.tensor_tensor(
            out=x3t.ap(), in0=sfb.ap(), in1=x2a.ap(), op=ALU.add),
            reads=[sfb, x2a], writes=[x3t])
        s.op("sy", lambda h, i=i: h.dma_start(
            out=x3_hbm.ap()[i * 128:(i + 1) * 128, :], in_=x3t.ap()),
            reads=[x3t], writes=[x3_hbm], dma=True)
    s.fence_all()
    p.free_to(mark_phase)


EPS6 = 1e-6


def build_prog_a():
    p = Prog("a")
    make_consts(p)
    x2 = p.inp("x2", [L, DM], BF16)
    for nm, shp in (("L1_attn_Wq", [DM, DM]), ("L1_attn_Wkv", [512, DM]),
                    ("L1_attn_Wout", [DM, DM]), ("L1_attn_lambda_init", [NH]),
                    ("L1_attn_rms_weight", [HD])):
        p.inp(nm, shp, F32)
    p.inp("a_qcos", [QOWN, DM], BF16)
    p.inp("a_qsin", [QOWN, DM], BF16)
    p.inp("a_kcos", [L, 256], BF16)
    p.inp("a_ksin", [L, 256], BF16)
    p.inp("a_mask", [4, 8, 128, 512], BF16)
    p.inp("a_flags", [128, 2], F32)
    x3 = p.out("x3", [QOWN, DM], BF16)
    emit_attn(p, x2, x3)
    return p


def host_attn_tables(parity):
    A, Bl = blocks_of(0) if parity == 0 else blocks_of(1)
    own = sorted(A + Bl)
    own_rows = np.concatenate([np.arange(b * 128, (b + 1) * 128) for b in own])
    inv = (1.0 / 10000.0 ** (np.arange(0, HF, 2, dtype=np.float32) / HF)).astype(np.float32)
    d = np.arange(DM)
    f = (d % 32) // 2
    par = d % 2
    ang = own_rows[:, None].astype(np.float32) * inv[None, f]
    qcos = np.cos(ang).astype(BF16NP)
    sg = np.where(par[None, :] == 0, -1.0, 1.0).astype(np.float32)
    qsin = (np.sin(ang) * sg).astype(BF16NP)
    dk = np.arange(256)
    fk = (dk % 32) // 2
    park = dk % 2
    tok = np.arange(L)
    angk = tok[:, None].astype(np.float32) * inv[None, fk]
    kcos = np.cos(angk).astype(BF16NP)
    sgk = np.where(park[None, :] == 0, -1.0, 1.0).astype(np.float32)
    ksin = (np.sin(angk) * sgk).astype(BF16NP)
    mask = np.zeros((4, 8, 128, 512), BF16NP)
    for gq in range(4):
        E = 8 * gq + 8
        for w in range(8):
            t = E - 8 + w
            kglob = t * 128 + np.arange(128)
            qglob = own_rows[gq * 512:(gq + 1) * 512]
            mask[gq, w] = (kglob[:, None] <= qglob[None, :]).astype(BF16NP)
    flags = np.zeros((128, 2), np.float32)
    flags[:, parity] = 1.0
    return {"a_qcos": qcos, "a_qsin": qsin, "a_kcos": kcos, "a_ksin": ksin,
            "a_mask": mask, "a_flags": flags}, own_rows





def emit_moe(p, x3, out_own):
    nc, s = p.nc, p.s
    mark_phase = p.mark()
    NQT = QOWN // 128
    ident, onesb = p.ident, p.onesb

    featsh = p.scratch("e_featsh", [QOWN, DM], BF16)
    featch = p.scratch("e_featch", [QOWN, DM], BF16)
    w1bh = p.scratch("e_w1bh", [NEXP * DFF, DM], BF16)
    w2bh = p.scratch("e_w2bh", [NEXP * DM, DFF], BF16)

    emask = p.sbuf("e_emask", [128, NQT * NEXP], F32)
    alph = p.sbuf("e_alph", [128, NQT], F32)
    mark_e1 = p.mark()
    st2 = p.sbuf("e_st2", [128, 2], F32)
    scrap = p.sbuf("e_scrap", [128, 1024], F32)
    x3t = [p.sbuf(f"e_x3t{i}", [128, 1024], BF16) for i in range(2)]
    ft32 = [p.sbuf(f"e_ft32{i}", [128, 1024], F32) for i in range(2)]
    fbf = p.sbuf("e_fbf", [128, 1024], BF16)
    cent = p.sbuf("e_cent", [16, 1024], F32)
    centT = p.sbuf("e_centT", [128, 8 * 16], F32)
    fTf = p.sbuf("e_fTf", [128, 8 * 128], F32)
    scoT = p.sbuf("e_scoT", [16, QOWN], F32)
    sco = p.sbuf("e_sco", [128, NQT * 8], F32)
    smax = p.sbuf("e_smax", [128, 8], F32)
    sidx = p.sbuf("e_sidx", [128, 8], mybir.dt.uint32)
    sidxf = p.sbuf("e_sidxf", [128, 1], F32)
    pt = p.psum("e_pt", [128, 128])
    psc = p.psum("e_psc", [16, 512])

    s.op("sy", lambda h: h.dma_start(out=cent.ap(),
                                     in_=p.ext_in["cent_pad"].ap()),
         writes=[cent], dma=True)
    for c in range(8):
        s.op("te", lambda h, c=c: h.transpose(
            pt.ap()[:, 0:16], cent.ap()[:, c * 128:(c + 1) * 128],
            p.identf.ap()[0:16, 0:16]),
            reads=[cent, p.identf], writes=[pt])
        s.op("dv", lambda h, c=c: h.tensor_copy(
            out=centT.ap()[:, c * 16:(c + 1) * 16], in_=pt.ap()[:, 0:16]),
            reads=[pt], writes=[centT])

    for i in range(NQT):
        xb, fb = x3t[i % 2], ft32[i % 2]
        s.op("sy", lambda h, i=i, xb=xb: h.dma_start(
            out=xb.ap(), in_=x3.ap()[i * 128:(i + 1) * 128, :]), writes=[xb], dma=True)
        ln0_rows(p, xb, xb.ap(), st2, 0, scrap)
        s.op("dv", lambda h, xb=xb, fb=fb: h.tensor_scalar(
            out=fb.ap(), in0=xb.ap(), scalar1=st2.ap()[:, 0:1],
            scalar2=st2.ap()[:, 1:2], op0=ALU.subtract, op1=ALU.mult),
            reads=[xb, st2], writes=[fb])
        s.op("dv", lambda h, fb=fb: h.tensor_copy(out=fbf.ap(), in_=fb.ap()),
             reads=[fb], writes=[fbf])
        s.op("sy", lambda h, i=i: h.dma_start(
            out=featsh.ap()[i * 128:(i + 1) * 128, :], in_=fbf.ap()),
            reads=[fbf], writes=[featsh], dma=True)
        ln0_rows(p, fb, fb.ap(), st2, 0, scrap)
        s.op("dv", lambda h, fb=fb: h.tensor_scalar(
            out=scrap.ap(), in0=fb.ap(), scalar1=st2.ap()[:, 0:1],
            scalar2=st2.ap()[:, 1:2], op0=ALU.subtract, op1=ALU.mult),
            reads=[fb, st2], writes=[scrap])
        s.op("dv", lambda h: h.tensor_copy(out=fbf.ap(), in_=scrap.ap()),
             reads=[scrap], writes=[fbf])
        s.op("sy", lambda h, i=i: h.dma_start(
            out=featch.ap()[i * 128:(i + 1) * 128, :], in_=fbf.ap()),
            reads=[fbf], writes=[featch], dma=True)
        for c in range(8):
            s.op("te", lambda h, c=c, fb=fb: h.transpose(
                pt.ap(), fb.ap()[:, c * 128:(c + 1) * 128], p.identf.ap()),
                reads=[fb, p.identf], writes=[pt])
            s.op("dv", lambda h, c=c: h.tensor_copy(
                out=fTf.ap()[:, c * 128:(c + 1) * 128], in_=pt.ap()),
                reads=[pt], writes=[fTf])
        for c in range(8):
            s.op("te", lambda h, c=c: h.matmul(
                psc.ap()[:, 0:128], centT.ap()[:, c * 16:(c + 1) * 16],
                fTf.ap()[:, c * 128:(c + 1) * 128],
                start=(c == 0), stop=(c == 7)), reads=[centT, fTf], writes=[psc])
        s.op("dv", lambda h, i=i: h.tensor_copy(
            out=scoT.ap()[:, i * 128:(i + 1) * 128], in_=psc.ap()[:, 0:128]),
            reads=[psc], writes=[scoT])
    for i in range(NQT):
        s.op("te", lambda h, i=i: h.transpose(
            pt.ap()[:, 0:16], scoT.ap()[:, i * 128:(i + 1) * 128],
            p.identf.ap()[0:16, 0:16]),
            reads=[scoT, p.identf], writes=[pt])
        s.op("dv", lambda h, i=i: h.tensor_copy(
            out=sco.ap()[:, i * 8:(i + 1) * 8], in_=pt.ap()[:, 0:8]),
            reads=[pt], writes=[sco])
        s.op("dv", lambda h, i=i: h.max(
            out=smax.ap(), in_=sco.ap()[:, i * 8:(i + 1) * 8]),
            reads=[sco], writes=[smax])
        s.op("dv", lambda h, i=i: h.max_index(
            out=sidx.ap(), in_max=smax.ap(), in_values=sco.ap()[:, i * 8:(i + 1) * 8]),
            reads=[smax, sco], writes=[sidx])
        s.op("sc", lambda h, i=i: h.activation(
            out=alph.ap()[:, i:i + 1], in_=smax.ap()[:, 0:1], func=AF.Sigmoid),
            reads=[smax], writes=[alph])
        s.op("dv", lambda h: h.tensor_copy(out=sidxf.ap(), in_=sidx.ap()[:, 0:1]),
             reads=[sidx], writes=[sidxf])
        for e in range(NEXP):
            s.op("dv", lambda h, i=i, e=e: h.tensor_scalar(
                out=emask.ap()[:, i * NEXP + e:i * NEXP + e + 1],
                in0=sidxf.ap(), scalar1=float(e), scalar2=None, op0=ALU.is_equal),
                reads=[sidxf], writes=[emask])
    s.fence_all()
    p.free_to(mark_e1)

    mark_cast = p.mark()
    stg = p.sbuf("e_stg", [128, 4096], BF16)
    cast_via_hbm(p, p.ext_in["L1_ffn_W1"].ap().rearrange("e f d -> (e f) d"),
                 w1bh, NEXP * DFF, DM, stg, "w1")
    cast_via_hbm(p, p.ext_in["L1_ffn_W2"].ap().rearrange("e d f -> (e d) f"),
                 w2bh, NEXP * DM, DFF, stg, "w2")
    s.fence_all()
    p.free_to(mark_cast)

    mark_e2 = p.mark()
    gcolv = p.sbuf("e_gcol", [128, 8 * NEXP], F32)
    bcolv = p.sbuf("e_bcol", [128, 8 * NEXP], F32)
    b1c = p.sbuf("e_b1c", [128, 32 * NEXP], F32)
    b2r = p.sbuf("e_b2r", [1, 1024], BF16)
    yacch = p.scratch("e_yacch", [QOWN, DM], BF16)
    yat = [p.sbuf(f"e_yat{i}", [128, 1024], BF16) for i in range(2)]
    mark_wts = p.mark()
    fcg = [p.sbuf(f"e_fcg{i}", [128, 512], BF16) for i in range(2)]
    w1T = p.sbuf("e_w1T", [128, 8 * 4096], BF16)
    w2T = p.sbuf("e_w2T", [128, 32 * 1024], BF16)
    subT = [p.sbuf(f"e_subT{i}", [128, 8 * 512], BF16) for i in range(2)]
    h_sb = p.sbuf("e_h", [128, 32 * 512], BF16)
    ph = [p.psum(f"e_ph{i}", [128, 512]) for i in range(2)]
    py = [p.psum(f"e_py{i}", [128, 1024]) for i in range(2)]

    for e in range(NEXP):
        for c in range(8):
            s.op("sy", lambda h, e=e, c=c: h.dma_start(
                out=gcolv.ap()[:, c * NEXP + e:c * NEXP + e + 1],
                in_=p.ext_in["L1_ffn_gamma"].ap()[e, c * 128:(c + 1) * 128].unsqueeze(1)),
                writes=[gcolv], dma=True)
            s.op("sy", lambda h, e=e, c=c: h.dma_start(
                out=bcolv.ap()[:, c * NEXP + e:c * NEXP + e + 1],
                in_=p.ext_in["L1_ffn_beta"].ap()[e, c * 128:(c + 1) * 128].unsqueeze(1)),
                writes=[bcolv], dma=True)
        for c in range(32):
            s.op("sy", lambda h, e=e, c=c: h.dma_start(
                out=b1c.ap()[:, c * NEXP + e:c * NEXP + e + 1],
                in_=p.ext_in["L1_ffn_b1"].ap()[e, c * 128:(c + 1) * 128].unsqueeze(1)),
                writes=[b1c], dma=True)

    for e in range(NEXP):
        for c in range(8):
            s.op("sc", lambda h, e=e, c=c: h.dma_start_transpose(
                w1T.ap()[:, c * 4096:(c + 1) * 4096],
                w1bh.ap()[e * DFF:(e + 1) * DFF, c * 128:(c + 1) * 128]),
                reads=[w1bh], writes=[w1T], dma=True)
        for c in range(32):
            s.op("sc", lambda h, e=e, c=c: h.dma_start_transpose(
                w2T.ap()[:, c * 1024:(c + 1) * 1024],
                w2bh.ap()[e * DM:(e + 1) * DM, c * 128:(c + 1) * 128]),
                reads=[w2bh], writes=[w2T], dma=True)
        s.op("gp", lambda h, e=e: h.dma_start(
            out=b2r.ap(), in_=p.ext_in["L1_ffn_b2"].ap()[e].unsqueeze(0)),
            writes=[b2r], dma=True)
        def emit_sub(g, sT):
            for d in range(8):
                fg = fcg[d % 2]
                s.op("sc", lambda h, g=g, d=d, fg=fg: h.dma_start_transpose(
                    fg.ap(), featch.ap()[g * 512:(g + 1) * 512, d * 128:(d + 1) * 128]),
                    reads=[featch], writes=[fg], dma=True)
                s.op("dv", lambda h, e=e, d=d, fg=fg, sT=sT: h.tensor_scalar(
                    out=sT.ap()[:, d * 512:(d + 1) * 512],
                    in0=fg.ap(),
                    scalar1=gcolv.ap()[:, d * NEXP + e:d * NEXP + e + 1],
                    scalar2=bcolv.ap()[:, d * NEXP + e:d * NEXP + e + 1],
                    op0=ALU.mult, op1=ALU.add),
                    reads=[fg, gcolv, bcolv], writes=[sT])

        emit_sub(0, subT[0])
        for g in range(4):
            sT = subT[g % 2]
            for fc in range(32):
                phh = ph[fc % 2]

                def w1grp(h, fc=fc, phh=phh, sT=sT):
                    for d in range(8):
                        ins = h.matmul(
                            phh.ap(),
                            w1T.ap()[:, d * 4096 + fc * 128:d * 4096 + (fc + 1) * 128],
                            sT.ap()[:, d * 512:(d + 1) * 512],
                            start=(d == 0), stop=(d == 7))
                    return ins
                s.op("te", w1grp, reads=[w1T, sT], writes=[phh])
                s.op("sc", lambda h, e=e, fc=fc, phh=phh: h.activation(
                    out=h_sb.ap()[:, fc * 512:(fc + 1) * 512], in_=phh.ap(),
                    func=AF.Relu,
                    bias=b1c.ap()[:, fc * NEXP + e:fc * NEXP + e + 1]),
                    reads=[phh, b1c], writes=[h_sb])
            if g < 3:
                emit_sub(g + 1, subT[(g + 1) % 2])
            for ts in range(4):
                i = g * 4 + ts
                pyg = py[ts % 2]

                def w2grp(h, ts=ts, pyg=pyg):
                    for fc in range(32):
                        for dh in range(2):
                            ins = h.matmul(
                                pyg.ap()[:, dh * 512:(dh + 1) * 512],
                                h_sb.ap()[:, fc * 512 + ts * 128:fc * 512 + (ts + 1) * 128],
                                w2T.ap()[:, fc * 1024 + dh * 512:fc * 1024 + (dh + 1) * 512],
                                start=(fc == 0), stop=False)
                    for dh in range(2):
                        ins = h.matmul(
                            pyg.ap()[:, dh * 512:(dh + 1) * 512],
                            onesb.ap()[0:1, 0:128],
                            b2r.ap()[0:1, dh * 512:(dh + 1) * 512],
                            start=False, stop=(dh == 1))
                    return ins
                s.op("te", w2grp, reads=[h_sb, w2T, onesb, b2r], writes=[pyg])
                ya = yat[ts % 2]
                if e == 0:
                    s.op("dv", lambda h, i=i, e=e, pyg=pyg, ya=ya: h.tensor_scalar_mul(
                        out=ya.ap(), in0=pyg.ap(),
                        scalar1=emask.ap()[:, i * NEXP + e:i * NEXP + e + 1]),
                        reads=[pyg, emask], writes=[ya])
                else:
                    s.op("sy", lambda h, i=i, ya=ya: h.dma_start(
                        out=ya.ap(), in_=yacch.ap()[i * 128:(i + 1) * 128, :]),
                        reads=[yacch], writes=[ya], dma=True)
                    s.op("dv", lambda h, i=i, e=e, pyg=pyg, ya=ya: h.scalar_tensor_tensor(
                        out=ya.ap(), in0=pyg.ap(),
                        scalar=emask.ap()[:, i * NEXP + e:i * NEXP + e + 1],
                        in1=ya.ap(), op0=ALU.mult, op1=ALU.add),
                        reads=[pyg, emask, ya], writes=[ya])
                s.op("sy", lambda h, i=i, ya=ya: h.dma_start(
                    out=yacch.ap()[i * 128:(i + 1) * 128, :], in_=ya.ap()),
                    reads=[ya], writes=[yacch], dma=True)

    s.fence_all()
    p.free_to(mark_wts)
    ftl = p.sbuf("e_ftl", [128, 1024], BF16)
    x3l = p.sbuf("e_x3l", [128, 1024], BF16)
    omf = p.sbuf("e_omf", [128, 1024], F32)
    st2c = p.sbuf("e_st2c", [128, 2], F32)
    scrapc = p.sbuf("e_scrapc", [128, 1024], F32)
    outb = p.sbuf("e_outb", [128, 1024], F32)
    for i in range(NQT):
        s.op("sy", lambda h, i=i: h.dma_start(
            out=ftl.ap(), in_=featsh.ap()[i * 128:(i + 1) * 128, :]),
            writes=[ftl], dma=True)
        s.op("sy", lambda h, i=i: h.dma_start(
            out=x3l.ap(), in_=x3.ap()[i * 128:(i + 1) * 128, :]),
            writes=[x3l], dma=True)
        ya = yat[i % 2]
        s.op("sy", lambda h, i=i, ya=ya: h.dma_start(
            out=ya.ap(), in_=yacch.ap()[i * 128:(i + 1) * 128, :]),
            reads=[yacch], writes=[ya], dma=True)
        s.op("dv", lambda h, i=i, ya=ya: h.scalar_tensor_tensor(
            out=omf.ap(), in0=ya.ap(),
            scalar=alph.ap()[:, i:i + 1], in1=ftl.ap(),
            op0=ALU.mult, op1=ALU.add), reads=[ya, alph, ftl], writes=[omf])
        ln0_rows(p, omf, omf.ap(), st2c, 0, scrapc)
        s.op("dv", lambda h: h.tensor_scalar(
            out=omf.ap(), in0=omf.ap(), scalar1=st2c.ap()[:, 0:1],
            scalar2=st2c.ap()[:, 1:2], op0=ALU.subtract, op1=ALU.mult),
            reads=[omf, st2c], writes=[omf])
        s.op("dv", lambda h: h.tensor_tensor(
            out=outb.ap(), in0=omf.ap(), in1=x3l.ap(), op=ALU.add),
            reads=[omf, x3l], writes=[outb])
        s.op("sy", lambda h, i=i: h.dma_start(
            out=out_own.ap()[i * 128:(i + 1) * 128, :], in_=outb.ap()),
            reads=[outb], writes=[out_own], dma=True)
    s.fence_all()
    p.free_to(mark_phase)


def build_prog_e():
    p = Prog("e")
    make_consts(p)
    x3 = p.inp("x3", [QOWN, DM], BF16)
    p.inp("cent_pad", [16, DM], F32)
    for nm, shp in (("L1_ffn_gamma", [NEXP, DM]), ("L1_ffn_beta", [NEXP, DM]),
                    ("L1_ffn_W1", [NEXP, DFF, DM]), ("L1_ffn_b1", [NEXP, DFF]),
                    ("L1_ffn_W2", [NEXP, DM, DFF]), ("L1_ffn_b2", [NEXP, DM])):
        p.inp(nm, shp, F32)
    out = p.out("out_own", [QOWN, DM], F32)
    emit_moe(p, x3, out)
    return p




def build_prog_all():
    p = Prog("all")
    make_consts(p)
    x = p.inp("x", [L, DM], F32)
    for nm, shp in (("L0_attn_ln_gamma", [DM]), ("L0_attn_ln_beta", [DM]),
                    ("L0_attn_in_proj_w", [2 * DI, DM]),
                    ("L0_attn_conv_w", [DI, 4]), ("L0_attn_conv_b", [DI]),
                    ("xpw_pad", [128, DI]),
                    ("dtp_pad", [DI, 128]), ("L0_attn_dt_proj_b", [DI]),
                    ("L0_attn_A_log", [DI, NST]), ("L0_attn_D_param", [DI]),
                    ("L0_attn_out_proj_w", [DM, DI]),
                    ("L0_ffn_W1", [DFF, DM]), ("L0_ffn_b1", [DFF]),
                    ("L0_ffn_W2", [DM, DFF]), ("L0_ffn_b2", [DM]),
                    ("L1_attn_Wq", [DM, DM]), ("L1_attn_Wkv", [512, DM]),
                    ("L1_attn_Wout", [DM, DM]), ("L1_attn_lambda_init", [NH]),
                    ("L1_attn_rms_weight", [HD]),
                    ("cent_pad", [16, DM]),
                    ("L1_ffn_gamma", [NEXP, DM]), ("L1_ffn_beta", [NEXP, DM]),
                    ("L1_ffn_W1", [NEXP, DFF, DM]), ("L1_ffn_b1", [NEXP, DFF]),
                    ("L1_ffn_W2", [NEXP, DM, DFF]), ("L1_ffn_b2", [NEXP, DM])):
        p.inp(nm, shp, F32)
    p.inp("a_qcos", [QOWN, DM], BF16)
    p.inp("a_qsin", [QOWN, DM], BF16)
    p.inp("a_kcos", [L, 256], BF16)
    p.inp("a_ksin", [L, 256], BF16)
    p.inp("a_mask", [4, 8, 128, 512], BF16)
    p.inp("a_flags", [128, 2], F32)
    out = p.out("out_own", [QOWN, DM], F32)
    x1 = p.scratch("x1_s", [L, DM], BF16)
    x2 = p.scratch("x2_s", [L, DM], BF16)
    x3 = p.scratch("x3_s", [QOWN, DM], BF16)
    emit_mamba(p, x, x1)
    emit_ffn(p, x1, x2)
    emit_attn(p, x2, x3)
    emit_moe(p, x3, out)
    return p


def kernel_fused(inputs):
    consts = const_arrays()
    xpw_pad = np.zeros((128, DI), np.float32)
    xpw_pad[:96] = inputs["L0_attn_x_proj_w"]
    dtp_pad = np.zeros((DI, 128), np.float32)
    dtp_pad[:, :64] = inputs["L0_attn_dt_proj_w"]
    cent_pad = np.zeros((16, DM), np.float32)
    cent_pad[:8] = inputs["L1_ffn_centroids"]
    tabs = [host_attn_tables(par) for par in (0, 1)]
    pp = build_prog_all()
    ncx = pp.finish()
    names = set(pp.in_shapes)
    maps = []
    for c in range(8):
        m = dict(consts)
        m["x"] = np.ascontiguousarray(np.asarray(inputs["L0_attn_x"])[c // 2])
        m["xpw_pad"] = xpw_pad
        m["dtp_pad"] = dtp_pad
        m["cent_pad"] = cent_pad
        m.update(tabs[c % 2][0])
        for nm in names:
            if nm not in m and nm in inputs:
                m[nm] = np.ascontiguousarray(np.asarray(inputs[nm]))
        maps.append(m)
    res = run_bass_kernel_spmd(ncx, maps, core_ids=list(range(NCORES)))
    out = np.empty((B, L, DM), np.float32)
    for c in range(8):
        out[c // 2][tabs[c % 2][1]] = res.results[c]["out_own"]
    return out

M_IN = ("L0_attn_ln_gamma", "L0_attn_ln_beta", "L0_attn_in_proj_w",
        "L0_attn_conv_w", "L0_attn_conv_b", "L0_attn_dt_proj_b",
        "L0_attn_A_log", "L0_attn_D_param", "L0_attn_out_proj_w")
F_IN = ("L0_ffn_W1", "L0_ffn_b1", "L0_ffn_W2", "L0_ffn_b2")
A_IN = ("L1_attn_Wq", "L1_attn_Wkv", "L1_attn_Wout", "L1_attn_lambda_init",
        "L1_attn_rms_weight")
E_IN = ("L1_ffn_gamma", "L1_ffn_beta", "L1_ffn_W1", "L1_ffn_b1",
        "L1_ffn_W2", "L1_ffn_b2")


def kernel(**inputs):
    return kernel_fused(inputs)


def kernel_unfused(**inputs):
    consts = const_arrays()
    xpw_pad = np.zeros((128, DI), np.float32)
    xpw_pad[:96] = inputs["L0_attn_x_proj_w"]
    dtp_pad = np.zeros((DI, 128), np.float32)
    dtp_pad[:, :64] = inputs["L0_attn_dt_proj_w"]
    cent_pad = np.zeros((16, DM), np.float32)
    cent_pad[:8] = inputs["L1_ffn_centroids"]
    tabs = [host_attn_tables(par) for par in (0, 1)]

    pm = build_prog_m()
    ncm = pm.finish()
    maps = []
    for c in range(8):
        m = dict(consts)
        m["x"] = np.ascontiguousarray(np.asarray(inputs["L0_attn_x"])[c // 2])
        m["xpw_pad"] = xpw_pad
        m["dtp_pad"] = dtp_pad
        for nm in M_IN:
            m[nm] = np.ascontiguousarray(np.asarray(inputs[nm]))
        maps.append(m)
    res = run_bass_kernel_spmd(ncm, maps, core_ids=list(range(NCORES)))
    x1s = [r["x1"] for r in res.results]

    pf = build_prog_f()
    ncf = pf.finish()
    maps = []
    for c in range(8):
        m = dict(consts)
        m["x1"] = x1s[c]
        for nm in F_IN:
            m[nm] = np.ascontiguousarray(np.asarray(inputs[nm]))
        maps.append(m)
    res = run_bass_kernel_spmd(ncf, maps, core_ids=list(range(NCORES)))
    x2s = [r["x2"] for r in res.results]

    pa = build_prog_a()
    nca = pa.finish()
    maps = []
    for c in range(8):
        m = dict(consts)
        m["x2"] = x2s[c]
        for nm in A_IN:
            m[nm] = np.ascontiguousarray(np.asarray(inputs[nm]))
        m.update(tabs[c % 2][0])
        maps.append(m)
    res = run_bass_kernel_spmd(nca, maps, core_ids=list(range(NCORES)))
    x3s = [r["x3"] for r in res.results]

    pe = build_prog_e()
    nce = pe.finish()
    maps = []
    for c in range(8):
        m = dict(consts)
        m["x3"] = x3s[c]
        m["cent_pad"] = cent_pad
        for nm in E_IN:
            m[nm] = np.ascontiguousarray(np.asarray(inputs[nm]))
        maps.append(m)
    res = run_bass_kernel_spmd(nce, maps, core_ids=list(range(NCORES)))

    out = np.empty((B, L, DM), np.float32)
    for c in range(8):
        own_rows = tabs[c % 2][1]
        out[c // 2][own_rows] = res.results[c]["out_own"]
    return out


# version 05 (final)
# speedup vs baseline: 1.1328x; 1.1328x over previous
import contextlib
import numpy as np
import ml_dtypes
import concourse.bass as bass
import concourse.mybir as mybir
from concourse.bass_utils import run_bass_kernel_spmd

F32 = mybir.dt.float32
BF16 = mybir.dt.bfloat16
AF = mybir.ActivationFunctionType
ALU = mybir.AluOpType
AX = mybir.AxisListType
BF16NP = ml_dtypes.bfloat16

NCORES = 8
B = 4
L = 4096
DM = 1024
DI = 2048
NST = 16
DTR = 64
NH = 16
NKV = 4
HD = 64
HF = 32
NEXP = 8
DFF = 4096
NBLK = L // 128
QOWN = L // 2
EPS_LN = 1e-5

ENGINES = ("te", "dv", "sc", "gp", "sy")


def own_blocks(parity):
    out = []
    for k in range(NBLK // 4):
        out += ([4 * k, 4 * k + 3] if parity == 0 else [4 * k + 1, 4 * k + 2])
    return sorted(out)


class Tok:
    __slots__ = ("eng", "cnt")

    def __init__(self, eng, cnt):
        self.eng = eng
        self.cnt = cnt


class Buf:
    __slots__ = ("ap_", "writer", "readers", "name", "cm")

    def __init__(self, ap, name="", cm=None):
        self.ap_ = ap
        self.writer = None
        self.readers = {}
        self.name = name
        self.cm = cm

    def ap(self):
        return self.ap_

    def __getitem__(self, idx):
        return self.ap_[idx]


class Sched:
    def __init__(self, nc, stack):
        self.nc = nc
        self.recs = {e: [] for e in ENGINES}
        self.cnt = {e: 0 for e in ENGINES}
        self.waited = {c: {p: 0 for p in ENGINES} for c in ENGINES}
        self.sems = {e: stack.enter_context(nc.semaphore("sem_" + e))
                     for e in ENGINES}

    def _wait(self, eng, tok):
        if tok is None or tok.cnt == 0:
            return
        if self.waited[eng][tok.eng] >= tok.cnt:
            return
        self.waited[eng][tok.eng] = tok.cnt
        sem, cnt = self.sems[tok.eng], tok.cnt

        def emit(h):
            h.wait_ge(sem, cnt)
        self.recs[eng].append(emit)

    def op(self, eng, emit_fn, reads=(), writes=(), dma=False):
        for b in reads:
            if isinstance(b, Buf):
                self._wait(eng, b.writer)
        for b in writes:
            if isinstance(b, Buf):
                self._wait(eng, b.writer)
                for re_, rc in b.readers.items():
                    self._wait(eng, Tok(re_, rc))
        inc = 16 if dma else 1
        self.cnt[eng] += inc
        tok = Tok(eng, self.cnt[eng])
        sem = self.sems[eng]

        def emit(h):
            emit_fn(h).then_inc(sem, inc)
        self.recs[eng].append(emit)
        for b in reads:
            if isinstance(b, Buf):
                b.readers[eng] = tok.cnt
        for b in writes:
            if isinstance(b, Buf):
                b.writer = tok
                b.readers = {}
        return tok

    def fence(self, eng):
        for e in ENGINES:
            self._wait(eng, Tok(e, self.cnt[e]))

    def fence_all(self):
        for e in ENGINES:
            self.fence(e)

    def flush(self, block):
        for ename, section in (("te", block.tensor), ("dv", block.vector),
                               ("sc", block.scalar), ("gp", block.gpsimd),
                               ("sy", block.sync)):
            recs = self.recs[ename]

            def make(rr):
                def _(h):
                    for r in rr:
                        r(h)
                return _
            section(make(recs))


class Prog:
    def __init__(self, name="p"):
        self.nc = bass.Bass("TRN2", target_bir_lowering=False, debug=False)
        self.stack = contextlib.ExitStack()
        self.stack.__enter__()
        self.s = Sched(self.nc, self.stack)
        self.ext_in = {}
        self.in_shapes = {}
        self.sb_stack = []
        self.ps_stack = []

    def inp(self, name, shape, dt=F32):
        b = Buf(self.nc.dram_tensor(name, shape, dt, kind="ExternalInput").ap(), name)
        self.ext_in[name] = b
        self.in_shapes[name] = (tuple(shape), dt)
        return b

    def out(self, name, shape, dt=F32):
        return Buf(self.nc.dram_tensor(name, shape, dt, kind="ExternalOutput").ap(), name)

    def scratch(self, name, shape, dt):
        return Buf(self.nc.dram_tensor(name, shape, dt).ap(), name)

    def sbuf(self, name, shape, dt):
        cm = self.nc.sbuf_tensor(name, shape, dt)
        h = cm.__enter__()
        b = Buf(h.ap(), name, cm)
        self.sb_stack.append(b)
        return b

    def psum(self, name, shape, dt=F32):
        cm = self.nc.psum_tensor(name, shape, dt)
        h = cm.__enter__()
        b = Buf(h.ap(), name, cm)
        self.ps_stack.append(b)
        return b

    def mark(self):
        return (len(self.sb_stack), len(self.ps_stack))

    def free_to(self, mark):
        ns, np_ = mark
        while len(self.sb_stack) > ns:
            b = self.sb_stack.pop()
            if b.cm is not None:
                b.cm.__exit__(None, None, None)
                b.cm = None
        while len(self.ps_stack) > np_:
            b = self.ps_stack.pop()
            if b.cm is not None:
                b.cm.__exit__(None, None, None)
                b.cm = None

    def finish(self):
        self.s.fence("sy")
        with self.nc.Block() as block:
            self.s.flush(block)
        self.free_to((0, 0))
        self.stack.__exit__(None, None, None)
        return self.nc



def ln0_rows(p, src_buf, src_ap, stat_buf, c0, scrap, F=DM):
    s = p.s
    mu = stat_buf.ap()[:, c0:c0 + 1]
    rs = stat_buf.ap()[:, c0 + 1:c0 + 2]
    s.op("dv", lambda h: h.reduce_sum(mu, src_ap, axis=AX.X),
         reads=[src_buf], writes=[stat_buf])
    s.op("sc", lambda h: h.activation(out=scrap.ap()[:, 0:F], in_=src_ap,
                                      func=AF.Square, accum_out=rs),
         reads=[src_buf], writes=[stat_buf, scrap])
    s.op("dv", lambda h: h.tensor_scalar_mul(out=mu, in0=mu, scalar1=1.0 / F),
         reads=[stat_buf], writes=[stat_buf])
    s.op("dv", lambda h: h.tensor_scalar_mul(out=rs, in0=rs, scalar1=1.0 / F),
         reads=[stat_buf], writes=[stat_buf])
    s.op("dv", lambda h: h.scalar_tensor_tensor(out=rs, in0=mu, scalar=mu, in1=rs,
                                                op0=ALU.mult, op1=ALU.subtract),
         reads=[stat_buf], writes=[stat_buf])
    s.op("sc", lambda h: h.activation(out=rs, in_=rs, func=AF.Sqrt, scale=-1.0,
                                      bias=EPS_LN), reads=[stat_buf], writes=[stat_buf])
    s.op("dv", lambda h: h.reciprocal(out=rs, in_=rs), reads=[stat_buf], writes=[stat_buf])


def cast_via_hbm(p, src_dram_ap, dst_dram_buf, nrows, ncols, stg, tag):
    s = p.s
    CH = max(128, ((1 << 22) // (ncols * 4)) // 128 * 128)
    r = 0
    while r < nrows:
        rr = min(CH, nrows - r)
        s.op("gp", lambda h, r=r, rr=rr: h.dma_start(
            out=dst_dram_buf.ap()[r:r + rr, :], in_=src_dram_ap[r:r + rr, :]),
            writes=[dst_dram_buf], dma=True)
        r += rr



def emit_mamba(p, x, x1_hbm, upto=99.0):
    nc, s = p.nc, p.s
    mark_phase = p.mark()
    dump = getattr(p, "dump", None)
    NT = L // 128
    NC = DI // 128
    NQ = 4
    QL = L // NQ

    ident, onesb = p.ident, p.onesb

    wb = p.scratch("m_wb", [2 * DI, DM], BF16)
    wdtp = p.scratch("m_wdtp", [DI, 128], BF16)
    wxp = p.scratch("m_wxp", [128, DI], BF16)
    wop = p.scratch("m_wop", [DM, DI], BF16)
    t2h = p.scratch("m_t2h", [L, DM], BF16)
    xch = p.scratch("m_xch", [NC, 128, L], BF16)
    zsh = p.scratch("m_zsh", [NC, 128, L], BF16)
    ygh = p.scratch("m_ygh", [NC, 128, L], BF16)

    cast_via_hbm(p, p.ext_in["L0_attn_in_proj_w"].ap(), wb, 2 * DI, DM, None, "wb")
    cast_via_hbm(p, p.ext_in["xpw_pad"].ap(), wxp, 128, DI, None, "wxp")
    cast_via_hbm(p, p.ext_in["dtp_pad"].ap(), wdtp, DI, 128, None, "wdtp")
    cast_via_hbm(p, p.ext_in["L0_attn_out_proj_w"].ap(), wop, DM, DI, None, "wop")
    if upto < 1:
        return

    gcol = p.sbuf("m_gcol", [128, 8], F32)
    bcol = p.sbuf("m_bcol", [128, 8], BF16)
    cvec = p.sbuf("m_cvec", [1, 4160], BF16)
    convw = p.sbuf("m_convw", [128, NC * 4], F32)
    convb = p.sbuf("m_convb", [128, NC], F32)
    dtb = p.sbuf("m_dtb", [128, NC], F32)
    dpar = p.sbuf("m_dpar", [128, NC], F32)
    asc = p.sbuf("m_asc", [128, NC * NST], F32)
    dtpT = p.sbuf("m_dtpT", [128, DI], BF16)
    xpwT = p.sbuf("m_xpwT", [128, NC * 128], BF16)
    musig = p.sbuf("m_musig", [128, 2 * NT], F32)
    st2 = p.sbuf("m_st2", [128, 2], F32)
    xdbl = p.sbuf("m_xdbl", [128, 4096], F32)
    mark_w = p.mark()
    pc1 = p.psum("m_pc1", [1, 512])
    inWT = p.sbuf("m_inWT", [128, 8 * 4096], BF16)
    t2T = p.sbuf("m_t2T", [128, 8 * 4096], BF16)

    for c in range(8):
        s.op("sy", lambda h, c=c: h.dma_start(
            out=gcol.ap()[:, c:c + 1],
            in_=p.ext_in["L0_attn_ln_gamma"].ap()[c * 128:(c + 1) * 128].unsqueeze(1)),
            writes=[gcol], dma=True)
        s.op("gp", lambda h, c=c: h.dma_start(
            out=bcol.ap()[:, c:c + 1],
            in_=p.ext_in["L0_attn_ln_beta"].ap()[c * 128:(c + 1) * 128].unsqueeze(1)),
            writes=[bcol], dma=True)
    for c in range(8):
        s.op("sc", lambda h, c=c: h.dma_start_transpose(
            inWT.ap()[:, c * 4096:(c + 1) * 4096], wb.ap()[:, c * 128:(c + 1) * 128]),
            reads=[wb], writes=[inWT], dma=True)
    for cc in range(8):
        for d in range(8):
            s.op("te", lambda h, cc=cc, d=d: h.matmul(
                pc1.ap(), bcol.ap()[:, d:d + 1],
                inWT.ap()[:, d * 4096 + cc * 512:d * 4096 + (cc + 1) * 512],
                start=(d == 0), stop=(d == 7)), reads=[bcol, inWT], writes=[pc1])
        s.op("dv", lambda h, cc=cc: h.tensor_copy(
            out=cvec.ap()[0:1, cc * 512:(cc + 1) * 512], in_=pc1.ap()),
            reads=[pc1], writes=[cvec])
    for c in range(8):
        s.op("dv", lambda h, c=c: h.tensor_scalar_mul(
            out=inWT.ap()[:, c * 4096:(c + 1) * 4096],
            in0=inWT.ap()[:, c * 4096:(c + 1) * 4096],
            scalar1=gcol.ap()[:, c:c + 1]), reads=[gcol, inWT], writes=[inWT])
    for c in range(NC):
        s.op("sy", lambda h, c=c: h.dma_start(
            out=convw.ap()[:, c * 4:(c + 1) * 4],
            in_=p.ext_in["L0_attn_conv_w"].ap()[c * 128:(c + 1) * 128, :]),
            writes=[convw], dma=True)
        s.op("sy", lambda h, c=c: h.dma_start(
            out=convb.ap()[:, c:c + 1],
            in_=p.ext_in["L0_attn_conv_b"].ap()[c * 128:(c + 1) * 128].unsqueeze(1)),
            writes=[convb], dma=True)
        s.op("sy", lambda h, c=c: h.dma_start(
            out=dtb.ap()[:, c:c + 1],
            in_=p.ext_in["L0_attn_dt_proj_b"].ap()[c * 128:(c + 1) * 128].unsqueeze(1)),
            writes=[dtb], dma=True)
        s.op("sy", lambda h, c=c: h.dma_start(
            out=dpar.ap()[:, c:c + 1],
            in_=p.ext_in["L0_attn_D_param"].ap()[c * 128:(c + 1) * 128].unsqueeze(1)),
            writes=[dpar], dma=True)
        s.op("sy", lambda h, c=c: h.dma_start(
            out=asc.ap()[:, c * NST:(c + 1) * NST],
            in_=p.ext_in["L0_attn_A_log"].ap()[c * 128:(c + 1) * 128, :]),
            writes=[asc], dma=True)
    s.op("sc", lambda h: h.activation(out=asc.ap(), in_=asc.ap(), func=AF.Exp),
         reads=[asc], writes=[asc])
    s.op("dv", lambda h: h.tensor_scalar_mul(out=asc.ap(), in0=asc.ap(), scalar1=-1.0),
         reads=[asc], writes=[asc])
    for c in range(NC):
        s.op("sc", lambda h, c=c: h.dma_start_transpose(
            xpwT.ap()[:, c * 128:(c + 1) * 128], wxp.ap()[:, c * 128:(c + 1) * 128]),
            reads=[wxp], writes=[xpwT], dma=True)
    s.op("sc", lambda h: h.dma_start_transpose(dtpT.ap(), wdtp.ap()),
         reads=[wdtp], writes=[dtpT], dma=True)
    if upto < 2:
        return

    mark_1a = p.mark()
    scrap = p.sbuf("m_scrap", [128, 1024], F32)
    xt = [p.sbuf(f"m_xt{i}", [128, 1024], F32) for i in range(2)]
    tt = [p.sbuf(f"m_tt{i}", [128, 1024], F32) for i in range(2)]
    t2t = [p.sbuf(f"m_t2t{i}", [128, 1024], BF16) for i in range(2)]

    for i in range(NT):
        xb, tb, t2b = xt[i % 2], tt[i % 2], t2t[i % 2]
        s.op("sy", lambda h, i=i, xb=xb: h.dma_start(
            out=xb.ap(), in_=x.ap()[i * 128:(i + 1) * 128, :]), writes=[xb], dma=True)
        ln0_rows(p, xb, xb.ap(), musig, 2 * i, scrap)
        mu = musig.ap()[:, 2 * i:2 * i + 1]
        rs = musig.ap()[:, 2 * i + 1:2 * i + 2]
        s.op("dv", lambda h, xb=xb, tb=tb, mu=mu, rs=rs: h.tensor_scalar(
            out=tb.ap(), in0=xb.ap(), scalar1=mu, scalar2=rs,
            op0=ALU.subtract, op1=ALU.mult), reads=[xb, musig], writes=[tb])
        ln0_rows(p, tb, tb.ap(), st2, 0, scrap)
        s.op("dv", lambda h, tb=tb, t2b=t2b: h.tensor_scalar(
            out=t2b.ap(), in0=tb.ap(), scalar1=st2.ap()[:, 0:1],
            scalar2=st2.ap()[:, 1:2], op0=ALU.subtract, op1=ALU.mult),
            reads=[tb, st2], writes=[t2b])
        s.op("sy", lambda h, i=i, t2b=t2b: h.dma_start(
            out=t2h.ap()[i * 128:(i + 1) * 128, :], in_=t2b.ap()),
            reads=[t2b], writes=[t2h], dma=True)
    for c in range(8):
        s.op("sc", lambda h, c=c: h.dma_start_transpose(
            t2T.ap()[:, c * 4096:(c + 1) * 4096], t2h.ap()[:, c * 128:(c + 1) * 128]),
            reads=[t2h], writes=[t2T], dma=True)
    s.fence_all()
    p.free_to(mark_1a)
    if upto < 3:
        return

    xs_sb = p.sbuf("m_xs", [128, 4100], F32)
    cacc = p.sbuf("m_cacc", [128, 2048], F32)
    xc_sb = p.sbuf("m_xcsb", [128, 4096], BF16)
    zc_sb = p.sbuf("m_zcsb", [128, 4096], BF16)
    pxz = [p.psum(f"m_pxz{i}", [128, 512]) for i in range(4)]
    pxd = p.psum("m_pxd", [128, 512])

    def inproj_chunk(row0, evac):
        for tb in range(8):
            ps = pxz[tb % 4]

            def grp(h, row0=row0, tb=tb, ps=ps):
                for d in range(8):
                    h.matmul(
                        ps.ap(), inWT.ap()[:, d * 4096 + row0:d * 4096 + row0 + 128],
                        t2T.ap()[:, d * 4096 + tb * 512:d * 4096 + (tb + 1) * 512],
                        start=(d == 0), stop=False)
                return h.matmul(
                    ps.ap(), cvec.ap()[0:1, row0:row0 + 128], onesb.ap()[0:1, 0:512],
                    start=False, stop=True)
            s.op("te", grp, reads=[inWT, t2T, cvec, onesb], writes=[ps])
            evac(tb, ps)

    s.op("dv", lambda h: h.memset(xs_sb.ap()[:, 0:4], 0.0), writes=[xs_sb])

    def emit_xs_chunk(cc):
        def evac_xs(tb, ps):
            s.op("sc", lambda h, tb=tb, ps=ps: h.copy(
                out=xs_sb.ap()[:, 4 + tb * 512:4 + (tb + 1) * 512], in_=ps.ap()),
                reads=[ps], writes=[xs_sb])
        inproj_chunk(cc * 128, evac_xs)
        for hf_ in range(2):
            o0 = hf_ * 2048
            s.op("dv", lambda h, cc=cc, o0=o0: h.tensor_scalar_mul(
                out=cacc.ap(), in0=xs_sb.ap()[:, 1 + o0:2049 + o0],
                scalar1=convw.ap()[:, cc * 4:cc * 4 + 1]),
                reads=[xs_sb, convw], writes=[cacc])
            for j in (1, 2, 3):
                s.op("dv", lambda h, cc=cc, j=j, o0=o0: h.scalar_tensor_tensor(
                    out=cacc.ap(), in0=xs_sb.ap()[:, 1 + j + o0:2049 + j + o0],
                    scalar=convw.ap()[:, cc * 4 + j:cc * 4 + j + 1],
                    in1=cacc.ap(), op0=ALU.mult, op1=ALU.add),
                    reads=[xs_sb, convw, cacc], writes=[cacc])
            s.op("sc", lambda h, cc=cc, o0=o0: h.activation(
                out=xc_sb.ap()[:, o0:o0 + 2048], in_=cacc.ap(), func=AF.Silu,
                bias=convb.ap()[:, cc:cc + 1]), reads=[cacc, convb], writes=[xc_sb])
        s.op("sy", lambda h, cc=cc: h.dma_start(
            out=xch.ap()[cc], in_=xc_sb.ap()), reads=[xc_sb], writes=[xch], dma=True)
        for tb in range(8):
            s.op("te", lambda h, cc=cc, tb=tb: h.matmul(
                pxd.ap(), xpwT.ap()[:, cc * 128:(cc + 1) * 128],
                xc_sb.ap()[:, tb * 512:(tb + 1) * 512], start=True, stop=True),
                reads=[xpwT, xc_sb], writes=[pxd])
            if cc == 0:
                s.op("dv", lambda h, tb=tb: h.tensor_copy(
                    out=xdbl.ap()[:, tb * 512:(tb + 1) * 512], in_=pxd.ap()),
                    reads=[pxd], writes=[xdbl])
            else:
                s.op("dv", lambda h, tb=tb: h.tensor_tensor(
                    out=xdbl.ap()[:, tb * 512:(tb + 1) * 512],
                    in0=xdbl.ap()[:, tb * 512:(tb + 1) * 512],
                    in1=pxd.ap(), op=ALU.add), reads=[pxd, xdbl], writes=[xdbl])

    def emit_z_chunk(zc):
        def evac_z(tb, ps):
            s.op("sc", lambda h, tb=tb, ps=ps: h.activation(
                out=zc_sb.ap()[:, tb * 512:(tb + 1) * 512], in_=ps.ap(),
                func=AF.Silu), reads=[ps], writes=[zc_sb])
        inproj_chunk(DI + zc * 128, evac_z)
        s.op("sy", lambda h, zc=zc: h.dma_start(
            out=zsh.ap()[zc], in_=zc_sb.ap()), reads=[zc_sb], writes=[zsh], dma=True)

    for cc in range(NC):
        emit_xs_chunk(cc)
        emit_z_chunk(cc)
    s.fence_all()
    p.free_to(mark_w)
    if upto < 4:
        return

    mark_2 = p.mark()
    dtlT = p.sbuf("m_dtlT", [64, 4096], BF16)
    s.op("dv", lambda h: h.tensor_copy(out=dtlT.ap(), in_=xdbl.ap()[0:64, :]),
         reads=[xdbl], writes=[dtlT])
    BbL = [p.sbuf(f"m_Bb{i}", [128, NST * QL], BF16) for i in range(2)]
    CbL = [p.sbuf(f"m_Cb{i}", [128, NST * QL], BF16) for i in range(2)]
    rowb = [p.sbuf(f"m_rowb{i}", [1, QL], BF16) for i in range(2)]
    rbuf = p.sbuf("m_rbuf", [128, QL], F32)
    abuf = p.sbuf("m_abuf", [128, QL], F32)
    ubuf = p.sbuf("m_ubuf", [128, QL], BF16)
    xcq = p.sbuf("m_xcq", [128, QL], BF16)
    zsq = p.sbuf("m_zsq", [128, QL], BF16)
    dab = [p.sbuf(f"m_dab{i}", [128, QL], BF16) for i in range(2)]
    dxb = [p.sbuf(f"m_dxb{i}", [128, QL], BF16) for i in range(2)]
    hbuf = [p.sbuf(f"m_hbuf{i}", [128, QL], BF16) for i in range(2)]
    zbuf = [p.sbuf(f"m_zbuf{i}", [128, QL], BF16) for i in range(2)]
    hmid = p.sbuf("m_hmid", [128, NC * NST], BF16)
    gb = ubuf
    pdt = p.psum("m_pdt", [128, QL])
    pyy = p.psum("m_pyy", [128, QL])
    pbc = p.psum("m_pbc", [128, QL])

    def emit_bc(q):
        q0 = q * QL
        Bq, Cq = BbL[q % 2], CbL[q % 2]
        for n in range(2 * NST):
            rb = rowb[n % 2]
            dst = Bq if n < NST else Cq
            nn = n % NST
            s.op("gp", lambda h, n=n, q0=q0, rb=rb: h.dma_start(
                out=rb.ap(), in_=xdbl.ap()[64 + n:65 + n, q0:q0 + QL]),
                reads=[xdbl], writes=[rb], dma=True)

            def bcgrp(h, rb=rb):
                for sub in range(QL // 512):
                    ins = h.matmul(
                        pbc.ap()[:, sub * 512:(sub + 1) * 512], onesb.ap()[0:1, 0:128],
                        rb.ap()[0:1, sub * 512:(sub + 1) * 512], start=True, stop=True)
                return ins
            s.op("te", bcgrp, reads=[onesb, rb], writes=[pbc])
            s.op("dv", lambda h, nn=nn, dst=dst: h.tensor_copy(
                out=dst.ap()[:, nn * QL:(nn + 1) * QL], in_=pbc.ap()),
                reads=[pbc], writes=[dst])

    emit_bc(0)
    for q in range(NQ):
        q0 = q * QL
        Bb, Cb = BbL[q % 2], CbL[q % 2]
        for cc in range(NC):
            if cc == 1 and q < NQ - 1:
                emit_bc(q + 1)
            def dtgrp(h, cc=cc, q0=q0):
                for sub in range(QL // 512):
                    ins = h.matmul(
                        pdt.ap()[:, sub * 512:(sub + 1) * 512],
                        dtpT.ap()[0:64, cc * 128:(cc + 1) * 128],
                        dtlT.ap()[:, q0 + sub * 512:q0 + (sub + 1) * 512],
                        start=True, stop=True)
                return ins
            s.op("te", dtgrp, reads=[dtpT, dtlT], writes=[pdt])
            s.op("dv", lambda h, cc=cc: h.tensor_scalar(
                out=rbuf.ap(), in0=pdt.ap(), scalar1=dtb.ap()[:, cc:cc + 1],
                scalar2=0.0, op0=ALU.add, op1=ALU.max),
                reads=[pdt, dtb], writes=[rbuf])
            s.op("dv", lambda h: h.scalar_tensor_tensor(
                out=abuf.ap(), in0=rbuf.ap(), scalar=-2.0, in1=pdt.ap(),
                op0=ALU.mult, op1=ALU.add), reads=[rbuf, pdt], writes=[abuf])
            s.op("sc", lambda h, cc=cc: h.activation(out=abuf.ap(), in_=abuf.ap(),
                                              func=AF.Exp,
                                              bias=dtb.ap()[:, cc:cc + 1]),
                 reads=[abuf, dtb], writes=[abuf])
            s.op("sc", lambda h: h.activation(out=abuf.ap(), in_=abuf.ap(),
                                              func=AF.Ln, bias=1.0),
                 reads=[abuf], writes=[abuf])
            s.op("dv", lambda h: h.tensor_tensor(out=rbuf.ap(), in0=rbuf.ap(),
                                                 in1=abuf.ap(), op=ALU.add),
                 reads=[rbuf, abuf], writes=[rbuf])
            s.op("sy", lambda h, cc=cc, q0=q0: h.dma_start(
                out=xcq.ap(), in_=xch.ap()[cc][:, q0:q0 + QL]), writes=[xcq], dma=True)
            s.op("sy", lambda h, cc=cc, q0=q0: h.dma_start(
                out=zsq.ap(), in_=zsh.ap()[cc][:, q0:q0 + QL]), writes=[zsq], dma=True)
            s.op("dv", lambda h: h.tensor_tensor(out=ubuf.ap(), in0=rbuf.ap(),
                                                 in1=xcq.ap(), op=ALU.mult),
                 reads=[rbuf, xcq], writes=[ubuf])
            for n in range(NST):
                da, dx, hb, zb = dab[n % 2], dxb[n % 2], hbuf[n % 2], zbuf[n % 2]
                s.op("sc", lambda h, cc=cc, n=n, da=da: h.activation(
                    out=da.ap(), in_=rbuf.ap(), func=AF.Exp,
                    scale=asc.ap()[:, cc * NST + n:cc * NST + n + 1]),
                    reads=[rbuf, asc], writes=[da])
                s.op("gp", lambda h, n=n, dx=dx, Bb=Bb: h.tensor_tensor(
                    out=dx.ap(), in0=ubuf.ap(), in1=Bb.ap()[:, n * QL:(n + 1) * QL],
                    op=ALU.mult), reads=[ubuf, Bb], writes=[dx])
                init = 0.0 if q == 0 else hmid.ap()[:, cc * NST + n:cc * NST + n + 1]
                s.op("dv", lambda h, da=da, dx=dx, hb=hb, init=init: h.tensor_tensor_scan(
                    out=hb.ap(), data0=da.ap(), data1=dx.ap(), initial=init,
                    op0=ALU.mult, op1=ALU.add),
                    reads=[da, dx] + ([hmid] if q > 0 else []), writes=[hb])
                if q < NQ - 1:
                    s.op("sc", lambda h, cc=cc, n=n, hb=hb: h.copy(
                        out=hmid.ap()[:, cc * NST + n:cc * NST + n + 1],
                        in_=hb.ap()[:, QL - 1:QL]), reads=[hb], writes=[hmid])
                s.op("dv", lambda h, n=n, hb=hb, zb=zb, Cb=Cb: h.tensor_tensor(
                    out=zb.ap(), in0=hb.ap(), in1=Cb.ap()[:, n * QL:(n + 1) * QL],
                    op=ALU.mult), reads=[hb, Cb], writes=[zb])
                def ygrp(h, n=n, zb=zb):
                    for sub in range(QL // 512):
                        ins = h.matmul(
                            pyy.ap()[:, sub * 512:(sub + 1) * 512], ident.ap(),
                            zb.ap()[:, sub * 512:(sub + 1) * 512],
                            start=(n == 0), stop=(n == NST - 1))
                    return ins
                s.op("te", ygrp, reads=[ident, zb], writes=[pyy])
            s.op("dv", lambda h, cc=cc: h.scalar_tensor_tensor(
                out=gb.ap(), in0=xcq.ap(), scalar=dpar.ap()[:, cc:cc + 1],
                in1=pyy.ap(), op0=ALU.mult, op1=ALU.add),
                reads=[xcq, dpar, pyy], writes=[gb])
            s.op("dv", lambda h: h.tensor_tensor(out=gb.ap(), in0=gb.ap(),
                                                 in1=zsq.ap(), op=ALU.mult),
                 reads=[gb, zsq], writes=[gb])
            s.op("sy", lambda h, cc=cc, q0=q0: h.dma_start(
                out=ygh.ap()[cc][:, q0:q0 + QL], in_=gb.ap()),
                reads=[gb], writes=[ygh], dma=True)
    s.fence_all()
    p.free_to(mark_2)

    if dump is not None:
        nm, src, shp = dump
        dout = p.out("dump", shp, BF16)
        s.op("sy", lambda h: h.dma_start(
            out=dout.ap(), in_={"xch": xch, "zsh": zsh, "ygh": ygh,
                                "t2h": t2h}[src].ap().rearrange("a b c -> (a b) c")
            if src in ("xch", "zsh", "ygh") else
            {"t2h": t2h}[src].ap()), reads=[], writes=[dout], dma=True)

    mark_3 = p.mark()
    opwT = p.sbuf("m_opwT", [128, NC * 1024], BF16)
    scrap = p.sbuf("m_scrap3", [128, 1024], F32)
    xt = [p.sbuf(f"m_3xt{i}", [128, 1024], F32) for i in range(2)]
    tt = [p.sbuf(f"m_3tt{i}", [128, 1024], F32) for i in range(2)]
    sbfL = [p.sbuf(f"m_sbf{i}", [128, 1024], F32) for i in range(2)]
    x1t = [p.sbuf(f"m_x1t{i}", [128, 1024], BF16) for i in range(2)]
    ygt = [p.sbuf(f"m_ygt{i}", [128, 2048], BF16) for i in range(2)]
    poutL = [p.psum(f"m_pout{i}", [128, 1024]) for i in range(2)]

    for c in range(NC):
        s.op("sc", lambda h, c=c: h.dma_start_transpose(
            opwT.ap()[:, c * 1024:(c + 1) * 1024], wop.ap()[:, c * 128:(c + 1) * 128]),
            reads=[wop], writes=[opwT], dma=True)

    for i in range(NT):
        ygs = ygt[i % 2]
        pout = poutL[i % 2]
        sbf = sbfL[i % 2]
        for cc in range(NC):
            s.op("sy", lambda h, i=i, cc=cc, ygs=ygs: h.dma_start(
                out=ygs.ap()[:, cc * 128:(cc + 1) * 128],
                in_=ygh.ap()[cc][:, i * 128:(i + 1) * 128]),
                writes=[ygs], dma=True)

        def opgrp(h, ygs=ygs, pout=pout):
            for cc in range(NC):
                for dh in range(2):
                    ins = h.matmul(
                        pout.ap()[:, dh * 512:(dh + 1) * 512],
                        ygs.ap()[:, cc * 128:(cc + 1) * 128],
                        opwT.ap()[:, cc * 1024 + dh * 512:cc * 1024 + (dh + 1) * 512],
                        start=(cc == 0), stop=(cc == NC - 1))
            return ins
        s.op("te", opgrp, reads=[ygs, opwT], writes=[pout])
        xb, tb = xt[i % 2], tt[i % 2]
        s.op("sy", lambda h, i=i, xb=xb: h.dma_start(
            out=xb.ap(), in_=x.ap()[i * 128:(i + 1) * 128, :]), writes=[xb], dma=True)
        mu = musig.ap()[:, 2 * i:2 * i + 1]
        rs = musig.ap()[:, 2 * i + 1:2 * i + 2]
        s.op("dv", lambda h, xb=xb, tb=tb, mu=mu, rs=rs: h.tensor_scalar(
            out=tb.ap(), in0=xb.ap(), scalar1=mu, scalar2=rs,
            op0=ALU.subtract, op1=ALU.mult), reads=[xb, musig], writes=[tb])
        s.op("dv", lambda h, tb=tb, sbf=sbf, pout=pout: h.tensor_tensor(
            out=sbf.ap(), in0=pout.ap(), in1=tb.ap(), op=ALU.add),
            reads=[pout, tb], writes=[sbf])
        ln0_rows(p, sbf, sbf.ap(), st2, 0, scrap)
        x1b = x1t[i % 2]
        s.op("dv", lambda h, sbf=sbf: h.tensor_scalar(
            out=sbf.ap(), in0=sbf.ap(), scalar1=st2.ap()[:, 0:1],
            scalar2=st2.ap()[:, 1:2], op0=ALU.subtract, op1=ALU.mult),
            reads=[sbf, st2], writes=[sbf])
        s.op("dv", lambda h, xb=xb, x1b=x1b, sbf=sbf: h.tensor_tensor(
            out=x1b.ap(), in0=sbf.ap(), in1=xb.ap(), op=ALU.add),
            reads=[sbf, xb], writes=[x1b])
        s.op("sy", lambda h, i=i, x1b=x1b: h.dma_start(
            out=x1_hbm.ap()[i * 128:(i + 1) * 128, :], in_=x1b.ap()),
            reads=[x1b], writes=[x1_hbm], dma=True)
    s.fence_all()
    p.free_to(mark_phase)



CONST_VALS = [0.0, 1.0, -1.0, 0.5, -0.5, 1e-5, 1e-6, 2.0]


def make_consts(p):
    identf_h = p.inp("c_identf", [128, 128], F32)
    ident_h = p.inp("c_ident", [128, 128], BF16)
    ones_h = p.inp("c_ones", [1, 512], BF16)
    cf_h = p.inp("c_f32", [128, len(CONST_VALS)], F32)
    p.ident = p.sbuf("c_ident_sb", [128, 128], BF16)
    p.identf = p.sbuf("c_identf_sb", [128, 128], F32)
    p.onesb = p.sbuf("c_ones_sb", [1, 512], BF16)
    p.cpool = p.sbuf("c_f32_sb", [128, len(CONST_VALS)], F32)
    p.cpoolb = p.sbuf("c_bf_sb", [128, len(CONST_VALS)], BF16)
    p.s.op("sy", lambda h: h.dma_start(out=p.ident.ap(), in_=ident_h.ap()),
           writes=[p.ident], dma=True)
    p.s.op("sy", lambda h: h.dma_start(out=p.identf.ap(), in_=identf_h.ap()),
           writes=[p.identf], dma=True)
    p.s.op("sy", lambda h: h.dma_start(out=p.onesb.ap(), in_=ones_h.ap()),
           writes=[p.onesb], dma=True)
    p.s.op("sy", lambda h: h.dma_start(out=p.cpool.ap(), in_=cf_h.ap()),
           writes=[p.cpool], dma=True)
    p.s.op("gp", lambda h: h.dma_start(out=p.cpoolb.ap(), in_=cf_h.ap()),
           writes=[p.cpoolb], dma=True)
    for i, v in enumerate(CONST_VALS):
        p.nc.const_aps.register(F32, v, p.cpool.ap()[:, i:i + 1])
        p.nc.const_aps.register(BF16, v, p.cpoolb.ap()[:, i:i + 1])


def const_arrays():
    return {
        "c_ident": np.eye(128, dtype=BF16NP),
        "c_identf": np.eye(128, dtype=np.float32),
        "c_ones": np.ones((1, 512), dtype=BF16NP),
        "c_f32": np.tile(np.array([CONST_VALS], dtype=np.float32), (128, 1)),
    }


M_WEIGHTS = ("L0_attn_ln_gamma", "L0_attn_ln_beta", "L0_attn_in_proj_w",
             "L0_attn_conv_w", "L0_attn_conv_b", "L0_attn_dt_proj_w",
             "L0_attn_dt_proj_b", "L0_attn_A_log", "L0_attn_D_param",
             "L0_attn_out_proj_w")


def build_prog_m(upto=99.0, dump=None):
    p = Prog("m")
    p.dump = dump
    make_consts(p)
    x = p.inp("x", [L, DM], F32)
    for nm, shp in (("L0_attn_ln_gamma", [DM]), ("L0_attn_ln_beta", [DM]),
                    ("L0_attn_in_proj_w", [2 * DI, DM]),
                    ("L0_attn_conv_w", [DI, 4]), ("L0_attn_conv_b", [DI]),
                    ("xpw_pad", [128, DI]),
                    ("dtp_pad", [DI, 128]), ("L0_attn_dt_proj_b", [DI]),
                    ("L0_attn_A_log", [DI, NST]), ("L0_attn_D_param", [DI]),
                    ("L0_attn_out_proj_w", [DM, DI])):
        p.inp(nm, shp, F32)
    x1 = p.out("x1", [L, DM], BF16)
    emit_mamba(p, x, x1, upto)
    return p





def emit_ffn(p, x1, x2_hbm):
    nc, s = p.nc, p.s
    mark_phase = p.mark()
    NT = L // 128
    ident, onesb = p.ident, p.onesb

    wq1h = p.scratch("f_wq1h", [DFF, DM], BF16)
    wq2h = p.scratch("f_wq2h", [DM, DFF], BF16)
    xn1h = p.scratch("f_xn1h", [L, DM], BF16)
    gscr = p.scratch("f_gscr", [1, 128], F32)

    mark_f = p.mark()
    st2 = p.sbuf("f_st2", [128, 2], F32)
    scrap = p.sbuf("f_scrap", [128, 1024], F32)

    wt = [p.sbuf(f"f_wt{i}", [128, 4096], F32) for i in range(2)]
    asum = p.sbuf("f_asum", [128, 64], F32)
    grow = p.sbuf("f_grow", [1, 130], F32)
    rg = p.sbuf("f_rg", [128, 2], F32)
    wq = p.sbuf("f_wq", [128, 4096], BF16)
    za = p.sbuf("f_za", [128, 4096], F32)
    zb = p.sbuf("f_zb", [128, 4096], F32)

    def quant(widx, src_name, nrows, ncols, dst):
        ntile = nrows // 128
        for t in range(ntile):
            w = wt[t % 2]
            s.op("sy", lambda h, t=t, w=w: h.dma_start(
                out=w.ap()[:, 0:ncols],
                in_=p.ext_in[src_name].ap()[t * 128:(t + 1) * 128, :]),
                writes=[w], dma=True)
            s.op("dv", lambda h, t=t, w=w: h.tensor_reduce(
                out=asum.ap()[:, t:t + 1], in_=w.ap()[:, 0:ncols],
                axis=AX.X, op=ALU.add, apply_absolute_value=True),
                reads=[w], writes=[asum])
        s.op("dv", lambda h, ntile=ntile: h.tensor_reduce(
            out=rg.ap()[:, 0:1], in_=asum.ap()[:, 0:ntile], axis=AX.X, op=ALU.add),
            reads=[asum], writes=[rg])
        s.op("gp", lambda h: h.dma_start(out=grow.ap()[0:1, 0:128],
                                         in_=rg.ap()[:, 0:1]),
             reads=[rg], writes=[grow], dma=True)
        s.op("dv", lambda h: h.tensor_reduce(
            out=grow.ap()[0:1, 128:129], in_=grow.ap()[0:1, 0:128],
            axis=AX.X, op=ALU.add), reads=[grow], writes=[grow])
        s.op("dv", lambda h, n=nrows * ncols: h.tensor_scalar(
            out=grow.ap()[0:1, 128:129], in0=grow.ap()[0:1, 128:129],
            scalar1=1.0 / n, scalar2=EPS_LN, op0=ALU.mult, op1=ALU.add),
            reads=[grow], writes=[grow])
        s.op("dv", lambda h: h.reciprocal(out=grow.ap()[0:1, 129:130],
                                          in_=grow.ap()[0:1, 128:129]),
             reads=[grow], writes=[grow])
        s.op("sy", lambda h: h.dma_start(out=gscr.ap()[0:1, widx:widx + 1],
                                         in_=grow.ap()[0:1, 129:130]),
             reads=[grow], writes=[p.gscr_buf], dma=True)
        s.op("sy", lambda h: h.dma_start(
            out=rg.ap()[:, 1:2],
            in_=gscr.ap()[0:1, widx:widx + 1].partition_broadcast(128)),
            reads=[p.gscr_buf], writes=[rg], dma=True)
        for t in range(ntile):
            w = wt[t % 2]
            s.op("sy", lambda h, t=t, w=w: h.dma_start(
                out=w.ap()[:, 0:ncols],
                in_=p.ext_in[src_name].ap()[t * 128:(t + 1) * 128, :]),
                writes=[w], dma=True)
            s.op("dv", lambda h, w=w: h.tensor_scalar_mul(
                out=w.ap()[:, 0:ncols], in0=w.ap()[:, 0:ncols],
                scalar1=rg.ap()[:, 1:2]), reads=[w, rg], writes=[w])
            s.op("dv", lambda h, w=w: h.tensor_scalar(
                out=za.ap()[:, 0:ncols], in0=w.ap()[:, 0:ncols],
                scalar1=0.5, scalar2=None, op0=ALU.is_ge),
                reads=[w], writes=[za])
            s.op("dv", lambda h, w=w: h.tensor_scalar(
                out=zb.ap()[:, 0:ncols], in0=w.ap()[:, 0:ncols],
                scalar1=-0.5, scalar2=None, op0=ALU.is_le),
                reads=[w], writes=[zb])
            s.op("dv", lambda h: h.tensor_tensor(
                out=wq.ap()[:, 0:ncols], in0=za.ap()[:, 0:ncols],
                in1=zb.ap()[:, 0:ncols], op=ALU.subtract),
                reads=[za, zb], writes=[wq])
            s.op("sy", lambda h, t=t: h.dma_start(
                out=dst.ap()[t * 128:(t + 1) * 128, :], in_=wq.ap()[:, 0:ncols]),
                reads=[wq], writes=[dst], dma=True)

    p.gscr_buf = Buf(gscr.ap(), "gscr")
    quant(0, "L0_ffn_W1", DFF, DM, wq1h)
    quant(1, "L0_ffn_W2", DM, DFF, wq2h)

    x1t = [p.sbuf(f"f_x1t{i}", [128, 1024], BF16) for i in range(2)]
    xnt = [p.sbuf(f"f_xnt{i}", [128, 1024], BF16) for i in range(2)]
    for i in range(NT):
        xb, xn = x1t[i % 2], xnt[i % 2]
        s.op("sy", lambda h, i=i, xb=xb: h.dma_start(
            out=xb.ap(), in_=x1.ap()[i * 128:(i + 1) * 128, :]), writes=[xb], dma=True)
        ln0_rows(p, xb, xb.ap(), st2, 0, scrap)
        s.op("dv", lambda h, xb=xb, xn=xn: h.tensor_scalar(
            out=xn.ap(), in0=xb.ap(), scalar1=st2.ap()[:, 0:1],
            scalar2=st2.ap()[:, 1:2], op0=ALU.subtract, op1=ALU.mult),
            reads=[xb, st2], writes=[xn])
        s.op("sy", lambda h, i=i, xn=xn: h.dma_start(
            out=xn1h.ap()[i * 128:(i + 1) * 128, :], in_=xn.ap()),
            reads=[xn], writes=[xn1h], dma=True)
    s.fence_all()
    p.free_to(mark_f)

    mark_mm = p.mark()
    st2 = p.sbuf("f_st2b", [128, 2], F32)
    scrap = p.sbuf("f_scrapb", [128, 1024], F32)
    w1T = p.sbuf("f_w1T", [128, 8 * 4096], BF16)
    w2T = p.sbuf("f_w2T", [128, 32 * 1024], BF16)
    b1c = p.sbuf("f_b1c", [128, 32], F32)
    b2r = p.sbuf("f_b2r", [1, 1024], BF16)
    xn1g = p.sbuf("f_xn1g", [128, 8 * 512], BF16)
    h_sb = p.sbuf("f_h", [128, 32 * 512], BF16)
    x1s = p.sbuf("f_x1s", [128, 1024], BF16)
    x2s = p.sbuf("f_x2s", [128, 1024], BF16)
    sfb = p.sbuf("f_sfb", [128, 1024], F32)
    ph = [p.psum(f"f_ph{i}", [128, 512]) for i in range(2)]
    py = p.psum("f_py", [128, 1024])

    for c in range(8):
        s.op("sc", lambda h, c=c: h.dma_start_transpose(
            w1T.ap()[:, c * 4096:(c + 1) * 4096], wq1h.ap()[:, c * 128:(c + 1) * 128]),
            reads=[wq1h], writes=[w1T], dma=True)
    for c in range(32):
        s.op("sc", lambda h, c=c: h.dma_start_transpose(
            w2T.ap()[:, c * 1024:(c + 1) * 1024], wq2h.ap()[:, c * 128:(c + 1) * 128]),
            reads=[wq2h], writes=[w2T], dma=True)
    for c in range(32):
        s.op("sy", lambda h, c=c: h.dma_start(
            out=b1c.ap()[:, c:c + 1],
            in_=p.ext_in["L0_ffn_b1"].ap()[c * 128:(c + 1) * 128].unsqueeze(1)),
            writes=[b1c], dma=True)
    s.op("gp", lambda h: h.dma_start(out=b2r.ap(),
                                     in_=p.ext_in["L0_ffn_b2"].ap().unsqueeze(0)),
         writes=[b2r], dma=True)

    for g in range(8):
        for d in range(8):
            s.op("sc", lambda h, g=g, d=d: h.dma_start_transpose(
                xn1g.ap()[:, d * 512:(d + 1) * 512],
                xn1h.ap()[g * 512:(g + 1) * 512, d * 128:(d + 1) * 128]),
                reads=[xn1h], writes=[xn1g], dma=True)
        for fc in range(32):
            pp = ph[fc % 2]

            def f1grp(h, fc=fc, pp=pp):
                for d in range(8):
                    ins = h.matmul(
                        pp.ap(), w1T.ap()[:, d * 4096 + fc * 128:d * 4096 + (fc + 1) * 128],
                        xn1g.ap()[:, d * 512:(d + 1) * 512],
                        start=(d == 0), stop=(d == 7))
                return ins
            s.op("te", f1grp, reads=[w1T, xn1g], writes=[pp])
            s.op("sc", lambda h, fc=fc, pp=pp: h.activation(
                out=h_sb.ap()[:, fc * 512:(fc + 1) * 512], in_=pp.ap(),
                func=AF.Relu, bias=b1c.ap()[:, fc:fc + 1]),
                reads=[pp, b1c], writes=[h_sb])
        for ts in range(4):
            i = g * 4 + ts

            def f2grp(h, ts=ts):
                for fc in range(32):
                    for dh in range(2):
                        ins = h.matmul(
                            py.ap()[:, dh * 512:(dh + 1) * 512],
                            h_sb.ap()[:, fc * 512 + ts * 128:fc * 512 + (ts + 1) * 128],
                            w2T.ap()[:, fc * 1024 + dh * 512:fc * 1024 + (dh + 1) * 512],
                            start=(fc == 0), stop=False)
                for dh in range(2):
                    ins = h.matmul(
                        py.ap()[:, dh * 512:(dh + 1) * 512], onesb.ap()[0:1, 0:128],
                        b2r.ap()[0:1, dh * 512:(dh + 1) * 512],
                        start=False, stop=(dh == 1))
                return ins
            s.op("te", f2grp, reads=[h_sb, w2T, onesb, b2r], writes=[py])
            s.op("sy", lambda h, i=i: h.dma_start(
                out=x1s.ap(), in_=x1.ap()[i * 128:(i + 1) * 128, :]),
                writes=[x1s], dma=True)
            ln0_rows(p, py, py.ap(), st2, 0, scrap)
            s.op("dv", lambda h: h.tensor_scalar(
                out=sfb.ap(), in0=py.ap(), scalar1=st2.ap()[:, 0:1],
                scalar2=st2.ap()[:, 1:2], op0=ALU.subtract, op1=ALU.mult),
                reads=[py, st2], writes=[sfb])
            s.op("dv", lambda h: h.tensor_tensor(
                out=x2s.ap(), in0=sfb.ap(), in1=x1s.ap(), op=ALU.add),
                reads=[sfb, x1s], writes=[x2s])
            s.op("sy", lambda h, i=i: h.dma_start(
                out=x2_hbm.ap()[i * 128:(i + 1) * 128, :], in_=x2s.ap()),
                reads=[x2s], writes=[x2_hbm], dma=True)
    s.fence_all()
    p.free_to(mark_phase)


def build_prog_f():
    p = Prog("f")
    make_consts(p)
    x1 = p.inp("x1", [L, DM], BF16)
    for nm, shp in (("L0_ffn_W1", [DFF, DM]), ("L0_ffn_b1", [DFF]),
                    ("L0_ffn_W2", [DM, DFF]), ("L0_ffn_b2", [DM])):
        p.inp(nm, shp, F32)
    x2 = p.out("x2", [L, DM], BF16)
    emit_ffn(p, x1, x2)
    return p





def blocks_of(parity):
    A, Bl = [], []
    for k in range(NBLK // 4):
        if parity == 0:
            A.append(4 * k)
            Bl.append(4 * k + 3)
        else:
            A.append(4 * k + 1)
            Bl.append(4 * k + 2)
    return A, Bl


def emit_attn(p, x2, x3_hbm):
    nc, s = p.nc, p.s
    mark_phase = p.mark()
    NT = L // 128
    NQT = QOWN // 128
    ident, onesb = p.ident, p.onesb
    NHH = 65

    xn2h = p.scratch("a_xn2h", [L, DM], BF16)
    xo2h = p.scratch("a_xo2h", [QOWN, DM], BF16)
    wqb = p.scratch("a_wqb", [DM, DM], BF16)
    wkvb = p.scratch("a_wkvb", [512, DM], BF16)
    wob = p.scratch("a_wob", [DM, DM], BF16)
    qph = p.scratch("a_qph", [QOWN, DM], BF16)
    kph = p.scratch("a_kph", [L, 256], BF16)
    y1h = p.scratch("a_y1h", [NH * NHH, QOWN], BF16)
    y2h = p.scratch("a_y2h", [NH * NHH, QOWN], BF16)
    ynh = p.scratch("a_ynh", [QOWN, DM], BF16)
    lamh = p.scratch("a_lamh", [1, NH], F32)

    cast_via_hbm(p, p.ext_in["L1_attn_Wq"].ap(), wqb, DM, DM, None, "wq")
    cast_via_hbm(p, p.ext_in["L1_attn_Wkv"].ap(), wkvb, 512, DM, None, "wkv")
    cast_via_hbm(p, p.ext_in["L1_attn_Wout"].ap(), wob, DM, DM, None, "wo")

    st2 = p.sbuf("a_st2", [128, 2], F32)
    scrap = p.sbuf("a_scrap", [128, 1024], F32)
    lam_b = p.sbuf("a_lamb", [128, NH], F32)
    rmsw = p.sbuf("a_rmsw", [128, HD], F32)
    flg = p.sbuf("a_flg", [128, 2], F32)

    s.op("sy", lambda h: h.dma_start(
        out=lam_b.ap()[0:1, :], in_=p.ext_in["L1_attn_lambda_init"].ap().unsqueeze(0)),
        writes=[lam_b], dma=True)
    s.op("sc", lambda h: h.activation(out=lam_b.ap()[0:1, :], in_=lam_b.ap()[0:1, :],
                                      func=AF.Sigmoid), reads=[lam_b], writes=[lam_b])
    s.op("sy", lambda h: h.dma_start(out=lamh.ap(), in_=lam_b.ap()[0:1, :]),
         reads=[lam_b], writes=[lamh], dma=True)
    s.op("sy", lambda h: h.dma_start(
        out=lam_b.ap(), in_=lamh.ap()[0:1, :].partition_broadcast(128)),
        reads=[lamh], writes=[lam_b], dma=True)
    s.op("sy", lambda h: h.dma_start(
        out=rmsw.ap(),
        in_=p.ext_in["L1_attn_rms_weight"].ap().unsqueeze(0).partition_broadcast(128)),
        writes=[rmsw], dma=True)
    s.op("sy", lambda h: h.dma_start(out=flg.ap(), in_=p.ext_in["a_flags"].ap()),
         writes=[flg], dma=True)

    mark_xn = p.mark()
    x2t = [p.sbuf(f"a_x2t{i}", [128, 1024], BF16) for i in range(2)]
    xnt = [p.sbuf(f"a_xnt{i}", [128, 1024], BF16) for i in range(2)]
    for i in range(NT):
        xb, xn = x2t[i % 2], xnt[i % 2]
        s.op("sy", lambda h, i=i, xb=xb: h.dma_start(
            out=xb.ap(), in_=x2.ap()[i * 128:(i + 1) * 128, :]), writes=[xb], dma=True)
        ln0_rows(p, xb, xb.ap(), st2, 0, scrap)
        s.op("dv", lambda h, xb=xb, xn=xn: h.tensor_scalar(
            out=xn.ap(), in0=xb.ap(), scalar1=st2.ap()[:, 0:1],
            scalar2=st2.ap()[:, 1:2], op0=ALU.subtract, op1=ALU.mult),
            reads=[xb, st2], writes=[xn])
        s.op("sy", lambda h, i=i, xn=xn: h.dma_start(
            out=xn2h.ap()[i * 128:(i + 1) * 128, :], in_=xn.ap()),
            reads=[xn], writes=[xn2h], dma=True)
    s.fence_all()
    p.free_to(mark_xn)

    A0, B0 = blocks_of(0)
    A1, B1 = blocks_of(1)
    ga = p.sbuf("a_ga", [128, 16 * 1024], BF16)
    gbv = p.sbuf("a_gb", [128, 16 * 1024], BF16)
    for (pats, dst) in (((A0, B0), ga), ((A1, B1), gbv)):
        s1l, s2l = pats
        s.op("sy", lambda h, s1l=s1l, dst=dst: h.dma_start(
            out=dst.ap().rearrange("p (m two d) -> p m two d", two=2, d=1024)[:, :, 0, :],
            in_=xn2h.ap().rearrange("(m g p) d -> p m g d", p=128, g=4)[:, :, s1l[0] % 4, :]),
            reads=[xn2h], writes=[dst], dma=True)
        s.op("sy", lambda h, s2l=s2l, dst=dst: h.dma_start(
            out=dst.ap().rearrange("p (m two d) -> p m two d", two=2, d=1024)[:, :, 1, :],
            in_=xn2h.ap().rearrange("(m g p) d -> p m g d", p=128, g=4)[:, :, s2l[0] % 4, :]),
            reads=[xn2h], writes=[dst], dma=True)
    xo = p.sbuf("a_xo", [128, 16 * 1024], BF16)
    s.op("dv", lambda h: h.tensor_scalar_mul(out=xo.ap(), in0=ga.ap(),
                                             scalar1=flg.ap()[:, 0:1]),
         reads=[ga, flg], writes=[xo])
    s.op("dv", lambda h: h.scalar_tensor_tensor(
        out=xo.ap(), in0=gbv.ap(), scalar=flg.ap()[:, 1:2], in1=xo.ap(),
        op0=ALU.mult, op1=ALU.add), reads=[gbv, flg, xo], writes=[xo])
    for i in range(NQT):
        s.op("sy", lambda h, i=i: h.dma_start(
            out=xo2h.ap()[i * 128:(i + 1) * 128, :],
            in_=xo.ap()[:, i * 1024:(i + 1) * 1024]),
            reads=[xo], writes=[xo2h], dma=True)
    s.fence_all()
    p.free_to(mark_xn)

    va = p.sbuf("a_va", [128, NKV * 32 * NHH], BF16)
    mark_pr = p.mark()
    xn2T = p.sbuf("a_xn2T", [128, 8 * 4096], BF16)
    xooT = p.sbuf("a_xooT", [128, 8 * 2048], BF16)
    wqT = p.sbuf("a_wqT", [128, 8 * 1024], BF16)
    wkvT = p.sbuf("a_wkvT", [128, 8 * 512], BF16)
    qcs = [p.sbuf(f"a_qcs{i}", [128, 1024], BF16) for i in range(2)]
    qsn = [p.sbuf(f"a_qsn{i}", [128, 1024], BF16) for i in range(2)]
    rr = [p.sbuf(f"a_rr{i}", [128, 1024], BF16) for i in range(4)]
    pq = p.psum("a_pq", [128, 1024])
    for c in range(8):
        s.op("sc", lambda h, c=c: h.dma_start_transpose(
            xn2T.ap()[:, c * 4096:(c + 1) * 4096], xn2h.ap()[:, c * 128:(c + 1) * 128]),
            reads=[xn2h], writes=[xn2T], dma=True)
        s.op("sc", lambda h, c=c: h.dma_start_transpose(
            xooT.ap()[:, c * 2048:(c + 1) * 2048], xo2h.ap()[:, c * 128:(c + 1) * 128]),
            reads=[xo2h], writes=[xooT], dma=True)
        s.op("sc", lambda h, c=c: h.dma_start_transpose(
            wqT.ap()[:, c * 1024:(c + 1) * 1024], wqb.ap()[:, c * 128:(c + 1) * 128]),
            reads=[wqb], writes=[wqT], dma=True)
        s.op("sc", lambda h, c=c: h.dma_start_transpose(
            wkvT.ap()[:, c * 512:(c + 1) * 512], wkvb.ap()[:, c * 128:(c + 1) * 128]),
            reads=[wkvb], writes=[wkvT], dma=True)
    for kh in range(NKV):
        for t in range(32):
            s.op("dv", lambda h, kh=kh, t=t: h.memset(
                va.ap()[:, (kh * 32 + t) * NHH + HD:(kh * 32 + t) * NHH + NHH], 1.0),
                writes=[va])

    def rope(src_psum, cosb, sinb, width, outb):
        r0, r1, r2 = rr[0], rr[1], rr[2]
        s.op("dv", lambda h: h.tensor_copy(out=r0.ap()[:, 0:width], in_=src_psum.ap()[:, 0:width]),
             reads=[src_psum], writes=[r0])
        s.op("dv", lambda h: h.tensor_copy(
            out=r1.ap()[:, 0:width].rearrange("p (a two) -> p a two", two=2)[:, :, 0],
            in_=r0.ap()[:, 0:width].rearrange("p (a two) -> p a two", two=2)[:, :, 1]),
            reads=[r0], writes=[r1])
        s.op("dv", lambda h: h.tensor_copy(
            out=r1.ap()[:, 0:width].rearrange("p (a two) -> p a two", two=2)[:, :, 1],
            in_=r0.ap()[:, 0:width].rearrange("p (a two) -> p a two", two=2)[:, :, 0]),
            reads=[r0], writes=[r1])
        s.op("dv", lambda h: h.tensor_tensor(out=r2.ap()[:, 0:width],
                                             in0=r0.ap()[:, 0:width],
                                             in1=cosb.ap()[:, 0:width], op=ALU.mult),
             reads=[r0, cosb], writes=[r2])
        s.op("dv", lambda h: h.tensor_tensor(out=r1.ap()[:, 0:width],
                                             in0=r1.ap()[:, 0:width],
                                             in1=sinb.ap()[:, 0:width], op=ALU.mult),
             reads=[r1, sinb], writes=[r1])
        s.op("dv", lambda h: h.tensor_tensor(out=outb.ap()[:, 0:width],
                                             in0=r2.ap()[:, 0:width],
                                             in1=r1.ap()[:, 0:width], op=ALU.add),
             reads=[r2, r1], writes=[outb])

    for i in range(NQT):
        def qgrp(h, i=i):
            for dh in range(2):
                for d in range(8):
                    ins = h.matmul(
                        pq.ap()[:, dh * 512:(dh + 1) * 512],
                        xooT.ap()[:, d * 2048 + i * 128:d * 2048 + (i + 1) * 128],
                        wqT.ap()[:, d * 1024 + dh * 512:d * 1024 + (dh + 1) * 512],
                        start=(d == 0), stop=(d == 7))
            return ins
        s.op("te", qgrp, reads=[xooT, wqT], writes=[pq])
        cq, sq = qcs[i % 2], qsn[i % 2]
        s.op("sy", lambda h, i=i, cq=cq: h.dma_start(
            out=cq.ap(), in_=p.ext_in["a_qcos"].ap()[i * 128:(i + 1) * 128, :]),
            writes=[cq], dma=True)
        s.op("sy", lambda h, i=i, sq=sq: h.dma_start(
            out=sq.ap(), in_=p.ext_in["a_qsin"].ap()[i * 128:(i + 1) * 128, :]),
            writes=[sq], dma=True)
        rope(pq, cq, sq, 1024, rr[3])
        s.op("sy", lambda h, i=i: h.dma_start(
            out=qph.ap()[i * 128:(i + 1) * 128, :], in_=rr[3].ap()),
            reads=[rr[3]], writes=[qph], dma=True)
    for i in range(NT):
        def kvgrp(h, i=i):
            for d in range(8):
                ins = h.matmul(
                    pq.ap()[:, 0:512],
                    xn2T.ap()[:, d * 4096 + i * 128:d * 4096 + (i + 1) * 128],
                    wkvT.ap()[:, d * 512:(d + 1) * 512],
                    start=(d == 0), stop=(d == 7))
            return ins
        s.op("te", kvgrp, reads=[xn2T, wkvT], writes=[pq])
        cq, sq = qcs[i % 2], qsn[i % 2]
        s.op("sy", lambda h, i=i, cq=cq: h.dma_start(
            out=cq.ap()[:, 0:256], in_=p.ext_in["a_kcos"].ap()[i * 128:(i + 1) * 128, :]),
            writes=[cq], dma=True)
        s.op("sy", lambda h, i=i, sq=sq: h.dma_start(
            out=sq.ap()[:, 0:256], in_=p.ext_in["a_ksin"].ap()[i * 128:(i + 1) * 128, :]),
            writes=[sq], dma=True)
        rope(pq, cq, sq, 256, rr[3])
        s.op("sy", lambda h, i=i: h.dma_start(
            out=kph.ap()[i * 128:(i + 1) * 128, :], in_=rr[3].ap()[:, 0:256]),
            reads=[rr[3]], writes=[kph], dma=True)
        for kh in range(NKV):
            s.op("dv", lambda h, i=i, kh=kh: h.tensor_copy(
                out=va.ap()[:, (kh * 32 + i) * NHH:(kh * 32 + i) * NHH + HD],
                in_=pq.ap()[:, 256 + kh * HD:256 + (kh + 1) * HD]),
                reads=[pq], writes=[va])
    s.fence_all()
    p.free_to(mark_pr)

    mark_sc = p.mark()
    va2 = p.sbuf("a_va2", [128, NKV * 32 * NHH], BF16)
    p.free_to(mark_sc)

    qT = p.sbuf("a_qT", [128, NH * 2048], BF16)
    kT = p.sbuf("a_kT", [128, NKV * 4096], BF16)
    msk = p.sbuf("a_msk", [128, 32 * 512], BF16)
    tmpT = p.sbuf("a_tmpT", [128, 4096], BF16)
    e_sb = [p.sbuf(f"a_esb{i}", [128, 1024], BF16) for i in range(2)]
    yev = p.sbuf("a_yev", [65, 512], BF16)
    psT = [p.psum(f"a_psT{i}", [128, 1024]) for i in range(2)]
    py1 = p.psum("a_py1", [65, 512])
    py2 = p.psum("a_py2", [65, 512])

    s.op("sy", lambda h: h.dma_start(
        out=msk.ap().rearrange("p (g w q) -> p g w q", g=4, w=8),
        in_=p.ext_in["a_mask"].ap().rearrange("g w p q -> p g w q")),
        writes=[msk], dma=True)

    for c in range(8):
        s.op("sc", lambda h, c=c: h.dma_start_transpose(
            tmpT.ap()[:, 0:2048], qph.ap()[:, c * 128:(c + 1) * 128]),
            reads=[qph], writes=[tmpT], dma=True)
        for j in range(4):
            hh = c * 2 + j // 2
            half = j % 2
            s.op("gp", lambda h, hh=hh, half=half, j=j: h.dma_start(
                out=qT.ap()[half * 64:half * 64 + 32, hh * 2048:(hh + 1) * 2048],
                in_=tmpT.ap()[j * 32:(j + 1) * 32, 0:2048]),
                reads=[tmpT], writes=[qT], dma=True)
    for c in range(2):
        s.op("sc", lambda h, c=c: h.dma_start_transpose(
            tmpT.ap()[:, 0:4096], kph.ap()[:, c * 128:(c + 1) * 128]),
            reads=[kph], writes=[tmpT], dma=True)
        for j in range(4):
            kh = c * 2 + j // 2
            half = j % 2
            s.op("gp", lambda h, c=c, j=j, kh=kh, half=half: h.dma_start(
                out=kT.ap()[half * 64:half * 64 + 32, kh * 4096:(kh + 1) * 4096],
                in_=tmpT.ap()[j * 32:(j + 1) * 32, 0:4096]),
                reads=[tmpT], writes=[kT], dma=True)

    SCL = 1.0 / (HF ** 0.5)
    for kh in range(NKV):
        for hq in range(4):
            hh = kh * 4 + hq
            for gq in range(4):
                E = 8 * gq + 8
                for sm in range(2):
                    base = sm * 64
                    pyy_ = py1 if sm == 0 else py2
                    NTG = (E + 1) // 2

                    def emit_sgrp(tg):
                        nt = min(2, E - tg * 2)
                        pst = psT[tg % 2]
                        esb = e_sb[tg % 2]

                        def sgrp(h, kh=kh, hh=hh, gq=gq, tg=tg, nt=nt, base=base, pst=pst):
                            for tt_ in range(nt):
                                t = tg * 2 + tt_
                                ins = h.matmul(
                                    pst.ap()[:, tt_ * 512:(tt_ + 1) * 512],
                                    kT.ap()[base:base + 32,
                                            kh * 4096 + t * 128:kh * 4096 + (t + 1) * 128],
                                    qT.ap()[base:base + 32,
                                            hh * 2048 + gq * 512:hh * 2048 + (gq + 1) * 512],
                                    start=True, stop=True)
                            return ins
                        s.op("te", sgrp, reads=[kT, qT], writes=[pst])
                        s.op("sc", lambda h, nt=nt, pst=pst, esb=esb: h.activation(
                            out=esb.ap()[:, 0:nt * 512], in_=pst.ap()[:, 0:nt * 512],
                            func=AF.Exp, scale=SCL), reads=[pst], writes=[esb])
                        for tt_ in range(nt):
                            t = tg * 2 + tt_
                            if t >= E - 8:
                                w = t - (E - 8)
                                s.op("dv", lambda h, gq=gq, w=w, tt_=tt_, esb=esb: h.tensor_tensor(
                                    out=esb.ap()[:, tt_ * 512:(tt_ + 1) * 512],
                                    in0=esb.ap()[:, tt_ * 512:(tt_ + 1) * 512],
                                    in1=msk.ap()[:, (gq * 8 + w) * 512:(gq * 8 + w + 1) * 512],
                                    op=ALU.mult), reads=[esb, msk], writes=[esb])

                    def emit_pv(tg):
                        nt = min(2, E - tg * 2)
                        esb = e_sb[tg % 2]

                        def pvgrp(h, kh=kh, tg=tg, nt=nt, pyy_=pyy_, E=E, esb=esb):
                            for tt_ in range(nt):
                                t = tg * 2 + tt_
                                ins = h.matmul(
                                    pyy_.ap(),
                                    va.ap()[:, (kh * 32 + t) * NHH:(kh * 32 + t + 1) * NHH],
                                    esb.ap()[:, tt_ * 512:(tt_ + 1) * 512],
                                    start=(t == 0), stop=(t == E - 1))
                            return ins
                        s.op("te", pvgrp, reads=[va, esb], writes=[pyy_])

                    emit_sgrp(0)
                    for tg in range(1, NTG):
                        emit_sgrp(tg)
                        emit_pv(tg - 1)
                    emit_pv(NTG - 1)
                    s.op("dv", lambda h, pyy_=pyy_: h.tensor_copy(
                        out=yev.ap(), in_=pyy_.ap()), reads=[pyy_], writes=[yev])
                    dsth = y1h if sm == 0 else y2h
                    s.op("sy", lambda h, hh=hh, gq=gq, dsth=dsth: h.dma_start(
                        out=dsth.ap()[hh * NHH:(hh + 1) * NHH, gq * 512:(gq + 1) * 512],
                        in_=yev.ap()), reads=[yev], writes=[dsth], dma=True)
    s.fence_all()
    p.free_to(mark_sc)

    mark_a7 = p.mark()
    woT = p.sbuf("a_woT", [128, 8 * 1024], BF16)
    y1t = p.sbuf("a_y1t", [128, NH * NHH], BF16)
    y2t = p.sbuf("a_y2t", [128, NH * NHH], BF16)
    cb = p.sbuf("a_cb", [128, 4 * NH], F32)
    dbig = p.sbuf("a_dbig", [128, DM], F32)
    tbig = p.sbuf("a_tbig", [128, DM], F32)
    rmswt = p.sbuf("a_rmswt", [128, DM], F32)
    ynb = p.sbuf("a_ynb", [128, DM], BF16)
    ynT = p.sbuf("a_ynT", [128, 8 * 2048], BF16)
    x2a = p.sbuf("a_x2a", [128, 1024], BF16)
    x2b = p.sbuf("a_x2b", [128, 1024], BF16)
    sfb = p.sbuf("a_sfb", [128, 1024], F32)
    x3t = p.sbuf("a_x3t", [128, 1024], BF16)
    pat = p.psum("a_pat", [128, 1024])
    st2b = p.sbuf("a_st2b", [128, 2], F32)

    for c in range(8):
        s.op("sc", lambda h, c=c: h.dma_start_transpose(
            woT.ap()[:, c * 1024:(c + 1) * 1024], wob.ap()[:, c * 128:(c + 1) * 128]),
            reads=[wob], writes=[woT], dma=True)

    A0l, B0l = blocks_of(0)
    A1l, B1l = blocks_of(1)
    own0 = sorted(A0l + B0l)
    own1 = sorted(A1l + B1l)
    s.op("dv", lambda h: h.tensor_copy(
        out=rmswt.ap().rearrange("p (h u) -> p h u", u=HD),
        in_=rmsw.ap().unsqueeze(1).rearrange("p o u -> p o u").broadcast_to(
            [128, NH, HD]) if False else
        rmsw.ap().unsqueeze(1).broadcast_to([128, NH, HD])),
        reads=[rmsw], writes=[rmswt])
    for i in range(NQT):
        s.op("sc", lambda h, i=i: h.dma_start_transpose(
            y1t.ap(), y1h.ap()[:, i * 128:(i + 1) * 128]),
            reads=[y1h], writes=[y1t], dma=True)
        s.op("sc", lambda h, i=i: h.dma_start_transpose(
            y2t.ap(), y2h.ap()[:, i * 128:(i + 1) * 128]),
            reads=[y2h], writes=[y2t], dma=True)
        y1v = y1t.ap().rearrange("p (h u) -> p h u", u=NHH)
        y2v = y2t.ap().rearrange("p (h u) -> p h u", u=NHH)
        s.op("dv", lambda h: h.reciprocal(out=cb.ap()[:, 0:NH],
                                          in_=y1v[:, :, HD]),
             reads=[y1t], writes=[cb])
        s.op("dv", lambda h: h.reciprocal(out=cb.ap()[:, NH:2 * NH],
                                          in_=y2v[:, :, HD]),
             reads=[y2t], writes=[cb])
        s.op("dv", lambda h: h.tensor_tensor(
            out=cb.ap()[:, NH:2 * NH], in0=cb.ap()[:, NH:2 * NH],
            in1=lam_b.ap()[:, 0:NH], op=ALU.mult),
            reads=[cb, lam_b], writes=[cb])
        c1b = cb.ap()[:, 0:NH].unsqueeze(2).broadcast_to([128, NH, HD])
        c2b = cb.ap()[:, NH:2 * NH].unsqueeze(2).broadcast_to([128, NH, HD])
        db3 = dbig.ap().rearrange("p (h u) -> p h u", u=HD)
        tb3 = tbig.ap().rearrange("p (h u) -> p h u", u=HD)
        s.op("dv", lambda h: h.tensor_tensor(
            out=tb3, in0=y2v[:, :, 0:HD], in1=c2b, op=ALU.mult),
            reads=[y2t, cb], writes=[tbig])
        s.op("dv", lambda h: h.tensor_tensor(
            out=db3, in0=y1v[:, :, 0:HD], in1=c1b, op=ALU.mult),
            reads=[y1t, cb], writes=[dbig])
        s.op("dv", lambda h: h.tensor_tensor(
            out=dbig.ap(), in0=dbig.ap(), in1=tbig.ap(), op=ALU.subtract),
            reads=[dbig, tbig], writes=[dbig])
        s.op("sc", lambda h: h.activation(out=tbig.ap(), in_=dbig.ap(),
                                          func=AF.Square),
             reads=[dbig], writes=[tbig])
        s.op("dv", lambda h: h.tensor_reduce(
            out=cb.ap()[:, 2 * NH:3 * NH], in_=tb3, axis=AX.X, op=ALU.add),
            reads=[tbig], writes=[cb])
        s.op("sc", lambda h: h.activation(
            out=cb.ap()[:, 3 * NH:4 * NH], in_=cb.ap()[:, 2 * NH:3 * NH],
            func=AF.Sqrt, scale=1.0 / HD, bias=EPS6), reads=[cb], writes=[cb])
        s.op("dv", lambda h: h.reciprocal(
            out=cb.ap()[:, 3 * NH:4 * NH], in_=cb.ap()[:, 3 * NH:4 * NH]),
            reads=[cb], writes=[cb])
        rsb = cb.ap()[:, 3 * NH:4 * NH].unsqueeze(2).broadcast_to([128, NH, HD])
        s.op("dv", lambda h: h.tensor_tensor(
            out=db3, in0=db3, in1=rsb, op=ALU.mult), reads=[dbig, cb], writes=[dbig])
        s.op("dv", lambda h: h.tensor_tensor(
            out=ynb.ap(), in0=dbig.ap(), in1=rmswt.ap(), op=ALU.mult),
            reads=[dbig, rmswt], writes=[ynb])
        s.op("sy", lambda h, i=i: h.dma_start(
            out=ynh.ap()[i * 128:(i + 1) * 128, :], in_=ynb.ap()),
            reads=[ynb], writes=[ynh], dma=True)
    for c in range(8):
        s.op("sc", lambda h, c=c: h.dma_start_transpose(
            ynT.ap()[:, c * 2048:(c + 1) * 2048], ynh.ap()[:, c * 128:(c + 1) * 128]),
            reads=[ynh], writes=[ynT], dma=True)
    for i in range(NQT):
        def ogrp(h, i=i):
            for dh in range(2):
                for d in range(8):
                    ins = h.matmul(
                        pat.ap()[:, dh * 512:(dh + 1) * 512],
                        ynT.ap()[:, d * 2048 + i * 128:d * 2048 + (i + 1) * 128],
                        woT.ap()[:, d * 1024 + dh * 512:d * 1024 + (dh + 1) * 512],
                        start=(d == 0), stop=(d == 7))
            return ins
        s.op("te", ogrp, reads=[ynT, woT], writes=[pat])
        blkA = own0[i]
        blkB = own1[i]
        s.op("sy", lambda h, blkA=blkA: h.dma_start(
            out=x2a.ap(), in_=x2.ap()[blkA * 128:(blkA + 1) * 128, :]),
            writes=[x2a], dma=True)
        s.op("sy", lambda h, blkB=blkB: h.dma_start(
            out=x2b.ap(), in_=x2.ap()[blkB * 128:(blkB + 1) * 128, :]),
            writes=[x2b], dma=True)
        s.op("dv", lambda h: h.tensor_scalar_mul(
            out=x2a.ap(), in0=x2a.ap(), scalar1=flg.ap()[:, 0:1]),
            reads=[x2a, flg], writes=[x2a])
        s.op("dv", lambda h: h.scalar_tensor_tensor(
            out=x2a.ap(), in0=x2b.ap(), scalar=flg.ap()[:, 1:2], in1=x2a.ap(),
            op0=ALU.mult, op1=ALU.add), reads=[x2b, flg, x2a], writes=[x2a])
        ln0_rows(p, pat, pat.ap(), st2b, 0, scrap)
        s.op("dv", lambda h: h.tensor_scalar(
            out=sfb.ap(), in0=pat.ap(), scalar1=st2b.ap()[:, 0:1],
            scalar2=st2b.ap()[:, 1:2], op0=ALU.subtract, op1=ALU.mult),
            reads=[pat, st2b], writes=[sfb])
        s.op("dv", lambda h: h.tensor_tensor(
            out=x3t.ap(), in0=sfb.ap(), in1=x2a.ap(), op=ALU.add),
            reads=[sfb, x2a], writes=[x3t])
        s.op("sy", lambda h, i=i: h.dma_start(
            out=x3_hbm.ap()[i * 128:(i + 1) * 128, :], in_=x3t.ap()),
            reads=[x3t], writes=[x3_hbm], dma=True)
    s.fence_all()
    p.free_to(mark_phase)


EPS6 = 1e-6


def build_prog_a():
    p = Prog("a")
    make_consts(p)
    x2 = p.inp("x2", [L, DM], BF16)
    for nm, shp in (("L1_attn_Wq", [DM, DM]), ("L1_attn_Wkv", [512, DM]),
                    ("L1_attn_Wout", [DM, DM]), ("L1_attn_lambda_init", [NH]),
                    ("L1_attn_rms_weight", [HD])):
        p.inp(nm, shp, F32)
    p.inp("a_qcos", [QOWN, DM], BF16)
    p.inp("a_qsin", [QOWN, DM], BF16)
    p.inp("a_kcos", [L, 256], BF16)
    p.inp("a_ksin", [L, 256], BF16)
    p.inp("a_mask", [4, 8, 128, 512], BF16)
    p.inp("a_flags", [128, 2], F32)
    x3 = p.out("x3", [QOWN, DM], BF16)
    emit_attn(p, x2, x3)
    return p


def host_attn_tables(parity):
    A, Bl = blocks_of(0) if parity == 0 else blocks_of(1)
    own = sorted(A + Bl)
    own_rows = np.concatenate([np.arange(b * 128, (b + 1) * 128) for b in own])
    inv = (1.0 / 10000.0 ** (np.arange(0, HF, 2, dtype=np.float32) / HF)).astype(np.float32)
    d = np.arange(DM)
    f = (d % 32) // 2
    par = d % 2
    ang = own_rows[:, None].astype(np.float32) * inv[None, f]
    qcos = np.cos(ang).astype(BF16NP)
    sg = np.where(par[None, :] == 0, -1.0, 1.0).astype(np.float32)
    qsin = (np.sin(ang) * sg).astype(BF16NP)
    dk = np.arange(256)
    fk = (dk % 32) // 2
    park = dk % 2
    tok = np.arange(L)
    angk = tok[:, None].astype(np.float32) * inv[None, fk]
    kcos = np.cos(angk).astype(BF16NP)
    sgk = np.where(park[None, :] == 0, -1.0, 1.0).astype(np.float32)
    ksin = (np.sin(angk) * sgk).astype(BF16NP)
    mask = np.zeros((4, 8, 128, 512), BF16NP)
    for gq in range(4):
        E = 8 * gq + 8
        for w in range(8):
            t = E - 8 + w
            kglob = t * 128 + np.arange(128)
            qglob = own_rows[gq * 512:(gq + 1) * 512]
            mask[gq, w] = (kglob[:, None] <= qglob[None, :]).astype(BF16NP)
    flags = np.zeros((128, 2), np.float32)
    flags[:, parity] = 1.0
    return {"a_qcos": qcos, "a_qsin": qsin, "a_kcos": kcos, "a_ksin": ksin,
            "a_mask": mask, "a_flags": flags}, own_rows





def emit_moe_prep(p):
    w1bh = p.scratch("e_w1bh", [NEXP * DFF, DM], BF16)
    w2bh = p.scratch("e_w2bh", [NEXP * DM, DFF], BF16)
    cast_via_hbm(p, p.ext_in["L1_ffn_W1"].ap().rearrange("e f d -> (e f) d"),
                 w1bh, NEXP * DFF, DM, None, "w1")
    cast_via_hbm(p, p.ext_in["L1_ffn_W2"].ap().rearrange("e d f -> (e d) f"),
                 w2bh, NEXP * DM, DFF, None, "w2")
    p.e_w1bh = w1bh
    p.e_w2bh = w2bh


def emit_moe(p, x3, out_own):
    nc, s = p.nc, p.s
    mark_phase = p.mark()
    NQT = QOWN // 128
    ident, onesb = p.ident, p.onesb
    w1bh, w2bh = p.e_w1bh, p.e_w2bh

    featsh = p.scratch("e_featsh", [QOWN, DM], BF16)
    featch = p.scratch("e_featch", [QOWN, DM], BF16)

    emask = p.sbuf("e_emask", [128, NQT * NEXP], F32)
    alph = p.sbuf("e_alph", [128, NQT], F32)
    mark_e1 = p.mark()
    st2 = p.sbuf("e_st2", [128, 2], F32)
    scrap = p.sbuf("e_scrap", [128, 1024], F32)
    x3t = [p.sbuf(f"e_x3t{i}", [128, 1024], BF16) for i in range(2)]
    ft32 = [p.sbuf(f"e_ft32{i}", [128, 1024], F32) for i in range(2)]
    fbf = p.sbuf("e_fbf", [128, 1024], BF16)
    cent = p.sbuf("e_cent", [16, 1024], F32)
    centT = p.sbuf("e_centT", [128, 8 * 16], F32)
    fTf = p.sbuf("e_fTf", [128, 8 * 128], F32)
    scoT = p.sbuf("e_scoT", [16, QOWN], F32)
    sco = p.sbuf("e_sco", [128, NQT * 8], F32)
    smax = p.sbuf("e_smax", [128, 8], F32)
    sidx = p.sbuf("e_sidx", [128, 8], mybir.dt.uint32)
    sidxf = p.sbuf("e_sidxf", [128, 1], F32)
    pt = p.psum("e_pt", [128, 128])
    psc = p.psum("e_psc", [16, 512])

    s.op("sy", lambda h: h.dma_start(out=cent.ap(),
                                     in_=p.ext_in["cent_pad"].ap()),
         writes=[cent], dma=True)
    for c in range(8):
        s.op("te", lambda h, c=c: h.transpose(
            pt.ap()[:, 0:16], cent.ap()[:, c * 128:(c + 1) * 128],
            p.identf.ap()[0:16, 0:16]),
            reads=[cent, p.identf], writes=[pt])
        s.op("dv", lambda h, c=c: h.tensor_copy(
            out=centT.ap()[:, c * 16:(c + 1) * 16], in_=pt.ap()[:, 0:16]),
            reads=[pt], writes=[centT])

    for i in range(NQT):
        xb, fb = x3t[i % 2], ft32[i % 2]
        s.op("sy", lambda h, i=i, xb=xb: h.dma_start(
            out=xb.ap(), in_=x3.ap()[i * 128:(i + 1) * 128, :]), writes=[xb], dma=True)
        ln0_rows(p, xb, xb.ap(), st2, 0, scrap)
        s.op("dv", lambda h, xb=xb, fb=fb: h.tensor_scalar(
            out=fb.ap(), in0=xb.ap(), scalar1=st2.ap()[:, 0:1],
            scalar2=st2.ap()[:, 1:2], op0=ALU.subtract, op1=ALU.mult),
            reads=[xb, st2], writes=[fb])
        s.op("dv", lambda h, fb=fb: h.tensor_copy(out=fbf.ap(), in_=fb.ap()),
             reads=[fb], writes=[fbf])
        s.op("sy", lambda h, i=i: h.dma_start(
            out=featsh.ap()[i * 128:(i + 1) * 128, :], in_=fbf.ap()),
            reads=[fbf], writes=[featsh], dma=True)
        ln0_rows(p, fb, fb.ap(), st2, 0, scrap)
        s.op("dv", lambda h, fb=fb: h.tensor_scalar(
            out=scrap.ap(), in0=fb.ap(), scalar1=st2.ap()[:, 0:1],
            scalar2=st2.ap()[:, 1:2], op0=ALU.subtract, op1=ALU.mult),
            reads=[fb, st2], writes=[scrap])
        s.op("dv", lambda h: h.tensor_copy(out=fbf.ap(), in_=scrap.ap()),
             reads=[scrap], writes=[fbf])
        s.op("sy", lambda h, i=i: h.dma_start(
            out=featch.ap()[i * 128:(i + 1) * 128, :], in_=fbf.ap()),
            reads=[fbf], writes=[featch], dma=True)
        for c in range(8):
            s.op("te", lambda h, c=c, fb=fb: h.transpose(
                pt.ap(), fb.ap()[:, c * 128:(c + 1) * 128], p.identf.ap()),
                reads=[fb, p.identf], writes=[pt])
            s.op("dv", lambda h, c=c: h.tensor_copy(
                out=fTf.ap()[:, c * 128:(c + 1) * 128], in_=pt.ap()),
                reads=[pt], writes=[fTf])
        for c in range(8):
            s.op("te", lambda h, c=c: h.matmul(
                psc.ap()[:, 0:128], centT.ap()[:, c * 16:(c + 1) * 16],
                fTf.ap()[:, c * 128:(c + 1) * 128],
                start=(c == 0), stop=(c == 7)), reads=[centT, fTf], writes=[psc])
        s.op("dv", lambda h, i=i: h.tensor_copy(
            out=scoT.ap()[:, i * 128:(i + 1) * 128], in_=psc.ap()[:, 0:128]),
            reads=[psc], writes=[scoT])
    for i in range(NQT):
        s.op("te", lambda h, i=i: h.transpose(
            pt.ap()[:, 0:16], scoT.ap()[:, i * 128:(i + 1) * 128],
            p.identf.ap()[0:16, 0:16]),
            reads=[scoT, p.identf], writes=[pt])
        s.op("dv", lambda h, i=i: h.tensor_copy(
            out=sco.ap()[:, i * 8:(i + 1) * 8], in_=pt.ap()[:, 0:8]),
            reads=[pt], writes=[sco])
        s.op("dv", lambda h, i=i: h.max(
            out=smax.ap(), in_=sco.ap()[:, i * 8:(i + 1) * 8]),
            reads=[sco], writes=[smax])
        s.op("dv", lambda h, i=i: h.max_index(
            out=sidx.ap(), in_max=smax.ap(), in_values=sco.ap()[:, i * 8:(i + 1) * 8]),
            reads=[smax, sco], writes=[sidx])
        s.op("sc", lambda h, i=i: h.activation(
            out=alph.ap()[:, i:i + 1], in_=smax.ap()[:, 0:1], func=AF.Sigmoid),
            reads=[smax], writes=[alph])
        s.op("dv", lambda h: h.tensor_copy(out=sidxf.ap(), in_=sidx.ap()[:, 0:1]),
             reads=[sidx], writes=[sidxf])
        for e in range(NEXP):
            s.op("dv", lambda h, i=i, e=e: h.tensor_scalar(
                out=emask.ap()[:, i * NEXP + e:i * NEXP + e + 1],
                in0=sidxf.ap(), scalar1=float(e), scalar2=None, op0=ALU.is_equal),
                reads=[sidxf], writes=[emask])
    s.fence_all()
    p.free_to(mark_e1)

    mark_e2 = p.mark()
    gcolv = p.sbuf("e_gcol", [128, 8 * NEXP], F32)
    bcolv = p.sbuf("e_bcol", [128, 8 * NEXP], F32)
    b1c = p.sbuf("e_b1c", [128, 32 * NEXP], F32)
    b2r = p.sbuf("e_b2r", [1, 1024], BF16)
    yacch = p.scratch("e_yacch", [QOWN, DM], BF16)
    yat = [p.sbuf(f"e_yat{i}", [128, 1024], BF16) for i in range(2)]
    mark_wts = p.mark()
    fcg = [p.sbuf(f"e_fcg{i}", [128, 512], BF16) for i in range(2)]
    w1T = p.sbuf("e_w1T", [128, 8 * 4096], BF16)
    w2T = p.sbuf("e_w2T", [128, 32 * 1024], BF16)
    subT = [p.sbuf(f"e_subT{i}", [128, 8 * 512], BF16) for i in range(2)]
    h_sb = p.sbuf("e_h", [128, 32 * 512], BF16)
    ph = [p.psum(f"e_ph{i}", [128, 512]) for i in range(2)]
    py = [p.psum(f"e_py{i}", [128, 1024]) for i in range(2)]

    for e in range(NEXP):
        for c in range(8):
            s.op("sy", lambda h, e=e, c=c: h.dma_start(
                out=gcolv.ap()[:, c * NEXP + e:c * NEXP + e + 1],
                in_=p.ext_in["L1_ffn_gamma"].ap()[e, c * 128:(c + 1) * 128].unsqueeze(1)),
                writes=[gcolv], dma=True)
            s.op("sy", lambda h, e=e, c=c: h.dma_start(
                out=bcolv.ap()[:, c * NEXP + e:c * NEXP + e + 1],
                in_=p.ext_in["L1_ffn_beta"].ap()[e, c * 128:(c + 1) * 128].unsqueeze(1)),
                writes=[bcolv], dma=True)
        for c in range(32):
            s.op("sy", lambda h, e=e, c=c: h.dma_start(
                out=b1c.ap()[:, c * NEXP + e:c * NEXP + e + 1],
                in_=p.ext_in["L1_ffn_b1"].ap()[e, c * 128:(c + 1) * 128].unsqueeze(1)),
                writes=[b1c], dma=True)

    for e in range(NEXP):
        for c in range(8):
            s.op("sc", lambda h, e=e, c=c: h.dma_start_transpose(
                w1T.ap()[:, c * 4096:(c + 1) * 4096],
                w1bh.ap()[e * DFF:(e + 1) * DFF, c * 128:(c + 1) * 128]),
                reads=[w1bh], writes=[w1T], dma=True)
        for c in range(32):
            s.op("sc", lambda h, e=e, c=c: h.dma_start_transpose(
                w2T.ap()[:, c * 1024:(c + 1) * 1024],
                w2bh.ap()[e * DM:(e + 1) * DM, c * 128:(c + 1) * 128]),
                reads=[w2bh], writes=[w2T], dma=True)
        s.op("gp", lambda h, e=e: h.dma_start(
            out=b2r.ap(), in_=p.ext_in["L1_ffn_b2"].ap()[e].unsqueeze(0)),
            writes=[b2r], dma=True)
        def emit_sub(g, sT):
            for d in range(8):
                fg = fcg[d % 2]
                s.op("sc", lambda h, g=g, d=d, fg=fg: h.dma_start_transpose(
                    fg.ap(), featch.ap()[g * 512:(g + 1) * 512, d * 128:(d + 1) * 128]),
                    reads=[featch], writes=[fg], dma=True)
                s.op("dv", lambda h, e=e, d=d, fg=fg, sT=sT: h.tensor_scalar(
                    out=sT.ap()[:, d * 512:(d + 1) * 512],
                    in0=fg.ap(),
                    scalar1=gcolv.ap()[:, d * NEXP + e:d * NEXP + e + 1],
                    scalar2=bcolv.ap()[:, d * NEXP + e:d * NEXP + e + 1],
                    op0=ALU.mult, op1=ALU.add),
                    reads=[fg, gcolv, bcolv], writes=[sT])

        emit_sub(0, subT[0])
        for g in range(4):
            sT = subT[g % 2]
            for fc in range(32):
                phh = ph[fc % 2]

                def w1grp(h, fc=fc, phh=phh, sT=sT):
                    for d in range(8):
                        ins = h.matmul(
                            phh.ap(),
                            w1T.ap()[:, d * 4096 + fc * 128:d * 4096 + (fc + 1) * 128],
                            sT.ap()[:, d * 512:(d + 1) * 512],
                            start=(d == 0), stop=(d == 7))
                    return ins
                s.op("te", w1grp, reads=[w1T, sT], writes=[phh])
                s.op("sc", lambda h, e=e, fc=fc, phh=phh: h.activation(
                    out=h_sb.ap()[:, fc * 512:(fc + 1) * 512], in_=phh.ap(),
                    func=AF.Relu,
                    bias=b1c.ap()[:, fc * NEXP + e:fc * NEXP + e + 1]),
                    reads=[phh, b1c], writes=[h_sb])
            if g < 3:
                emit_sub(g + 1, subT[(g + 1) % 2])
            for ts in range(4):
                i = g * 4 + ts
                pyg = py[ts % 2]

                def w2grp(h, ts=ts, pyg=pyg):
                    for fc in range(32):
                        for dh in range(2):
                            ins = h.matmul(
                                pyg.ap()[:, dh * 512:(dh + 1) * 512],
                                h_sb.ap()[:, fc * 512 + ts * 128:fc * 512 + (ts + 1) * 128],
                                w2T.ap()[:, fc * 1024 + dh * 512:fc * 1024 + (dh + 1) * 512],
                                start=(fc == 0), stop=False)
                    for dh in range(2):
                        ins = h.matmul(
                            pyg.ap()[:, dh * 512:(dh + 1) * 512],
                            onesb.ap()[0:1, 0:128],
                            b2r.ap()[0:1, dh * 512:(dh + 1) * 512],
                            start=False, stop=(dh == 1))
                    return ins
                s.op("te", w2grp, reads=[h_sb, w2T, onesb, b2r], writes=[pyg])
                ya = yat[ts % 2]
                if e == 0:
                    s.op("dv", lambda h, i=i, e=e, pyg=pyg, ya=ya: h.tensor_scalar_mul(
                        out=ya.ap(), in0=pyg.ap(),
                        scalar1=emask.ap()[:, i * NEXP + e:i * NEXP + e + 1]),
                        reads=[pyg, emask], writes=[ya])
                else:
                    s.op("sy", lambda h, i=i, ya=ya: h.dma_start(
                        out=ya.ap(), in_=yacch.ap()[i * 128:(i + 1) * 128, :]),
                        reads=[yacch], writes=[ya], dma=True)
                    s.op("dv", lambda h, i=i, e=e, pyg=pyg, ya=ya: h.scalar_tensor_tensor(
                        out=ya.ap(), in0=pyg.ap(),
                        scalar=emask.ap()[:, i * NEXP + e:i * NEXP + e + 1],
                        in1=ya.ap(), op0=ALU.mult, op1=ALU.add),
                        reads=[pyg, emask, ya], writes=[ya])
                s.op("sy", lambda h, i=i, ya=ya: h.dma_start(
                    out=yacch.ap()[i * 128:(i + 1) * 128, :], in_=ya.ap()),
                    reads=[ya], writes=[yacch], dma=True)

    s.fence_all()
    p.free_to(mark_wts)
    ftl = p.sbuf("e_ftl", [128, 1024], BF16)
    x3l = p.sbuf("e_x3l", [128, 1024], BF16)
    omf = p.sbuf("e_omf", [128, 1024], F32)
    st2c = p.sbuf("e_st2c", [128, 2], F32)
    scrapc = p.sbuf("e_scrapc", [128, 1024], F32)
    outb = p.sbuf("e_outb", [128, 1024], F32)
    for i in range(NQT):
        s.op("sy", lambda h, i=i: h.dma_start(
            out=ftl.ap(), in_=featsh.ap()[i * 128:(i + 1) * 128, :]),
            writes=[ftl], dma=True)
        s.op("sy", lambda h, i=i: h.dma_start(
            out=x3l.ap(), in_=x3.ap()[i * 128:(i + 1) * 128, :]),
            writes=[x3l], dma=True)
        ya = yat[i % 2]
        s.op("sy", lambda h, i=i, ya=ya: h.dma_start(
            out=ya.ap(), in_=yacch.ap()[i * 128:(i + 1) * 128, :]),
            reads=[yacch], writes=[ya], dma=True)
        s.op("dv", lambda h, i=i, ya=ya: h.scalar_tensor_tensor(
            out=omf.ap(), in0=ya.ap(),
            scalar=alph.ap()[:, i:i + 1], in1=ftl.ap(),
            op0=ALU.mult, op1=ALU.add), reads=[ya, alph, ftl], writes=[omf])
        ln0_rows(p, omf, omf.ap(), st2c, 0, scrapc)
        s.op("dv", lambda h: h.tensor_scalar(
            out=omf.ap(), in0=omf.ap(), scalar1=st2c.ap()[:, 0:1],
            scalar2=st2c.ap()[:, 1:2], op0=ALU.subtract, op1=ALU.mult),
            reads=[omf, st2c], writes=[omf])
        s.op("dv", lambda h: h.tensor_tensor(
            out=outb.ap(), in0=omf.ap(), in1=x3l.ap(), op=ALU.add),
            reads=[omf, x3l], writes=[outb])
        s.op("sy", lambda h, i=i: h.dma_start(
            out=out_own.ap()[i * 128:(i + 1) * 128, :], in_=outb.ap()),
            reads=[outb], writes=[out_own], dma=True)
    s.fence_all()
    p.free_to(mark_phase)


def build_prog_e():
    p = Prog("e")
    make_consts(p)
    x3 = p.inp("x3", [QOWN, DM], BF16)
    p.inp("cent_pad", [16, DM], F32)
    for nm, shp in (("L1_ffn_gamma", [NEXP, DM]), ("L1_ffn_beta", [NEXP, DM]),
                    ("L1_ffn_W1", [NEXP, DFF, DM]), ("L1_ffn_b1", [NEXP, DFF]),
                    ("L1_ffn_W2", [NEXP, DM, DFF]), ("L1_ffn_b2", [NEXP, DM])):
        p.inp(nm, shp, F32)
    out = p.out("out_own", [QOWN, DM], F32)
    emit_moe_prep(p)
    emit_moe(p, x3, out)
    return p




def build_prog_all():
    p = Prog("all")
    make_consts(p)
    x = p.inp("x", [L, DM], F32)
    for nm, shp in (("L0_attn_ln_gamma", [DM]), ("L0_attn_ln_beta", [DM]),
                    ("L0_attn_in_proj_w", [2 * DI, DM]),
                    ("L0_attn_conv_w", [DI, 4]), ("L0_attn_conv_b", [DI]),
                    ("xpw_pad", [128, DI]),
                    ("dtp_pad", [DI, 128]), ("L0_attn_dt_proj_b", [DI]),
                    ("L0_attn_A_log", [DI, NST]), ("L0_attn_D_param", [DI]),
                    ("L0_attn_out_proj_w", [DM, DI]),
                    ("L0_ffn_W1", [DFF, DM]), ("L0_ffn_b1", [DFF]),
                    ("L0_ffn_W2", [DM, DFF]), ("L0_ffn_b2", [DM]),
                    ("L1_attn_Wq", [DM, DM]), ("L1_attn_Wkv", [512, DM]),
                    ("L1_attn_Wout", [DM, DM]), ("L1_attn_lambda_init", [NH]),
                    ("L1_attn_rms_weight", [HD]),
                    ("cent_pad", [16, DM]),
                    ("L1_ffn_gamma", [NEXP, DM]), ("L1_ffn_beta", [NEXP, DM]),
                    ("L1_ffn_W1", [NEXP, DFF, DM]), ("L1_ffn_b1", [NEXP, DFF]),
                    ("L1_ffn_W2", [NEXP, DM, DFF]), ("L1_ffn_b2", [NEXP, DM])):
        p.inp(nm, shp, F32)
    p.inp("a_qcos", [QOWN, DM], BF16)
    p.inp("a_qsin", [QOWN, DM], BF16)
    p.inp("a_kcos", [L, 256], BF16)
    p.inp("a_ksin", [L, 256], BF16)
    p.inp("a_mask", [4, 8, 128, 512], BF16)
    p.inp("a_flags", [128, 2], F32)
    out = p.out("out_own", [QOWN, DM], F32)
    x1 = p.scratch("x1_s", [L, DM], BF16)
    x2 = p.scratch("x2_s", [L, DM], BF16)
    x3 = p.scratch("x3_s", [QOWN, DM], BF16)
    emit_moe_prep(p)
    emit_mamba(p, x, x1)
    emit_ffn(p, x1, x2)
    emit_attn(p, x2, x3)
    emit_moe(p, x3, out)
    return p


def kernel_fused(inputs):
    consts = const_arrays()
    xpw_pad = np.zeros((128, DI), np.float32)
    xpw_pad[:96] = inputs["L0_attn_x_proj_w"]
    dtp_pad = np.zeros((DI, 128), np.float32)
    dtp_pad[:, :64] = inputs["L0_attn_dt_proj_w"]
    cent_pad = np.zeros((16, DM), np.float32)
    cent_pad[:8] = inputs["L1_ffn_centroids"]
    tabs = [host_attn_tables(par) for par in (0, 1)]
    pp = build_prog_all()
    ncx = pp.finish()
    names = set(pp.in_shapes)
    maps = []
    for c in range(8):
        m = dict(consts)
        m["x"] = np.ascontiguousarray(np.asarray(inputs["L0_attn_x"])[c // 2])
        m["xpw_pad"] = xpw_pad
        m["dtp_pad"] = dtp_pad
        m["cent_pad"] = cent_pad
        m.update(tabs[c % 2][0])
        for nm in names:
            if nm not in m and nm in inputs:
                m[nm] = np.ascontiguousarray(np.asarray(inputs[nm]))
        maps.append(m)
    res = run_bass_kernel_spmd(ncx, maps, core_ids=list(range(NCORES)))
    out = np.empty((B, L, DM), np.float32)
    for c in range(8):
        out[c // 2][tabs[c % 2][1]] = res.results[c]["out_own"]
    return out

M_IN = ("L0_attn_ln_gamma", "L0_attn_ln_beta", "L0_attn_in_proj_w",
        "L0_attn_conv_w", "L0_attn_conv_b", "L0_attn_dt_proj_b",
        "L0_attn_A_log", "L0_attn_D_param", "L0_attn_out_proj_w")
F_IN = ("L0_ffn_W1", "L0_ffn_b1", "L0_ffn_W2", "L0_ffn_b2")
A_IN = ("L1_attn_Wq", "L1_attn_Wkv", "L1_attn_Wout", "L1_attn_lambda_init",
        "L1_attn_rms_weight")
E_IN = ("L1_ffn_gamma", "L1_ffn_beta", "L1_ffn_W1", "L1_ffn_b1",
        "L1_ffn_W2", "L1_ffn_b2")


def kernel(**inputs):
    return kernel_fused(inputs)


def kernel_unfused(**inputs):
    consts = const_arrays()
    xpw_pad = np.zeros((128, DI), np.float32)
    xpw_pad[:96] = inputs["L0_attn_x_proj_w"]
    dtp_pad = np.zeros((DI, 128), np.float32)
    dtp_pad[:, :64] = inputs["L0_attn_dt_proj_w"]
    cent_pad = np.zeros((16, DM), np.float32)
    cent_pad[:8] = inputs["L1_ffn_centroids"]
    tabs = [host_attn_tables(par) for par in (0, 1)]

    pm = build_prog_m()
    ncm = pm.finish()
    maps = []
    for c in range(8):
        m = dict(consts)
        m["x"] = np.ascontiguousarray(np.asarray(inputs["L0_attn_x"])[c // 2])
        m["xpw_pad"] = xpw_pad
        m["dtp_pad"] = dtp_pad
        for nm in M_IN:
            m[nm] = np.ascontiguousarray(np.asarray(inputs[nm]))
        maps.append(m)
    res = run_bass_kernel_spmd(ncm, maps, core_ids=list(range(NCORES)))
    x1s = [r["x1"] for r in res.results]

    pf = build_prog_f()
    ncf = pf.finish()
    maps = []
    for c in range(8):
        m = dict(consts)
        m["x1"] = x1s[c]
        for nm in F_IN:
            m[nm] = np.ascontiguousarray(np.asarray(inputs[nm]))
        maps.append(m)
    res = run_bass_kernel_spmd(ncf, maps, core_ids=list(range(NCORES)))
    x2s = [r["x2"] for r in res.results]

    pa = build_prog_a()
    nca = pa.finish()
    maps = []
    for c in range(8):
        m = dict(consts)
        m["x2"] = x2s[c]
        for nm in A_IN:
            m[nm] = np.ascontiguousarray(np.asarray(inputs[nm]))
        m.update(tabs[c % 2][0])
        maps.append(m)
    res = run_bass_kernel_spmd(nca, maps, core_ids=list(range(NCORES)))
    x3s = [r["x3"] for r in res.results]

    pe = build_prog_e()
    nce = pe.finish()
    maps = []
    for c in range(8):
        m = dict(consts)
        m["x3"] = x3s[c]
        m["cent_pad"] = cent_pad
        for nm in E_IN:
            m[nm] = np.ascontiguousarray(np.asarray(inputs[nm]))
        maps.append(m)
    res = run_bass_kernel_spmd(nce, maps, core_ids=list(range(NCORES)))

    out = np.empty((B, L, DM), np.float32)
    for c in range(8):
        own_rows = tabs[c % 2][1]
        out[c // 2][own_rows] = res.results[c]["out_own"]
    return out
